# version 03
# baseline (speedup 1.0000x reference)
import contextlib
import numpy as np
import ml_dtypes
import concourse.bass as bass
import concourse.mybir as mybir
from concourse.bass_utils import run_bass_kernel_spmd
from concourse.expressions import RuntimeValue

F32 = mybir.dt.float32
BF16 = mybir.dt.bfloat16
AF = mybir.ActivationFunctionType
ALU = mybir.AluOpType
BF = ml_dtypes.bfloat16

NCORE = 8
B_SZ = 2
SEQ = 4096
D = 2048
DC = D // 128
TPC = 1024
H0 = 5632
H1 = 8192
NH = 16
DH = 128
DT_RANK = 64
D_STATE = 32
LN_EPS = 1e-5
EXP_BIAS = -20.0
MASK_NEG = -30000.0

C_LNEPS, C_ONE, C_NEG1, C_NEGH, C_EXPB, C_HALF, C_EPS8, C_ZERO = range(8)
CONST_VALS = [LN_EPS, 1.0, -1.0, -0.5, EXP_BIAS, 0.5, 1e-8, 0.0]

ENGS = ("sync", "gpsimd", "scalar", "vector", "tensor")


class Rec:

    def __init__(self):
        self.ops = {k: [] for k in ENGS}
        self.cnt = {"ld": 0, "lde": 0, "ldo": 0, "g": 0, "sc": 0, "ve": 0, "te": 0, "cc": 0,
                    "t0": 0, "t1": 0, "t2": 0, "t3": 0, "t4": 0, "t5": 0}

    DMA_SEMS = ("ld", "lde", "ldo", "g", "cc", "t0", "t1", "t2", "t3", "t4", "t5")

    def _emit(self, eng, waits, fn, inc_sem=None, inc_by=0):
        waits = tuple(
            (s, self.cnt[s] - (inc_by if s == inc_sem else 0))
            if s in self.DMA_SEMS else (s, v) for (s, v) in waits
        )

        def op(e, sems):
            for (s, v) in waits:
                e.wait_ge(sems[s], v)
            ins = fn(e)
            if inc_sem is not None and ins is not None:
                ins.then_inc(sems[inc_sem], inc_by)

        self.ops[eng].append(op)

    def dma(self, q, out, in_, waits=(), sem=None):
        if sem is None:
            sem = {"sync": "ld", "gpsimd": "g"}[q]
        self.cnt[sem] += 16
        self._emit(q, waits, lambda e: e.dma_start(out=out, in_=in_), sem, 16)
        return (sem, self.cnt[sem])

    def op(self, eng, fn, waits=()):
        sem = {"scalar": "sc", "vector": "ve", "tensor": "te"}[eng]
        self.cnt[sem] += 1
        self._emit(eng, waits, fn, sem, 1)
        return (sem, self.cnt[sem])

    def wait(self, eng, waits):
        self._emit(eng, waits, lambda e: None)

    def fence(self):
        return [(s, self.cnt[s]) for s in self.cnt]

    def collective(self, kind, op, groups, ins, outs, waits=()):
        waits = list(waits) + ([("cc", self.cnt["cc"])] if self.cnt["cc"] else [])
        self.cnt["cc"] += 1
        self._emit("gpsimd", waits, lambda e: e.collective_compute(
            kind, op, replica_groups=groups, ins=ins, outs=outs), "cc", 1)
        return ("cc", self.cnt["cc"])

    def replay(self, block, sems):
        for eng in ENGS:
            if not self.ops[eng]:
                continue
            engops = list(self.ops[eng])

            def body(e, engops=engops):
                for f in engops:
                    f(e, sems)

            getattr(block, eng)(body)


def _mk_sems(nc, stack):
    return {n: stack.enter_context(nc.semaphore(n))
            for n in ("ld", "lde", "ldo", "g", "sc", "ve", "te", "cc",
                      "t0", "t1", "t2", "t3", "t4", "t5")}



def emit_ln(r, sb, ps, x_tiles, xl_tiles, n_tok, deps=()):
    stats = sb["stats"]
    sums = stats[:, 0:n_tok]
    sumsq = stats[:, n_tok:2 * n_tok]
    scr = stats[:, 2 * n_tok:3 * n_tok]
    cons = sb["consts"]
    nhalf = n_tok // 512

    casts = []
    for c in range(DC):
        casts.append(r.op("vector", lambda e, c=c: e.tensor_copy(out=xl_tiles[c][:], in_=x_tiles[c][:]), waits=deps))

    prev_copy_s = None
    prev_copy_q = None
    for h in range(nhalf):
        sl = slice(512 * h, 512 * (h + 1))
        prev_mmq = None
        for c in range(DC):
            sqw = [casts[c]] + ([prev_mmq] if prev_mmq else [])
            wsq = r.op("scalar", lambda e, c=c, sl=sl: e.activation(
                out=sb["sqt"][:], in_=xl_tiles[c][:, sl], func=AF.Square), waits=sqw)
            smw = [casts[c]]
            if c == 0 and prev_copy_s is not None:
                smw.append(prev_copy_s)
            r.op("tensor", lambda e, c=c, sl=sl: e.matmul(
                ps["sts"][:], lhsT=sb["ones_bf"][:], rhs=xl_tiles[c][:, sl],
                start=(c == 0), stop=(c == DC - 1)), waits=smw)
            qmw = [wsq]
            if c == 0 and prev_copy_q is not None:
                qmw.append(prev_copy_q)
            prev_mmq = r.op("tensor", lambda e, c=c: e.matmul(
                ps["stq"][:], lhsT=sb["ones_bf"][:], rhs=sb["sqt"][:],
                start=(c == 0), stop=(c == DC - 1)), waits=qmw)
        wt = ("te", r.cnt["te"])
        prev_copy_s = r.op("scalar", lambda e, sl=sl: e.copy(out=sums[:, sl], in_=ps["sts"][:]), waits=[wt])
        prev_copy_q = r.op("scalar", lambda e, sl=sl: e.copy(out=sumsq[:, sl], in_=ps["stq"][:]), waits=[wt])

    w1 = r.op("vector", lambda e: e.tensor_scalar_mul(out=sums, in0=sums, scalar1=1.0 / D), waits=[prev_copy_s, prev_copy_q])
    w2 = r.op("scalar", lambda e: e.activation(out=scr, in_=sums, func=AF.Square), waits=[w1])
    w3 = r.op("vector", lambda e: e.tensor_scalar_mul(out=sumsq, in0=sumsq, scalar1=1.0 / D), waits=[w1])
    w4 = r.op("vector", lambda e: e.tensor_sub(out=sumsq, in0=sumsq, in1=scr), waits=[w2, w3])
    w5 = r.op("scalar", lambda e: e.activation(out=sumsq, in_=sumsq, func=AF.Ln, bias=cons[0:1, C_LNEPS:C_LNEPS + 1]), waits=[w4])
    wrstd = r.op("scalar", lambda e: e.activation(out=sumsq, in_=sumsq, func=AF.Exp, scale=cons[0:1, C_NEGH:C_NEGH + 1]), waits=[w5])

    guard = wrstd
    wlast = None
    for h in range(nhalf):
        sl = slice(512 * h, 512 * (h + 1))
        wm = r.op("tensor", lambda e, sl=sl: e.matmul(ps["mb"][:], lhsT=sb["ones1"][:], rhs=sums[:, sl], start=True, stop=True), waits=[guard])
        wr = r.op("tensor", lambda e, sl=sl: e.matmul(ps["rb"][:], lhsT=sb["ones1"][:], rhs=sumsq[:, sl], start=True, stop=True), waits=[guard])
        for c in range(DC):
            waw = ([wm, wr] if c == 0 else []) + ([wlast] if wlast else [])
            wa = r.op("vector", lambda e, c=c, sl=sl: e.tensor_sub(out=sb["tmp"][:], in0=x_tiles[c][:, sl], in1=ps["mb"][:]), waits=waw)
            wlast = r.op("vector", lambda e, c=c, sl=sl: e.tensor_mul(out=xl_tiles[c][:, sl], in0=sb["tmp"][:], in1=ps["rb"][:]), waits=[wa])
        guard = wlast
    return wlast



def build_A():
    nc = bass.Bass("TRN2", target_bir_lowering=False, debug=False)
    xT = nc.dram_tensor("xT", [D, TPC], F32, kind="ExternalInput").ap()
    wbcT = nc.dram_tensor("wbcT", [D, 128], F32, kind="ExternalInput").ap()
    wdtT = nc.dram_tensor("wdtT", [DT_RANK, D], F32, kind="ExternalInput").ap()
    bdt = nc.dram_tensor("bdt", [128, DC], F32, kind="ExternalInput").ap()
    ones_bf_d = nc.dram_tensor("ones_bf", [128, 1], BF16, kind="ExternalInput").ap()
    ones1_d = nc.dram_tensor("ones1", [1, 128], F32, kind="ExternalInput").ap()
    consts_d = nc.dram_tensor("consts", [128, 8], F32, kind="ExternalInput").ap()

    o_delta = nc.dram_tensor("o_delta", [D, TPC], BF16, kind="ExternalOutput").ap()
    o_u = nc.dram_tensor("o_u", [D, TPC], BF16, kind="ExternalOutput").ap()
    o_bc = nc.dram_tensor("o_bc", [64, TPC], BF16, kind="ExternalOutput").ap()

    r = Rec()
    with contextlib.ExitStack() as stack:
        ec = stack.enter_context
        x_tiles = [ec(nc.sbuf_tensor(f"x{c}", [128, TPC], F32)) for c in range(DC)]
        xl_tiles = [ec(nc.sbuf_tensor(f"xl{c}", [128, TPC], BF16)) for c in range(DC)]
        sb = {
            "stats": ec(nc.sbuf_tensor("stats", [1, 3 * TPC], F32)),
            "sqt": ec(nc.sbuf_tensor("sqt", [128, 512], BF16)),
            "tmp": ec(nc.sbuf_tensor("tmp", [128, 512], F32)),
            "ones_bf": ec(nc.sbuf_tensor("onesbf", [128, 1], BF16)),
            "ones1": ec(nc.sbuf_tensor("ones1s", [1, 128], F32)),
            "consts": ec(nc.sbuf_tensor("constsS", [128, 8], F32)),
        }
        wbc_sb = ec(nc.sbuf_tensor("wbc", [128, DC * 128], BF16))
        wdt_sb = ec(nc.sbuf_tensor("wdt", [DT_RANK, D], BF16))
        bdt_sb = ec(nc.sbuf_tensor("bdtS", [128, DC], F32))
        dtr_sb = ec(nc.sbuf_tensor("dtr", [DT_RANK, TPC], BF16))
        bc_sb = ec(nc.sbuf_tensor("bcS", [64, TPC], BF16))
        delta_sb = [ec(nc.sbuf_tensor(f"dl{c}", [128, TPC], BF16)) for c in range(DC)]
        t_abs = ec(nc.sbuf_tensor("tabs", [128, 512], F32))
        t_rel = ec(nc.sbuf_tensor("trel", [128, 512], F32))

        ps = {
            "sts": ec(nc.psum_tensor("psts", [1, 512], F32)),
            "stq": ec(nc.psum_tensor("pstq", [1, 512], F32)),
            "mb": ec(nc.psum_tensor("pmb", [128, 512], F32)),
            "rb": ec(nc.psum_tensor("prb", [128, 512], F32)),
        }
        ps_db = [ec(nc.psum_tensor(f"pdb{h}", [128, 512], F32)) for h in range(2)]
        ps_dt = [ec(nc.psum_tensor(f"pdt{h}", [128, 512], F32)) for h in range(2)]

        sems = _mk_sems(nc, stack)

        wx = [r.dma("sync", x_tiles[c][:], xT[128 * c:128 * (c + 1), :]) for c in range(DC)]
        ww1 = r.dma("gpsimd", wbc_sb[:].rearrange("p (c o) -> p c o", o=128),
                    wbcT.rearrange("(c p) o -> p c o", p=128))
        ww2 = r.dma("gpsimd", wdt_sb[:], wdtT)
        wb = r.dma("sync", bdt_sb[:], bdt)
        wo1 = r.dma("sync", sb["ones_bf"][:], ones_bf_d)
        wo2 = r.dma("sync", sb["ones1"][:], ones1_d)
        wcn = r.dma("sync", sb["consts"][:], consts_d)

        wln = emit_ln(r, sb, ps, x_tiles, xl_tiles, TPC,
                      deps=[wx[DC - 1], wo1, wo2, wcn])

        ev_dtr = [None, None]
        ev_bc = [None, None]
        for h in range(2):
            sl = slice(512 * h, 512 * (h + 1))
            for c in range(DC):
                r.op("tensor", lambda e, c=c, h=h, sl=sl: e.matmul(
                    ps_db[h][:], lhsT=wbc_sb[:, 128 * c:128 * (c + 1)], rhs=xl_tiles[c][:, sl],
                    start=(c == 0), stop=(c == DC - 1)),
                    waits=[wln, ww1] if (h == 0 and c == 0) else [])
            wt = ("te", r.cnt["te"])
            ev_dtr[h] = r.op("scalar", lambda e, h=h, sl=sl: e.copy(out=dtr_sb[:, sl], in_=ps_db[h][0:64, :]), waits=[wt])
            ev_bc[h] = r.op("scalar", lambda e, h=h, sl=sl: e.copy(out=bc_sb[:, sl], in_=ps_db[h][64:128, :]), waits=[wt])

        prev_psum_reader = [None, None]
        prev_vadd = None
        for c in range(DC):
            for h in range(2):
                sl = slice(512 * h, 512 * (h + 1))
                w0 = [ev_dtr[h], ww2, wb]
                if prev_psum_reader[h] is not None:
                    w0.append(prev_psum_reader[h])
                wmm = r.op("tensor", lambda e, c=c, h=h, sl=sl: e.matmul(
                    ps_dt[h][:], lhsT=wdt_sb[:, 128 * c:128 * (c + 1)], rhs=dtr_sb[:, sl],
                    start=True, stop=True), waits=w0)
                wa_w = [wmm] + ([prev_vadd] if prev_vadd is not None else [])
                wa = r.op("scalar", lambda e, c=c, h=h: e.activation(
                    out=t_abs[:], in_=ps_dt[h][:], func=AF.Abs, bias=bdt_sb[:, c:c + 1]), waits=wa_w)
                wr2 = r.op("scalar", lambda e, c=c, h=h: e.activation(
                    out=t_rel[:], in_=ps_dt[h][:], func=AF.Relu, bias=bdt_sb[:, c:c + 1]), waits=[])
                prev_psum_reader[h] = wr2
                we = r.op("scalar", lambda e: e.activation(out=t_abs[:], in_=t_abs[:], func=AF.Exp, scale=sb["consts"][:, C_NEG1:C_NEG1 + 1]), waits=[])
                wl = r.op("scalar", lambda e: e.activation(out=t_abs[:], in_=t_abs[:], func=AF.Ln, bias=sb["consts"][:, C_ONE:C_ONE + 1]), waits=[])
                prev_vadd = r.op("vector", lambda e, c=c, sl=sl: e.tensor_add(
                    out=delta_sb[c][:, sl], in0=t_rel[:], in1=t_abs[:]), waits=[wl])

        wve = ("ve", r.cnt["ve"])
        wsc = ("sc", r.cnt["sc"])
        for c in range(DC):
            r.dma("sync", o_delta[128 * c:128 * (c + 1), :], delta_sb[c][:], waits=[wve] if c == 0 else [])
            r.dma("sync", o_u[128 * c:128 * (c + 1), :], xl_tiles[c][:], waits=[])
        r.dma("sync", o_bc, bc_sb[:], waits=[wsc])
        final = r.cnt["ld"]
        r.wait("sync", [("ld", final)])

        with nc.Block() as block:
            r.replay(block, sems)
    return nc



TC = 512
NCH = SEQ // TC
NDT = 4
TC1 = TC + 2


def build_B():
    nc = bass.Bass("TRN2", target_bir_lowering=False, debug=False)
    delta_d = nc.dram_tensor("delta", [512, SEQ], BF16, kind="ExternalInput").ap()
    u_d = nc.dram_tensor("u", [512, SEQ], BF16, kind="ExternalInput").ap()
    bc_d = nc.dram_tensor("bc", [64, SEQ], BF16, kind="ExternalInput").ap()
    alog_d = nc.dram_tensor("alog", [128, 128], F32, kind="ExternalInput").ap()
    dpar_d = nc.dram_tensor("dpar", [128, NDT], F32, kind="ExternalInput").ap()
    o_y = nc.dram_tensor("o_y", [512, SEQ], BF16, kind="ExternalOutput").ap()

    r = Rec()
    with contextlib.ExitStack() as stack:
        ec = stack.enter_context
        dA = ec(nc.sbuf_tensor("dA", [128, D_STATE * TC1], BF16))
        dBu = ec(nc.sbuf_tensor("dBu", [128, D_STATE * TC1], BF16))
        hh = ec(nc.sbuf_tensor("hh", [128, D_STATE * TC1], BF16))
        brep = ec(nc.sbuf_tensor("brep", [128, D_STATE * TC], BF16))
        crep = ec(nc.sbuf_tensor("crep", [128, D_STATE * TC], BF16))
        dl_sb = ec(nc.sbuf_tensor("dl", [128, TC], BF16))
        u_sb = ec(nc.sbuf_tensor("uS", [128, TC], BF16))
        du_sb = ec(nc.sbuf_tensor("du", [128, TC], BF16))
        y_sb = ec(nc.sbuf_tensor("yS", [128, TC], BF16))
        carry = ec(nc.sbuf_tensor("carry", [128, D_STATE * NDT], BF16))
        alog_sb = ec(nc.sbuf_tensor("alogS", [128, 128], F32))
        a_sb = ec(nc.sbuf_tensor("aS", [128, 128], F32))
        dpar_sb = ec(nc.sbuf_tensor("dparS", [128, NDT], F32))

        sems = _mk_sems(nc, stack)

        dA3 = dA[:].rearrange("p (n t) -> p n t", t=TC1)
        dBu3 = dBu[:].rearrange("p (n t) -> p n t", t=TC1)
        hh3 = hh[:].rearrange("p (n t) -> p n t", t=TC1)
        brep3 = brep[:].rearrange("p (n t) -> p n t", t=TC)
        crep3 = crep[:].rearrange("p (n t) -> p n t", t=TC)

        wal = r.dma("sync", alog_sb[:], alog_d)
        wdp = r.dma("sync", dpar_sb[:], dpar_d)
        wexp = r.op("scalar", lambda e: e.activation(out=a_sb[:], in_=alog_sb[:], func=AF.Exp), waits=[wal])
        wA = r.op("vector", lambda e: e.tensor_scalar_mul(out=a_sb[:], in0=a_sb[:], scalar1=-1.0), waits=[wexp])

        prev_ve = wA
        prev_sc = wexp
        prev_ystore = None
        for ch in range(NCH):
            tsl = slice(TC * ch, TC * (ch + 1))
            wbr = r.dma("sync", brep3, bc_d[0:32, tsl].unsqueeze(0).broadcast_to([128, 32, TC]), waits=[prev_ve])
            wcr = r.dma("sync", crep3, bc_d[32:64, tsl].unsqueeze(0).broadcast_to([128, 32, TC]), waits=[])
            for dt in range(NDT):
                psl = slice(128 * dt, 128 * (dt + 1))
                wld = r.dma("sync", dl_sb[:], delta_d[psl, tsl], waits=[prev_sc, prev_ve])
                wlu = r.dma("sync", u_sb[:], u_d[psl, tsl], waits=[])
                wexps = None
                for n in range(D_STATE):
                    wexps = r.op("scalar", lambda e, n=n, dt=dt: e.activation(
                        out=dA[:, n * TC1 + 1: (n + 1) * TC1], in_=dl_sb[:],
                        func=AF.Exp, scale=a_sb[:, dt * 32 + n: dt * 32 + n + 1]),
                        waits=[wld, prev_ve, wA] if n == 0 else [])
                wms = r.op("vector", lambda e: e.memset(dA3[:, :, 0:1], 0.0), waits=[prev_ve] if ch + dt else [])
                wdu = r.op("vector", lambda e: e.tensor_mul(out=du_sb[:], in0=dl_sb[:], in1=u_sb[:]), waits=[wld, wlu])
                wdb = r.op("vector", lambda e: e.tensor_tensor(
                    out=dBu3[:, :, 1:TC + 1], in0=du_sb[:].unsqueeze(1).broadcast_to([128, 32, TC]),
                    in1=brep3, op=ALU.mult), waits=[wbr, wdu])
                if ch == 0:
                    wcc = r.op("vector", lambda e, dt=dt: e.memset(dBu3[:, :, 0:1], 0.0), waits=[])
                else:
                    wcc = r.op("vector", lambda e, dt=dt: e.tensor_copy(
                        out=dBu3[:, :, 0:1], in_=carry[:, dt * 32:(dt + 1) * 32].unsqueeze(2)), waits=[])
                wsc_ = r.op("vector", lambda e: e.tensor_tensor_scan(
                    out=hh[:], data0=dA[:], data1=dBu[:], initial=0.0,
                    op0=ALU.mult, op1=ALU.add), waits=[wexps, wms, wdb, wcc])
                wco = r.op("vector", lambda e, dt=dt: e.tensor_copy(
                    out=carry[:, dt * 32:(dt + 1) * 32].unsqueeze(2), in_=hh3[:, :, TC:TC + 1]), waits=[wsc_])
                whc = r.op("vector", lambda e: e.tensor_tensor(
                    out=dA3[:, :, 1:TC + 1], in0=hh3[:, :, 1:TC + 1], in1=crep3, op=ALU.mult), waits=[wcr, wsc_])
                wtr = whc
                for half in (16, 8, 4, 2, 1):
                    wtr = r.op("vector", lambda e, half=half: e.tensor_tensor(
                        out=dA3[:, 0:half, 1:TC + 1], in0=dA3[:, 0:half, 1:TC + 1],
                        in1=dA3[:, half:2 * half, 1:TC + 1], op=ALU.add), waits=[wtr])
                yw = [wdp, wtr] + ([prev_ystore] if prev_ystore else [])
                wy = r.op("vector", lambda e, dt=dt: e.scalar_tensor_tensor(
                    out=y_sb[:], in0=u_sb[:], scalar=dpar_sb[:, dt:dt + 1],
                    in1=dA[:, 1:TC + 1], op0=ALU.mult, op1=ALU.add), waits=yw)
                prev_ystore = r.dma("sync", o_y[psl, tsl], y_sb[:], waits=[wy])
                prev_ve = wy
                prev_sc = wexps
        final = r.cnt["ld"]
        r.wait("sync", [("ld", final)])

        with nc.Block() as block:
            r.replay(block, sems)
    return nc


def host_prep_B(inputs, resA):
    dseq = {b: np.concatenate([resA[4 * b + j]["o_delta"] for j in range(4)], axis=1) for b in range(B_SZ)}
    useq = {b: np.concatenate([resA[4 * b + j]["o_u"] for j in range(4)], axis=1) for b in range(B_SZ)}
    bcseq = {b: np.ascontiguousarray(np.concatenate([resA[4 * b + j]["o_bc"] for j in range(4)], axis=1)) for b in range(B_SZ)}
    alog = np.asarray(inputs["L0_attn_A_log"], dtype=np.float32)
    dpar = np.asarray(inputs["L0_attn_D_param"], dtype=np.float32)
    maps = []
    for c in range(NCORE):
        b, j = c // 4, c % 4
        rows = slice(512 * j, 512 * (j + 1))
        al = np.ascontiguousarray(
            alog[rows].reshape(4, 128, 32).transpose(1, 0, 2).reshape(128, 128))
        dp = np.ascontiguousarray(dpar[rows.start:rows.stop].reshape(4, 128).T)
        maps.append(dict(
            delta=np.ascontiguousarray(dseq[b][rows.start:rows.stop]),
            u=np.ascontiguousarray(useq[b][rows.start:rows.stop]),
            bc=bcseq[b], alog=al, dpar=dp))
    return maps



QSPEC = [
    ("w1s", 256, H0, 0),
    ("w2s", 256, H0, 0),
    ("w3s", 704, D, 0),
    ("f1s", 256, H1, 1),
    ("f2s", 1024, D, 1),
]
CSPEC = [("aq", 256, D), ("ak", 256, D), ("av", 256, D), ("ao", 256, D)]
QCOUNT = [H0 * D, H0 * D, H0 * D, H1 * D, H1 * D]


def build_Q():
    nc = bass.Bass("TRN2", target_bir_lowering=False, debug=False)
    ins = {}
    outs = {}
    for (nm, rows, cols, _) in QSPEC:
        ins[nm] = nc.dram_tensor(nm, [rows, cols], F32, kind="ExternalInput").ap()
        outs[nm] = nc.dram_tensor("q" + nm, [rows, cols], BF16, kind="ExternalOutput").ap()
    for (nm, rows, cols) in CSPEC:
        ins[nm] = nc.dram_tensor(nm, [rows, cols], F32, kind="ExternalInput").ap()
        outs[nm] = nc.dram_tensor("q" + nm, [rows, cols], BF16, kind="ExternalOutput").ap()
    ones1_d = nc.dram_tensor("ones1", [1, 128], F32, kind="ExternalInput").ap()
    invcnt_d = nc.dram_tensor("invcnt", [1, 8], F32, kind="ExternalInput").ap()
    sel_d = nc.dram_tensor("sel", [1, 8], F32, kind="ExternalInput").ap()

    g_b = nc.dram_tensor("g_bounce", [1, 8], F32)
    g_ar = nc.dram_tensor("g_ar", [1, 8], F32)

    r = Rec()
    with contextlib.ExitStack() as stack:
        ec = stack.enter_context
        wt = ec(nc.sbuf_tensor("wt", [128, H1], F32))
        qt = ec(nc.sbuf_tensor("qt", [128, H1], BF16))
        q2 = ec(nc.sbuf_tensor("q2", [128, H1], BF16))
        pacc = ec(nc.sbuf_tensor("pacc", [128, 16], F32))
        redc = ec(nc.sbuf_tensor("redc", [128, 1], F32))
        grow = ec(nc.sbuf_tensor("grow", [1, 16], F32))
        thb = ec(nc.sbuf_tensor("thb", [128, 8], F32))
        nthb = ec(nc.sbuf_tensor("nthb", [128, 8], F32))
        ones1 = ec(nc.sbuf_tensor("ones1s", [1, 128], F32))
        onescol = ec(nc.sbuf_tensor("onescol", [128, 1], F32))
        ps1 = ec(nc.psum_tensor("ps1", [1, 8], F32))
        psb = ec(nc.psum_tensor("psb", [128, 8], F32))
        sems = _mk_sems(nc, stack)

        w_on = r.dma("sync", ones1[:], ones1_d)
        wic = r.dma("sync", grow[0:1, 8:16], invcnt_d)
        wsel = r.dma("sync", thb[0:1, 0:8], sel_d)
        wmo = r.op("vector", lambda e: e.memset(pacc[:], 0.0), waits=[])
        woc = r.op("vector", lambda e: e.memset(onescol[:], 1.0), waits=[])

        def tiles_of(rows):
            out = []
            p = 0
            while p < rows:
                out.append((p, min(128, rows - p)))
                p += 128
            return out

        prev = wmo
        wmm_dep = []
        for wi, (nm, rows, cols, _) in enumerate(QSPEC):
            accs = []
            for ti, (p0, np_) in enumerate(tiles_of(rows)):
                wl = r.dma("sync", wt[0:np_, 0:cols], ins[nm][p0:p0 + np_, :], waits=[prev])
                prev = r.op("vector", lambda e, np_=np_, cols=cols, ti=ti, wi=wi: e.reduce_sum(
                    out=pacc[0:np_, ti + 1:ti + 2], in_=wt[0:np_, 0:cols],
                    axis=mybir.AxisListType.X, apply_absolute_value=True), waits=[wl])
                accs.append(ti)
            prev = r.op("vector", lambda e, n=len(accs): e.reduce_sum(
                out=redc[:], in_=pacc[:, 0:16], axis=mybir.AxisListType.X), waits=[prev])
            wq = r.op("tensor", lambda e, wi=wi: e.matmul(
                ps1[0:1, wi:wi + 1], lhsT=onescol[:], rhs=redc[:],
                start=True, stop=True), waits=[prev, woc])
            wmm_dep.append(wq)
            prev = r.op("vector", lambda e: e.memset(pacc[:], 0.0), waits=[wq])
        wg = r.op("scalar", lambda e: e.copy(out=grow[0:1, 0:8], in_=ps1[0:1, 0:8]), waits=wmm_dep)
        wgd = r.dma("gpsimd", g_b.ap(), grow[0:1, 0:8], waits=[wg])
        war = r.collective("AllReduce", ALU.add, [list(range(NCORE))],
                           [g_b.ap().opt()], [g_ar.ap().opt()], waits=[wgd])
        wgb = r.dma("sync", grow[0:1, 0:8], g_ar.ap(), waits=[war])

        wm1 = r.op("vector", lambda e: e.tensor_mul(out=grow[0:1, 0:8], in0=grow[0:1, 0:8], in1=grow[0:1, 8:16]), waits=[wgb, wic])
        wa_ = r.op("vector", lambda e: e.tensor_scalar_add(out=grow[0:1, 8:16], in0=grow[0:1, 0:8], scalar1=1e-8), waits=[wm1])
        wb_ = r.op("vector", lambda e: e.tensor_scalar_max(out=grow[0:1, 0:8], in0=grow[0:1, 0:8], scalar1=1e-8), waits=[wa_])
        wbl1 = r.op("vector", lambda e: e.tensor_sub(out=grow[0:1, 8:16], in0=grow[0:1, 8:16], in1=grow[0:1, 0:8]), waits=[wa_, wb_])
        wbl2 = r.op("vector", lambda e: e.tensor_mul(out=grow[0:1, 8:16], in0=grow[0:1, 8:16], in1=thb[0:1, 0:8]), waits=[wbl1, wsel])
        wbl3 = r.op("vector", lambda e: e.tensor_add(out=grow[0:1, 0:8], in0=grow[0:1, 0:8], in1=grow[0:1, 8:16]), waits=[wbl2])
        wth = r.op("vector", lambda e: e.tensor_scalar_mul(out=grow[0:1, 0:8], in0=grow[0:1, 0:8], scalar1=0.5), waits=[wbl3])
        wtb = r.op("tensor", lambda e: e.matmul(psb[:], lhsT=ones1[:], rhs=grow[0:1, 0:8], start=True, stop=True), waits=[wth, w_on])
        wtc = r.op("vector", lambda e: e.tensor_copy(out=thb[:], in_=psb[:]), waits=[wtb])
        wtn = r.op("vector", lambda e: e.tensor_scalar_mul(out=nthb[:], in0=thb[:], scalar1=-1.0), waits=[wtc])

        prev_q = wtn
        prev_out = None
        for wi, (nm, rows, cols, _) in enumerate(QSPEC):
            for (p0, np_) in tiles_of(rows):
                wl = r.dma("sync", wt[0:np_, 0:cols], ins[nm][p0:p0 + np_, :], waits=[prev_q])
                w1w = [wl] + ([prev_out] if prev_out else [])
                w1_ = r.op("vector", lambda e, np_=np_, cols=cols, wi=wi: e.tensor_scalar(
                    out=qt[0:np_, 0:cols], in0=wt[0:np_, 0:cols], scalar1=thb[0:np_, wi:wi + 1],
                    scalar2=None, op0=ALU.is_ge), waits=w1w)
                w2_ = r.op("vector", lambda e, np_=np_, cols=cols, wi=wi: e.tensor_scalar(
                    out=q2[0:np_, 0:cols], in0=wt[0:np_, 0:cols], scalar1=nthb[0:np_, wi:wi + 1],
                    scalar2=None, op0=ALU.is_le), waits=[])
                w3_ = r.op("vector", lambda e, np_=np_, cols=cols: e.tensor_sub(
                    out=qt[0:np_, 0:cols], in0=qt[0:np_, 0:cols], in1=q2[0:np_, 0:cols]), waits=[w1_, w2_])
                prev_out = r.dma("sync", outs[nm][p0:p0 + np_, :], qt[0:np_, 0:cols], waits=[w3_])
                prev_q = w3_
        for (nm, rows, cols) in CSPEC:
            for (p0, np_) in tiles_of(rows):
                wl = r.dma("sync", wt[0:np_, 0:cols], ins[nm][p0:p0 + np_, :], waits=[prev_q])
                cw = [wl] + ([prev_out] if prev_out else [])
                prev_q = r.op("vector", lambda e, np_=np_, cols=cols: e.tensor_copy(
                    out=qt[0:np_, 0:cols], in_=wt[0:np_, 0:cols]), waits=cw)
                prev_out = r.dma("sync", outs[nm][p0:p0 + np_, :], qt[0:np_, 0:cols], waits=[prev_q])
        final = r.cnt["ld"]
        r.wait("sync", [("ld", final)])
        with nc.Block() as block:
            r.replay(block, sems)
    return nc


def host_prep_Q(inputs):
    w1T = np.ascontiguousarray(inputs["L0_ffn_w1"].T).astype(np.float32)
    w2T = np.ascontiguousarray(inputs["L0_ffn_w2"].T).astype(np.float32)
    w3T = np.ascontiguousarray(inputs["L0_ffn_w3"].T).astype(np.float32)
    f1T = np.ascontiguousarray(inputs["L1_ffn_w1"].T).astype(np.float32)
    f2T = np.ascontiguousarray(inputs["L1_ffn_w2"].T).astype(np.float32)
    aqT = np.ascontiguousarray(inputs["L1_attn_Wq"].T).astype(np.float32)
    akT = np.ascontiguousarray(inputs["L1_attn_Wk"].T).astype(np.float32)
    avT = np.ascontiguousarray(inputs["L1_attn_Wv"].T).astype(np.float32)
    aoT = np.ascontiguousarray(inputs["L1_attn_Wo"].T).astype(np.float32)
    full = dict(w1s=w1T, w2s=w2T, w3s=w3T, f1s=f1T, f2s=f2T, aq=aqT, ak=akT, av=avT, ao=aoT)
    shard_rows = dict(w1s=256, w2s=256, w3s=704, f1s=256, f2s=1024, aq=256, ak=256, av=256, ao=256)
    ones1 = np.ones((1, 128), dtype=np.float32)
    invcnt = np.zeros((1, 8), dtype=np.float32)
    for i, c in enumerate(QCOUNT):
        invcnt[0, i] = 1.0 / c
    sel = np.zeros((1, 8), dtype=np.float32)
    sel[0, 0:3] = 1.0
    maps = []
    for c in range(NCORE):
        m = dict(ones1=ones1, invcnt=invcnt, sel=sel)
        for nm, arr in full.items():
            rr = shard_rows[nm]
            m[nm] = np.ascontiguousarray(arr[rr * c:rr * (c + 1)])
        maps.append(m)
    return maps


def host_join_Q(resQ):
    out = {}
    for nm in ("w1s", "w2s", "w3s", "f1s", "f2s", "aq", "ak", "av", "ao"):
        out[nm] = np.concatenate([resQ[c]["q" + nm] for c in range(NCORE)], axis=0)
    return out



NF0 = H0 // 128


def build_C():
    nc = bass.Bass("TRN2", target_bir_lowering=False, debug=False)
    xT = nc.dram_tensor("xT", [D, TPC], F32, kind="ExternalInput").ap()
    yT = nc.dram_tensor("yT", [D, TPC], BF16, kind="ExternalInput").ap()
    qw1 = nc.dram_tensor("qw1", [NF0, 128, D], BF16, kind="ExternalInput").ap()
    qw2 = nc.dram_tensor("qw2", [NF0, 128, D], BF16, kind="ExternalInput").ap()
    qw3 = nc.dram_tensor("qw3", [DC, 128, H0], BF16, kind="ExternalInput").ap()
    ones_bf_d = nc.dram_tensor("ones_bf", [128, 1], BF16, kind="ExternalInput").ap()
    ones1_d = nc.dram_tensor("ones1", [1, 128], F32, kind="ExternalInput").ap()
    consts_d = nc.dram_tensor("consts", [128, 8], F32, kind="ExternalInput").ap()
    o_x2 = nc.dram_tensor("o_x2", [D, TPC], F32, kind="ExternalOutput").ap()

    r = Rec()
    with contextlib.ExitStack() as stack:
        ec = stack.enter_context
        x_tiles = [ec(nc.sbuf_tensor(f"x{c}", [128, TPC], F32)) for c in range(DC)]
        xl_tiles = [ec(nc.sbuf_tensor(f"xl{c}", [128, TPC], BF16)) for c in range(DC)]
        hid = [ec(nc.sbuf_tensor(f"hid{f}", [128, 512], BF16)) for f in range(NF0)]
        w1s = [ec(nc.sbuf_tensor(f"w1s{p}", [128, D], BF16)) for p in range(2)]
        w2s = [ec(nc.sbuf_tensor(f"w2s{p}", [128, D], BF16)) for p in range(2)]
        w3s = [ec(nc.sbuf_tensor(f"w3s{p}", [128, H0], BF16)) for p in range(2)]
        sg_sb = ec(nc.sbuf_tensor("sg", [128, 512], BF16))
        sb = {
            "stats": ec(nc.sbuf_tensor("stats", [1, 3 * TPC], F32)),
            "sqt": ec(nc.sbuf_tensor("sqt", [128, 512], BF16)),
            "tmp": ec(nc.sbuf_tensor("tmp", [128, 512], F32)),
            "ones_bf": ec(nc.sbuf_tensor("onesbf", [128, 1], BF16)),
            "ones1": ec(nc.sbuf_tensor("ones1s", [1, 128], F32)),
            "consts": ec(nc.sbuf_tensor("constsS", [128, 8], F32)),
        }
        ps = {
            "sts": ec(nc.psum_tensor("psts", [1, 512], F32)),
            "stq": ec(nc.psum_tensor("pstq", [1, 512], F32)),
            "mb": ec(nc.psum_tensor("pmb", [128, 512], F32)),
            "rb": ec(nc.psum_tensor("prb", [128, 512], F32)),
        }
        px1 = ec(nc.psum_tensor("px1", [128, 512], F32))
        px2 = ec(nc.psum_tensor("px2", [128, 512], F32))
        sems = _mk_sems(nc, stack)

        wx = [r.dma("sync", x_tiles[c][:], xT[128 * c:128 * (c + 1), :]) for c in range(DC)]
        wy = [r.dma("sync", xl_tiles[c][:], yT[128 * c:128 * (c + 1), :]) for c in range(DC)]
        wo1 = r.dma("sync", sb["ones_bf"][:], ones_bf_d)
        wo2 = r.dma("sync", sb["ones1"][:], ones1_d)
        wcn = r.dma("sync", sb["consts"][:], consts_d)

        wadd = None
        for c in range(DC):
            wadd = r.op("vector", lambda e, c=c: e.tensor_tensor(
                out=x_tiles[c][:], in0=x_tiles[c][:], in1=xl_tiles[c][:], op=ALU.add),
                waits=[wx[c], wy[c]])

        wln = emit_ln(r, sb, ps, x_tiles, xl_tiles, TPC, deps=[wadd, wo1, wo2, wcn])

        rd1 = [None, None]
        rd2 = [None, None]
        rd3 = [None, None]
        prev_sig = None
        prev_hct = None
        prev_out_te = None
        prev_radd = None
        for h in range(2):
            sl = slice(512 * h, 512 * (h + 1))
            for f in range(NF0):
                par = f % 2
                psem = "lde" if par == 0 else "ldo"
                wd1 = r.dma("sync", w1s[par][:], qw1[f], waits=[w for w in (rd1[par],) if w], sem=psem)
                wd2 = r.dma("sync", w2s[par][:], qw2[f], waits=[w for w in (rd2[par],) if w], sem=psem)
                w0 = [wd1, wln] + ([prev_sig] if prev_sig else [])
                for c in range(DC):
                    r.op("tensor", lambda e, c=c, f=f, par=par, sl=sl: e.matmul(
                        px1[:], lhsT=w1s[par][:, 128 * c:128 * (c + 1)], rhs=xl_tiles[c][:, sl],
                        start=(c == 0), stop=(c == DC - 1)), waits=w0 if c == 0 else [])
                rd1[par] = ("te", r.cnt["te"])
                wte1 = rd1[par]
                w0 = [wd2] + ([prev_hct] if prev_hct else [])
                for c in range(DC):
                    r.op("tensor", lambda e, c=c, f=f, par=par, sl=sl: e.matmul(
                        px2[:], lhsT=w2s[par][:, 128 * c:128 * (c + 1)], rhs=xl_tiles[c][:, sl],
                        start=(c == 0), stop=(c == DC - 1)), waits=w0 if c == 0 else [])
                rd2[par] = ("te", r.cnt["te"])
                wte2 = rd2[par]
                sgw = [wte1] + ([prev_hct] if prev_hct else [])
                prev_sig = r.op("scalar", lambda e: e.activation(
                    out=sg_sb[:], in_=px1[:], func=AF.Sigmoid), waits=sgw)
                hw = [wte2, prev_sig] + ([prev_out_te] if (prev_out_te and h == 1) else [])
                prev_hct = r.op("vector", lambda e, f=f: e.tensor_tensor(
                    out=hid[f][:], in0=px2[:], in1=sg_sb[:], op=ALU.mult), waits=hw)
            whid = prev_hct
            for dci in range(DC):
                par = dci % 2
                wd3 = r.dma("sync", w3s[par][:], qw3[dci], waits=[w for w in (rd3[par],) if w],
                            sem="lde" if par == 0 else "ldo")
                w0 = [wd3, whid, prev_sig] + ([prev_radd] if prev_radd else [])
                for k in range(NF0):
                    r.op("tensor", lambda e, k=k, par=par: e.matmul(
                        px1[:], lhsT=w3s[par][:, 128 * k:128 * (k + 1)], rhs=hid[k][:],
                        start=(k == 0), stop=(k == NF0 - 1)), waits=w0 if k == 0 else [])
                rd3[par] = ("te", r.cnt["te"])
                prev_out_te = rd3[par]
                prev_radd = r.op("vector", lambda e, dci=dci, sl=sl: e.tensor_tensor(
                    out=x_tiles[dci][:, sl], in0=x_tiles[dci][:, sl], in1=px1[:], op=ALU.add),
                    waits=[rd3[par]])
                prev_sig = prev_radd
        for c in range(DC):
            r.dma("sync", o_x2[128 * c:128 * (c + 1), :], x_tiles[c][:], waits=[prev_radd] if c == 0 else [])
        final = r.cnt["ld"]
        r.wait("sync", [("ld", final)])
        with nc.Block() as block:
            r.replay(block, sems)
    return nc


def slabify(wT, fchunks):
    K, F = wT.shape
    kc = K // 128
    assert F == fchunks * 128
    return np.ascontiguousarray(
        wT.reshape(kc, 128, fchunks, 128).transpose(2, 1, 0, 3).reshape(fchunks, 128, K))


def host_prep_C(resA_or_x, yseq, qj):
    s1 = slabify(qj["w1s"], NF0)
    s2 = slabify(qj["w2s"], NF0)
    s3 = slabify(qj["w3s"], DC)
    ones_bf = np.ones((128, 1), dtype=BF)
    ones1 = np.ones((1, 128), dtype=np.float32)
    consts = np.tile(np.asarray([CONST_VALS], dtype=np.float32), (128, 1))
    return [
        dict(xT=resA_or_x[c], yT=yseq[c], qw1=s1, qw2=s2, qw3=s3,
             ones_bf=ones_bf, ones1=ones1, consts=consts)
        for c in range(NCORE)
    ]


def host_y_from_B(resB):
    ys = []
    for c in range(NCORE):
        b = c // 4
        yb = np.concatenate([resB[4 * b + j]["o_y"] for j in range(4)], axis=0)
        tloc = (c % 4)
        ys.append(np.ascontiguousarray(yb[:, TPC * tloc:TPC * (tloc + 1)]))
    return ys




def build_D():
    nc = bass.Bass("TRN2", target_bir_lowering=False, debug=False)
    xT = nc.dram_tensor("xT", [D, TPC], F32, kind="ExternalInput").ap()
    qaq = nc.dram_tensor("qaq", [DC, 128, D], BF16, kind="ExternalInput").ap()
    qak = nc.dram_tensor("qak", [DC, 128, D], BF16, kind="ExternalInput").ap()
    qav = nc.dram_tensor("qav", [DC, 128, D], BF16, kind="ExternalInput").ap()
    bq_d = nc.dram_tensor("bq", [128, DC], F32, kind="ExternalInput").ap()
    bk_d = nc.dram_tensor("bk", [128, DC], F32, kind="ExternalInput").ap()
    bv_d = nc.dram_tensor("bv", [128, DC], F32, kind="ExternalInput").ap()
    cosq_d = nc.dram_tensor("cosq", [128, TPC], BF16, kind="ExternalInput").ap()
    sinq_d = nc.dram_tensor("sinq", [128, TPC], BF16, kind="ExternalInput").ap()
    cosk_d = nc.dram_tensor("cosk", [128, TPC], BF16, kind="ExternalInput").ap()
    sink_d = nc.dram_tensor("sink", [128, TPC], BF16, kind="ExternalInput").ap()
    ones_bf_d = nc.dram_tensor("ones_bf", [128, 1], BF16, kind="ExternalInput").ap()
    ones1_d = nc.dram_tensor("ones1", [1, 128], F32, kind="ExternalInput").ap()
    consts_d = nc.dram_tensor("consts", [128, 8], F32, kind="ExternalInput").ap()
    o_q = nc.dram_tensor("o_q", [D, TPC], BF16, kind="ExternalOutput").ap()
    o_k = nc.dram_tensor("o_k", [D, TPC], BF16, kind="ExternalOutput").ap()
    o_v = nc.dram_tensor("o_v", [D, TPC], BF16, kind="ExternalOutput").ap()

    r = Rec()
    with contextlib.ExitStack() as stack:
        ec = stack.enter_context
        x_tiles = [ec(nc.sbuf_tensor(f"x{c}", [128, TPC], F32)) for c in range(DC)]
        xl_tiles = [ec(nc.sbuf_tensor(f"xl{c}", [128, TPC], BF16)) for c in range(DC)]
        wsl = [ec(nc.sbuf_tensor(f"wsl{p}", [128, D], BF16)) for p in range(2)]
        pt = ec(nc.sbuf_tensor("pt", [128, TPC], BF16))
        rt = ec(nc.sbuf_tensor("rt", [128, TPC], BF16))
        rt2 = ec(nc.sbuf_tensor("rt2", [128, TPC], BF16))
        tabs = {nm: ec(nc.sbuf_tensor(nm + "S", [128, TPC], BF16))
                for nm in ("cosq", "sinq", "cosk", "sink")}
        bias_sb = {nm: ec(nc.sbuf_tensor(nm + "S", [128, DC], F32))
                   for nm in ("bq", "bk", "bv")}
        sb = {
            "stats": ec(nc.sbuf_tensor("stats", [1, 3 * TPC], F32)),
            "sqt": ec(nc.sbuf_tensor("sqt", [128, 512], BF16)),
            "tmp": ec(nc.sbuf_tensor("tmp", [128, 512], F32)),
            "ones_bf": ec(nc.sbuf_tensor("onesbf", [128, 1], BF16)),
            "ones1": ec(nc.sbuf_tensor("ones1s", [1, 128], F32)),
            "consts": ec(nc.sbuf_tensor("constsS", [128, 8], F32)),
        }
        ps = {
            "sts": ec(nc.psum_tensor("psts", [1, 512], F32)),
            "stq": ec(nc.psum_tensor("pstq", [1, 512], F32)),
            "mb": ec(nc.psum_tensor("pmb", [128, 512], F32)),
            "rb": ec(nc.psum_tensor("prb", [128, 512], F32)),
        }
        pq = ec(nc.psum_tensor("pq", [128, 512], F32))
        sems = _mk_sems(nc, stack)

        wx = [r.dma("sync", x_tiles[c][:], xT[128 * c:128 * (c + 1), :]) for c in range(DC)]
        wo1 = r.dma("sync", sb["ones_bf"][:], ones_bf_d)
        wo2 = r.dma("sync", sb["ones1"][:], ones1_d)
        wcn = r.dma("sync", sb["consts"][:], consts_d)
        for nm, d_ in (("cosq", cosq_d), ("sinq", sinq_d), ("cosk", cosk_d), ("sink", sink_d)):
            r.dma("sync", tabs[nm][:], d_)
        for nm, d_ in (("bq", bq_d), ("bk", bk_d), ("bv", bv_d)):
            r.dma("sync", bias_sb[nm][:], d_)
        wtabs = ("ld", r.cnt["ld"])

        wln = emit_ln(r, sb, ps, x_tiles, xl_tiles, TPC, deps=[wx[DC - 1], wo1, wo2, wcn, wtabs])

        rdw = [None, None]
        prev_ev = None
        prev_store = None
        projs = [("q", qaq, "bq", o_q, "cosq", "sinq"),
                 ("k", qak, "bk", o_k, "cosk", "sink"),
                 ("v", qav, "bv", o_v, None, None)]
        fidx = 0
        for (pname, slabs, bnm, odst, cosn, sinn) in projs:
            for f in range(DC):
                par = fidx % 2
                psem = "lde" if par == 0 else "ldo"
                wd = r.dma("sync", wsl[par][:], slabs[f], waits=[w for w in (rdw[par],) if w], sem=psem)
                for h in range(2):
                    sl = slice(512 * h, 512 * (h + 1))
                    w0 = [wd, wln] + ([prev_ev] if prev_ev else [])
                    for c in range(DC):
                        r.op("tensor", lambda e, c=c, par=par, sl=sl: e.matmul(
                            pq[:], lhsT=wsl[par][:, 128 * c:128 * (c + 1)], rhs=xl_tiles[c][:, sl],
                            start=(c == 0), stop=(c == DC - 1)), waits=w0 if c == 0 else [])
                    rdw[par] = ("te", r.cnt["te"])
                    evw = [rdw[par]] + ([prev_store] if prev_store else [])
                    prev_ev = r.op("scalar", lambda e, sl=sl, bnm=bnm, f=f: e.activation(
                        out=pt[:, sl], in_=pq[:], func=AF.Identity,
                        bias=bias_sb[bnm][:, f:f + 1]), waits=evw)
                if cosn is not None:
                    wc1 = r.op("vector", lambda e: e.tensor_copy(out=rt[0:64, :], in_=pt[64:128, :]), waits=[prev_ev])
                    wc2 = r.op("vector", lambda e: e.tensor_copy(out=rt[64:128, :], in_=pt[0:64, :]), waits=[prev_ev])
                    wm1 = r.op("vector", lambda e, cosn=cosn: e.tensor_tensor(
                        out=pt[:], in0=pt[:], in1=tabs[cosn][:], op=ALU.mult), waits=[wc1, wc2])
                    wm2 = r.op("vector", lambda e, sinn=sinn: e.tensor_tensor(
                        out=rt2[:], in0=rt[:], in1=tabs[sinn][:], op=ALU.mult), waits=[wm1])
                    wad = r.op("vector", lambda e: e.tensor_tensor(
                        out=pt[:], in0=pt[:], in1=rt2[:], op=ALU.add), waits=[wm2])
                    prev_ev = wad
                prev_store = r.dma("sync", odst[128 * f:128 * (f + 1), :], pt[:], waits=[prev_ev])
                fidx += 1
        r.wait("sync", [("ld", r.cnt["ld"])])
        with nc.Block() as block:
            r.replay(block, sems)
    return nc



NKT = SEQ // 128


def build_E():
    nc = bass.Bass("TRN2", target_bir_lowering=False, debug=False)
    xT = nc.dram_tensor("xT", [D, TPC], F32, kind="ExternalInput").ap()
    qf_d = nc.dram_tensor("qf", [D, TPC], BF16, kind="ExternalInput").ap()
    kf_d = nc.dram_tensor("kf", [NH, 128, SEQ], BF16, kind="ExternalInput").ap()
    vt_d = nc.dram_tensor("vt", [NH, 128, SEQ], BF16, kind="ExternalInput").ap()
    mask_d = nc.dram_tensor("mask8", [NKT, 128, TPC], mybir.dt.int8, kind="ExternalInput").ap()
    qao = nc.dram_tensor("qao", [DC, 128, D], BF16, kind="ExternalInput").ap()
    bo_d = nc.dram_tensor("bo", [128, DC], F32, kind="ExternalInput").ap()
    ones_bf_d = nc.dram_tensor("ones_bf", [128, 1], BF16, kind="ExternalInput").ap()
    ones1_d = nc.dram_tensor("ones1", [1, 128], F32, kind="ExternalInput").ap()
    consts_d = nc.dram_tensor("consts", [128, 8], F32, kind="ExternalInput").ap()
    o_x3 = nc.dram_tensor("o_x3", [D, TPC], F32, kind="ExternalOutput").ap()

    r = Rec()
    with contextlib.ExitStack() as stack:
        ec = stack.enter_context
        x_tiles = [ec(nc.sbuf_tensor(f"x{c}", [128, TPC], F32)) for c in range(DC)]
        qf = [ec(nc.sbuf_tensor(f"qf{c}", [128, TPC], BF16)) for c in range(DC)]
        mg = [ec(nc.sbuf_tensor(f"mg{c}", [128, TPC], BF16)) for c in range(NH)]
        m8 = ec(nc.sbuf_tensor("m8", [128, NKT * TPC], mybir.dt.int8))
        kvh = [ec(nc.sbuf_tensor(f"kvh{p}", [128, SEQ], BF16)) for p in range(2)]
        vvh = [ec(nc.sbuf_tensor(f"vvh{p}", [128, SEQ], BF16)) for p in range(2)]
        s_sb = ec(nc.sbuf_tensor("sS", [128, 512], BF16))
        binv = ec(nc.sbuf_tensor("binv", [128, 512], BF16))
        srow = ec(nc.sbuf_tensor("srow", [1, 512], F32))
        sb = {
            "ones_bf": ec(nc.sbuf_tensor("onesbf", [128, 1], BF16)),
            "ones1": ec(nc.sbuf_tensor("ones1s", [1, 128], F32)),
            "consts": ec(nc.sbuf_tensor("constsS", [128, 8], F32)),
        }
        bo_sb = ec(nc.sbuf_tensor("boS", [128, DC], F32))
        ps_s = [ec(nc.psum_tensor(f"pss{i}", [128, 512], F32)) for i in range(2)]
        ps_ctx = ec(nc.psum_tensor("psctx", [128, 512], F32))
        ps_sum = ec(nc.psum_tensor("pssum", [1, 512], F32))
        ps_b = ec(nc.psum_tensor("psb", [128, 512], F32))
        sems = _mk_sems(nc, stack)

        wx = [r.dma("sync", x_tiles[c][:], xT[128 * c:128 * (c + 1), :]) for c in range(DC)]
        wq = [r.dma("sync", qf[c][:], qf_d[128 * c:128 * (c + 1), :]) for c in range(DC)]
        wm8 = r.dma("sync", m8[:].rearrange("p (kt t) -> p kt t", t=TPC),
                    mask_d.rearrange("kt p t -> p kt t"))
        wo1 = r.dma("sync", sb["ones_bf"][:], ones_bf_d)
        wo2 = r.dma("sync", sb["ones1"][:], ones1_d)
        wcn = r.dma("sync", sb["consts"][:], consts_d)
        wbo = r.dma("sync", bo_sb[:], bo_d)
        m83 = m8[:].rearrange("p (kt t) -> p kt t", t=TPC)
        wall = ("ld", r.cnt["ld"])

        rdk = [None, None]
        prev_exp = None
        prev_pv = None
        prev_evs = None
        prev_ctxev = None
        prev_sumev = None
        prev_bev = None
        cons = sb["consts"]
        for h in range(NH):
            par = h % 2
            psem = "lde" if par == 0 else "ldo"
            wkh = r.dma("sync", kvh[par][:], kf_d[h], waits=[w for w in (rdk[par],) if w], sem=psem)
            wvh = r.dma("sync", vvh[par][:], vt_d[h], waits=[], sem=psem)
            for qc in range(2):
                sl = slice(512 * qc, 512 * (qc + 1))
                for kt in range(NKT):
                    pp = kt % 2
                    w0 = [wkh, wall] + ([prev_exp] if prev_exp else [])
                    wmm = r.op("tensor", lambda e, h=h, kt=kt, par=par, sl=sl, pp=pp: e.matmul(
                        ps_s[pp][:], lhsT=kvh[par][:, 128 * kt:128 * (kt + 1)], rhs=qf[h][:, sl],
                        start=True, stop=True), waits=w0)
                    evw = [wmm] + ([prev_exp] if prev_exp else []) + ([prev_pv] if prev_pv else [])
                    wev = r.op("vector", lambda e, kt=kt, sl=sl, pp=pp: e.scalar_tensor_tensor(
                        out=s_sb[:], in0=m83[:, kt, sl], scalar=MASK_NEG, in1=ps_s[pp][:],
                        op0=ALU.mult, op1=ALU.add), waits=evw)
                    wexp = r.op("scalar", lambda e: e.activation(
                        out=s_sb[:], in_=s_sb[:], func=AF.Exp,
                        bias=cons[:, C_EXPB:C_EXPB + 1]), waits=[wev])
                    prev_evs = wev
                    pvw = [wexp, wvh]
                    if kt == 0 and prev_ctxev:
                        pvw.append(prev_ctxev)
                    r.op("tensor", lambda e, kt=kt, par=par: e.matmul(
                        ps_ctx[:], lhsT=vvh[par][:, 128 * kt:128 * (kt + 1)], rhs=s_sb[:],
                        start=(kt == 0), stop=(kt == NKT - 1)), waits=pvw)
                    smw = []
                    if kt == 0 and prev_sumev:
                        smw.append(prev_sumev)
                    prev_pv = r.op("tensor", lambda e, kt=kt: e.matmul(
                        ps_sum[:], lhsT=sb["ones_bf"][:], rhs=s_sb[:],
                        start=(kt == 0), stop=(kt == NKT - 1)), waits=smw)
                    prev_exp = wexp
                wte = ("te", r.cnt["te"])
                prev_sumev = r.op("scalar", lambda e: e.copy(out=srow[:], in_=ps_sum[:]), waits=[wte])
                wrc = r.op("vector", lambda e: e.reciprocal(out=srow[:], in_=srow[:]), waits=[prev_sumev])
                wbc = r.op("tensor", lambda e: e.matmul(
                    ps_b[:], lhsT=sb["ones1"][:], rhs=srow[:], start=True, stop=True),
                    waits=[wrc] + ([prev_bev] if prev_bev else []))
                prev_bev = r.op("vector", lambda e: e.tensor_copy(out=binv[:], in_=ps_b[:]), waits=[wbc])
                prev_ctxev = r.op("vector", lambda e, h=h, sl=sl: e.tensor_tensor(
                    out=mg[h][:, sl], in0=ps_ctx[:], in1=binv[:], op=ALU.mult),
                    waits=[prev_bev, wte])
            rdk[par] = ("te", r.cnt["te"])
        rdw = [prev_ctxev, prev_ctxev]
        prev_radd = None
        wsl = kvh
        for dci in range(DC):
            par = dci % 2
            psem = "lde" if par == 0 else "ldo"
            wd = r.dma("sync", wsl[par][:, 0:D], qao[dci], waits=[w for w in (rdw[par],) if w], sem=psem)
            for hf in range(2):
                sl = slice(512 * hf, 512 * (hf + 1))
                w0 = [wd, prev_ctxev]
                if prev_radd:
                    w0.append(prev_radd)
                for k in range(DC):
                    r.op("tensor", lambda e, k=k, par=par, sl=sl: e.matmul(
                        ps_b[:], lhsT=wsl[par][:, 0:D][:, 128 * k:128 * (k + 1)], rhs=mg[k][:, sl],
                        start=(k == 0), stop=(k == DC - 1)), waits=w0 if k == 0 else [])
                rdw[par] = ("te", r.cnt["te"])
                prev_radd = r.op("vector", lambda e, dci=dci, sl=sl: e.scalar_tensor_tensor(
                    out=x_tiles[dci][:, sl], in0=ps_b[:], scalar=bo_sb[:, dci:dci + 1],
                    in1=x_tiles[dci][:, sl], op0=ALU.add, op1=ALU.add),
                    waits=[rdw[par], wx[dci]])
        for c in range(DC):
            r.dma("sync", o_x3[128 * c:128 * (c + 1), :], x_tiles[c][:], waits=[prev_radd] if c == 0 else [])
        r.wait("sync", [("ld", r.cnt["ld"])])
        with nc.Block() as block:
            r.replay(block, sems)
    return nc



NF1 = H1 // 128


def build_F():
    nc = bass.Bass("TRN2", target_bir_lowering=False, debug=False)
    xT = nc.dram_tensor("xT", [D, TPC], F32, kind="ExternalInput").ap()
    qf1 = nc.dram_tensor("qf1", [NF1, 128, D], BF16, kind="ExternalInput").ap()
    qf2 = nc.dram_tensor("qf2", [DC, 128, H1], BF16, kind="ExternalInput").ap()
    b1_d = nc.dram_tensor("b1", [128, NF1], F32, kind="ExternalInput").ap()
    b2_d = nc.dram_tensor("b2", [128, DC], F32, kind="ExternalInput").ap()
    ones_bf_d = nc.dram_tensor("ones_bf", [128, 1], BF16, kind="ExternalInput").ap()
    ones1_d = nc.dram_tensor("ones1", [1, 128], F32, kind="ExternalInput").ap()
    consts_d = nc.dram_tensor("consts", [128, 8], F32, kind="ExternalInput").ap()
    o_xo = nc.dram_tensor("o_xo", [D, TPC], F32, kind="ExternalOutput").ap()

    r = Rec()
    with contextlib.ExitStack() as stack:
        ec = stack.enter_context
        x_tiles = [ec(nc.sbuf_tensor(f"x{c}", [128, TPC], F32)) for c in range(DC)]
        xl_tiles = [ec(nc.sbuf_tensor(f"xl{c}", [128, TPC], BF16)) for c in range(DC)]
        hid = [ec(nc.sbuf_tensor(f"hid{f}", [128, 512], BF16)) for f in range(NF1)]
        w1s = [ec(nc.sbuf_tensor(f"w1s{p}", [128, D], BF16)) for p in range(2)]
        w2s = ec(nc.sbuf_tensor("w2s", [128, H1], BF16))
        b1_sb = ec(nc.sbuf_tensor("b1S", [128, NF1], F32))
        b2_sb = ec(nc.sbuf_tensor("b2S", [128, DC], F32))
        sb = {
            "stats": ec(nc.sbuf_tensor("stats", [1, 3 * TPC], F32)),
            "sqt": ec(nc.sbuf_tensor("sqt", [128, 512], BF16)),
            "tmp": ec(nc.sbuf_tensor("tmp", [128, 512], F32)),
            "ones_bf": ec(nc.sbuf_tensor("onesbf", [128, 1], BF16)),
            "ones1": ec(nc.sbuf_tensor("ones1s", [1, 128], F32)),
            "consts": ec(nc.sbuf_tensor("constsS", [128, 8], F32)),
        }
        ps = {
            "sts": ec(nc.psum_tensor("psts", [1, 512], F32)),
            "stq": ec(nc.psum_tensor("pstq", [1, 512], F32)),
            "mb": ec(nc.psum_tensor("pmb", [128, 512], F32)),
            "rb": ec(nc.psum_tensor("prb", [128, 512], F32)),
        }
        px1 = ec(nc.psum_tensor("px1", [128, 512], F32))
        sems = _mk_sems(nc, stack)

        wx = [r.dma("sync", x_tiles[c][:], xT[128 * c:128 * (c + 1), :]) for c in range(DC)]
        wo1 = r.dma("sync", sb["ones_bf"][:], ones_bf_d)
        wo2 = r.dma("sync", sb["ones1"][:], ones1_d)
        wcn = r.dma("sync", sb["consts"][:], consts_d)
        wb1 = r.dma("sync", b1_sb[:], b1_d)
        wb2 = r.dma("sync", b2_sb[:], b2_d)

        wln = emit_ln(r, sb, ps, x_tiles, xl_tiles, TPC, deps=[wx[DC - 1], wo1, wo2, wcn, wb1, wb2])

        rd1 = [None, None]
        rd2 = None
        prev_gel = None
        prev_out_te = None
        prev_radd = None
        for h in range(2):
            sl = slice(512 * h, 512 * (h + 1))
            for f in range(NF1):
                par = f % 2
                psem = "lde" if par == 0 else "ldo"
                wd1 = r.dma("sync", w1s[par][:], qf1[f], waits=[w for w in (rd1[par],) if w], sem=psem)
                w0 = [wd1, wln] + ([prev_gel] if prev_gel else [])
                for c in range(DC):
                    r.op("tensor", lambda e, c=c, par=par, sl=sl: e.matmul(
                        px1[:], lhsT=w1s[par][:, 128 * c:128 * (c + 1)], rhs=xl_tiles[c][:, sl],
                        start=(c == 0), stop=(c == DC - 1)), waits=w0 if c == 0 else [])
                rd1[par] = ("te", r.cnt["te"])
                gw = [rd1[par]]
                if h == 1 and prev_out_te:
                    gw.append(prev_out_te)
                prev_gel = r.op("scalar", lambda e, f=f: e.activation(
                    out=hid[f][:], in_=px1[:], func=AF.Gelu, bias=b1_sb[:, f:f + 1]), waits=gw)
            whid = prev_gel
            for dci in range(DC):
                wd2 = r.dma("sync", w2s[:], qf2[dci], waits=[w for w in (rd2,) if w], sem="lde")
                w0 = [wd2, whid, prev_gel] + ([prev_radd] if prev_radd else [])
                for k in range(NF1):
                    r.op("tensor", lambda e, k=k: e.matmul(
                        px1[:], lhsT=w2s[:, 128 * k:128 * (k + 1)], rhs=hid[k][:],
                        start=(k == 0), stop=(k == NF1 - 1)), waits=w0 if k == 0 else [])
                rd2 = ("te", r.cnt["te"])
                prev_out_te = rd2
                prev_radd = r.op("vector", lambda e, dci=dci, sl=sl: e.scalar_tensor_tensor(
                    out=x_tiles[dci][:, sl], in0=px1[:], scalar=b2_sb[:, dci:dci + 1],
                    in1=x_tiles[dci][:, sl], op0=ALU.add, op1=ALU.add),
                    waits=[rd2])
                prev_gel = prev_radd
        for c in range(DC):
            r.dma("sync", o_xo[128 * c:128 * (c + 1), :], x_tiles[c][:], waits=[prev_radd] if c == 0 else [])
        r.wait("sync", [("ld", r.cnt["ld"])])
        with nc.Block() as block:
            r.replay(block, sems)
    return nc



QSPEC_F = [
    ("w1f", 704, D, 0),
    ("w2f", 704, D, 0),
    ("w3f", 256, H0, 0),
    ("f1f", 1024, D, 1),
    ("f2f", 256, H1, 1),
]
CSPEC_F = [("aqf", 256, D), ("akf", 256, D), ("avf", 256, D), ("aof", 256, D)]
GRP4 = [[0, 1, 2, 3], [4, 5, 6, 7]]
GRP8 = [list(range(NCORE))]


def build_fused(upto="F"):
    nc = bass.Bass("TRN2", target_bir_lowering=False, debug=False)
    ap = lambda t: t.ap()
    xT = nc.dram_tensor("xT", [D, TPC], F32, kind="ExternalInput").ap()
    wbcT = nc.dram_tensor("wbcT", [D, 128], F32, kind="ExternalInput").ap()
    wdtT = nc.dram_tensor("wdtT", [DT_RANK, D], F32, kind="ExternalInput").ap()
    bdt = nc.dram_tensor("bdt", [128, DC], F32, kind="ExternalInput").ap()
    alog_d = nc.dram_tensor("alog", [128, 128], F32, kind="ExternalInput").ap()
    dpar_d = nc.dram_tensor("dpar", [128, NDT], F32, kind="ExternalInput").ap()
    wsh_in = {}
    for (nm, rows, cols, _) in QSPEC_F:
        wsh_in[nm] = nc.dram_tensor(nm, [rows, cols], F32, kind="ExternalInput").ap()
    for (nm, rows, cols) in CSPEC_F:
        wsh_in[nm] = nc.dram_tensor(nm, [rows, cols], F32, kind="ExternalInput").ap()
    invcnt_d = nc.dram_tensor("invcnt", [1, 8], F32, kind="ExternalInput").ap()
    sel_d = nc.dram_tensor("sel", [1, 8], F32, kind="ExternalInput").ap()
    bq_d = nc.dram_tensor("bq", [128, DC], F32, kind="ExternalInput").ap()
    bk_d = nc.dram_tensor("bk", [128, DC], F32, kind="ExternalInput").ap()
    bvrow_d = nc.dram_tensor("bvrow", [1, D], F32, kind="ExternalInput").ap()
    bo_d = nc.dram_tensor("bo", [128, DC], F32, kind="ExternalInput").ap()
    b1_d = nc.dram_tensor("b1", [128, NF1], F32, kind="ExternalInput").ap()
    b2_d = nc.dram_tensor("b2", [128, DC], F32, kind="ExternalInput").ap()
    rope_d = {nm: nc.dram_tensor(nm, [128, TPC], BF16, kind="ExternalInput").ap()
              for nm in ("cosq", "sinq", "cosk", "sink")}
    mask_d = nc.dram_tensor("mask8", [NKT, 128, TPC], mybir.dt.int8, kind="ExternalInput").ap()
    ones_bf_d = nc.dram_tensor("ones_bf", [128, 1], BF16, kind="ExternalInput").ap()
    ones1_d = nc.dram_tensor("ones1", [1, 128], F32, kind="ExternalInput").ap()
    consts_d = nc.dram_tensor("consts", [128, 8], F32, kind="ExternalInput").ap()
    o_out = nc.dram_tensor("o_out", [D, TPC], F32, kind="ExternalOutput").ap()

    g_b = nc.dram_tensor("g_b", [1, 8], F32).ap()
    g_ar = nc.dram_tensor("g_ar", [1, 8], F32).ap()
    ag_in = {}
    ag_out = {}
    for (nm, rows, cols, _) in QSPEC_F:
        ag_in[nm] = nc.dram_tensor("agi_" + nm, [rows, cols], BF16).ap()
        ag_out[nm] = nc.dram_tensor("ago_" + nm, [NCORE * rows, cols], BF16).ap()
    for (nm, rows, cols) in CSPEC_F:
        ag_in[nm] = nc.dram_tensor("agi_" + nm, [rows, cols], BF16).ap()
        ag_out[nm] = nc.dram_tensor("ago_" + nm, [NCORE * rows, cols], BF16).ap()
    du_cat = nc.dram_tensor("du_cat", [4 * TPC, TPC], BF16).ap()
    du_ag = nc.dram_tensor("du_ag", [16 * TPC, TPC], BF16).ap()
    bc_in = nc.dram_tensor("bc_in", [64, TPC], BF16).ap()
    bc_ag = nc.dram_tensor("bc_ag", [256, TPC], BF16).ap()
    y_in = nc.dram_tensor("y_in", [D, TPC], BF16).ap()
    y_ag = nc.dram_tensor("y_ag", [4 * D, TPC], BF16).ap()
    q_loc = nc.dram_tensor("q_loc", [D, TPC], BF16).ap()
    k_in = nc.dram_tensor("k_in", [D, TPC], BF16).ap()
    k_ag = nc.dram_tensor("k_ag", [4 * D, TPC], BF16).ap()
    v_in = nc.dram_tensor("v_in", [D, TPC], BF16).ap()
    v_ag = nc.dram_tensor("v_ag", [4 * D, TPC], BF16).ap()
    x2_loc = nc.dram_tensor("x2_loc", [D, TPC], F32).ap()
    x3_loc = nc.dram_tensor("x3_loc", [D, TPC], F32).ap()

    r = Rec()
    rst = {}

    def dyn_expr(e, key, build):
        k = (e.engine, key)
        if k not in rst:
            rst[k] = build(e)
        return rst[k]

    def jc_of(e):
        return dyn_expr(e, "jc", lambda e: e.scalar_reg_alu(
            ALU.mod, e.partition_id(), 4))

    with contextlib.ExitStack() as stack:
        ec = stack.enter_context
        sb = {
            "stats": ec(nc.sbuf_tensor("stats", [1, 3 * TPC], F32)),
            "sqt": ec(nc.sbuf_tensor("sqt", [128, 512], BF16)),
            "tmp": ec(nc.sbuf_tensor("tmpG", [128, 512], F32)),
            "ones_bf": ec(nc.sbuf_tensor("onesbf", [128, 1], BF16)),
            "ones1": ec(nc.sbuf_tensor("ones1s", [1, 128], F32)),
            "consts": ec(nc.sbuf_tensor("constsS", [128, 8], F32)),
        }
        ps = {
            "sts": ec(nc.psum_tensor("psts", [1, 512], F32)),
            "stq": ec(nc.psum_tensor("pstq", [1, 512], F32)),
            "p0": ec(nc.psum_tensor("p0", [128, 512], F32)),
            "p1": ec(nc.psum_tensor("p1", [128, 512], F32)),
            "p2": ec(nc.psum_tensor("p2", [128, 512], F32)),
            "p3": ec(nc.psum_tensor("p3", [128, 512], F32)),
            "row": ec(nc.psum_tensor("prow", [1, 512], F32)),
        }
        ps["mb"] = ps["p0"]
        ps["rb"] = ps["p1"]
        sems = _mk_sems(nc, stack)
        cons = sb["consts"]

        wstack = contextlib.ExitStack()
        wbc_sb = wstack.enter_context(nc.sbuf_tensor("wbc", [128, DC * 128], BF16))
        wdt_sb = wstack.enter_context(nc.sbuf_tensor("wdt", [DT_RANK, D], BF16))
        wo1 = r.dma("sync", sb["ones_bf"][:], ones_bf_d)
        wo2 = r.dma("sync", sb["ones1"][:], ones1_d)
        wcn = r.dma("sync", sb["consts"][:], consts_d)
        waux = ("ld", r.cnt["ld"])
        ww1 = r.dma("gpsimd", wbc_sb[:].rearrange("p (c o) -> p c o", o=128),
                    wbcT.rearrange("(c p) o -> p c o", p=128))
        ww2 = r.dma("gpsimd", wdt_sb[:], wdtT)

        def phase_Q():
            with contextlib.ExitStack() as qs:
                qc = qs.enter_context
                wt = qc(nc.sbuf_tensor("wt", [128, H1], F32))
                qt = qc(nc.sbuf_tensor("qt", [128, H1], BF16))
                q2 = qc(nc.sbuf_tensor("q2", [128, H1], BF16))
                pacc = qc(nc.sbuf_tensor("pacc", [128, 16], F32))
                redc = qc(nc.sbuf_tensor("redc", [128, 1], F32))
                grow = qc(nc.sbuf_tensor("grow", [1, 16], F32))
                thb = qc(nc.sbuf_tensor("thb", [128, 8], F32))
                nthb = qc(nc.sbuf_tensor("nthb", [128, 8], F32))
                onescol = qc(nc.sbuf_tensor("onescol", [128, 1], F32))

                wic = r.dma("sync", grow[0:1, 8:16], invcnt_d)
                wsel = r.dma("sync", thb[0:1, 0:8], sel_d)
                wmo = r.op("vector", lambda e: e.memset(pacc[:], 0.0), waits=[])
                woc = r.op("vector", lambda e: e.memset(onescol[:], 1.0), waits=[])

                def tiles_of(rows):
                    out_ = []
                    p = 0
                    while p < rows:
                        out_.append((p, min(128, rows - p)))
                        p += 128
                    return out_

                prev = wmo
                wmm_dep = []
                for wi, (nm, rows, cols, _) in enumerate(QSPEC_F):
                    for ti, (p0_, np_) in enumerate(tiles_of(rows)):
                        wl = r.dma("sync", wt[0:np_, 0:cols], wsh_in[nm][p0_:p0_ + np_, :], waits=[prev])
                        prev = r.op("vector", lambda e, np_=np_, cols=cols, ti=ti: e.reduce_sum(
                            out=pacc[0:np_, ti + 1:ti + 2], in_=wt[0:np_, 0:cols],
                            axis=mybir.AxisListType.X, apply_absolute_value=True), waits=[wl])
                    prev = r.op("vector", lambda e: e.reduce_sum(
                        out=redc[:], in_=pacc[:, 0:16], axis=mybir.AxisListType.X), waits=[prev])
                    wq = r.op("tensor", lambda e, wi=wi: e.matmul(
                        ps["row"][0:1, wi:wi + 1], lhsT=onescol[:], rhs=redc[:],
                        start=True, stop=True), waits=[prev, woc])
                    wmm_dep.append(wq)
                    prev = r.op("vector", lambda e: e.memset(pacc[:], 0.0), waits=[wq])
                wmg = r.op("vector", lambda e: e.memset(grow[0:1, 0:8], 1.0), waits=[])
                wg = r.op("scalar", lambda e: e.copy(out=grow[0:1, 0:5], in_=ps["row"][0:1, 0:5]), waits=wmm_dep + [wmg])
                wgd = r.dma("gpsimd", g_b, grow[0:1, 0:8], waits=[wg])
                war = r.collective("AllReduce", ALU.add, GRP8, [g_b.opt()], [g_ar.opt()], waits=[wgd])
                wgb = r.dma("sync", grow[0:1, 0:8], g_ar, waits=[war])

                wm1 = r.op("vector", lambda e: e.tensor_mul(out=grow[0:1, 0:8], in0=grow[0:1, 0:8], in1=grow[0:1, 8:16]), waits=[wgb, wic])
                wa_ = r.op("vector", lambda e: e.tensor_scalar_add(out=grow[0:1, 8:16], in0=grow[0:1, 0:8], scalar1=1e-8), waits=[wm1])
                wb_ = r.op("vector", lambda e: e.tensor_scalar_max(out=grow[0:1, 0:8], in0=grow[0:1, 0:8], scalar1=1e-8), waits=[wa_])
                wbl1 = r.op("vector", lambda e: e.tensor_sub(out=grow[0:1, 8:16], in0=grow[0:1, 8:16], in1=grow[0:1, 0:8]), waits=[wa_, wb_])
                wbl2 = r.op("vector", lambda e: e.tensor_mul(out=grow[0:1, 8:16], in0=grow[0:1, 8:16], in1=thb[0:1, 0:8]), waits=[wbl1, wsel])
                wbl3 = r.op("vector", lambda e: e.tensor_add(out=grow[0:1, 0:8], in0=grow[0:1, 0:8], in1=grow[0:1, 8:16]), waits=[wbl2])
                wth = r.op("vector", lambda e: e.tensor_scalar_mul(out=grow[0:1, 0:8], in0=grow[0:1, 0:8], scalar1=0.5), waits=[wbl3])
                wtb = r.op("tensor", lambda e: e.matmul(ps["p2"][0:128, 0:8], lhsT=sb["ones1"][:], rhs=grow[0:1, 0:8], start=True, stop=True), waits=[wth, waux])
                wtc = r.op("vector", lambda e: e.tensor_copy(out=thb[:], in_=ps["p2"][0:128, 0:8]), waits=[wtb])
                wtn = r.op("vector", lambda e: e.tensor_scalar_mul(out=nthb[:], in0=thb[:], scalar1=-1.0), waits=[wtc])

                prev_q = wtn
                prev_out = None
                ag_store = {}
                for wi, (nm, rows, cols, _) in enumerate(QSPEC_F):
                    for (p0_, np_) in tiles_of(rows):
                        wl = r.dma("sync", wt[0:np_, 0:cols], wsh_in[nm][p0_:p0_ + np_, :],
                                   waits=[prev_q] + ([prev_out] if prev_out else []))
                        w1w = [wl]
                        w1_ = r.op("vector", lambda e, np_=np_, cols=cols, wi=wi: e.tensor_scalar(
                            out=qt[0:np_, 0:cols], in0=wt[0:np_, 0:cols], scalar1=thb[0:np_, wi:wi + 1],
                            scalar2=None, op0=ALU.is_ge), waits=w1w)
                        w2_ = r.op("vector", lambda e, np_=np_, cols=cols, wi=wi: e.tensor_scalar(
                            out=q2[0:np_, 0:cols], in0=wt[0:np_, 0:cols], scalar1=nthb[0:np_, wi:wi + 1],
                            scalar2=None, op0=ALU.is_le), waits=[])
                        w3_ = r.op("vector", lambda e, np_=np_, cols=cols: e.tensor_sub(
                            out=qt[0:np_, 0:cols], in0=qt[0:np_, 0:cols], in1=q2[0:np_, 0:cols]), waits=[w1_, w2_])
                        prev_out = r.dma("sync", ag_in[nm][p0_:p0_ + np_, :], qt[0:np_, 0:cols], waits=[w3_])
                        prev_q = w3_
                    ag_store[nm] = prev_out
                for (nm, rows, cols) in CSPEC_F:
                    for (p0_, np_) in tiles_of(rows):
                        wl = r.dma("sync", wt[0:np_, 0:cols], wsh_in[nm][p0_:p0_ + np_, :],
                                   waits=[prev_q] + ([prev_out] if prev_out else []))
                        cw = [wl]
                        prev_q = r.op("vector", lambda e, np_=np_, cols=cols: e.tensor_copy(
                            out=qt[0:np_, 0:cols], in_=wt[0:np_, 0:cols]), waits=cw)
                        prev_out = r.dma("sync", ag_in[nm][p0_:p0_ + np_, :], qt[0:np_, 0:cols], waits=[prev_q])
                    ag_store[nm] = prev_out
            return ag_store
        ag_store = phase_Q()
        ag_waits = {}

        def phase_A():
            r.wait("sync", r.fence())
            with contextlib.ExitStack() as asx:
                ac = asx.enter_context
                x_tiles = [ac(nc.sbuf_tensor(f"ax{c}", [128, TPC], F32)) for c in range(DC)]
                xl_tiles = [ac(nc.sbuf_tensor(f"axl{c}", [128, TPC], BF16)) for c in range(DC)]
                bdt_sb = ac(nc.sbuf_tensor("bdtS", [128, DC], F32))
                dtr_sb = ac(nc.sbuf_tensor("dtr", [DT_RANK, TPC], BF16))
                bc_sb = ac(nc.sbuf_tensor("bcS", [64, TPC], BF16))
                delta_sb = [ac(nc.sbuf_tensor(f"adl{c}", [128, TPC], BF16)) for c in range(DC)]
                t_abs = ac(nc.sbuf_tensor("tabs", [128, 512], F32))
                t_rel = ac(nc.sbuf_tensor("trel", [128, 512], F32))
                ps_db = [ps["p2"], ps["p3"]]
                ps_dt = [ps["p0"], ps["p1"]]

                wx = [r.dma("sync", x_tiles[c][:], xT[128 * c:128 * (c + 1), :]) for c in range(DC)]
                wb = r.dma("sync", bdt_sb[:], bdt)

                wln = emit_ln(r, sb, ps, x_tiles, xl_tiles, TPC, deps=[wx[DC - 1], waux])

                ev_dtr = [None, None]
                for h in range(2):
                    sl = slice(512 * h, 512 * (h + 1))
                    for c in range(DC):
                        r.op("tensor", lambda e, c=c, h=h, sl=sl: e.matmul(
                            ps_db[h][:], lhsT=wbc_sb[:, 128 * c:128 * (c + 1)], rhs=xl_tiles[c][:, sl],
                            start=(c == 0), stop=(c == DC - 1)),
                            waits=[wln, ww1] if c == 0 else [])
                    wt_ = ("te", r.cnt["te"])
                    ev_dtr[h] = r.op("scalar", lambda e, h=h, sl=sl: e.copy(out=dtr_sb[:, sl], in_=ps_db[h][0:64, :]), waits=[wt_])
                    r.op("scalar", lambda e, h=h, sl=sl: e.copy(out=bc_sb[:, sl], in_=ps_db[h][64:128, :]), waits=[])
                wbc_done = ("sc", r.cnt["sc"])
                wbcst = r.dma("sync", bc_in, bc_sb[:], waits=[wbc_done])
                wag_bc = r.collective("AllGather", ALU.bypass, GRP4, [bc_in.opt()], [bc_ag.opt()], waits=[wbcst])

                prev_psum_reader = [None, None]
                prev_vadd = None
                du_st = None
                for c in range(DC):
                    for h in range(2):
                        sl = slice(512 * h, 512 * (h + 1))
                        w0 = [ev_dtr[h], ww2, wb]
                        if prev_psum_reader[h] is not None:
                            w0.append(prev_psum_reader[h])
                        wmm = r.op("tensor", lambda e, c=c, h=h, sl=sl: e.matmul(
                            ps_dt[h][:], lhsT=wdt_sb[:, 128 * c:128 * (c + 1)], rhs=dtr_sb[:, sl],
                            start=True, stop=True), waits=w0)
                        wa_w = [wmm] + ([prev_vadd] if prev_vadd is not None else [])
                        wa = r.op("scalar", lambda e, c=c, h=h: e.activation(
                            out=t_abs[:], in_=ps_dt[h][:], func=AF.Abs, bias=bdt_sb[:, c:c + 1]), waits=wa_w)
                        wr2 = r.op("scalar", lambda e, c=c, h=h: e.activation(
                            out=t_rel[:], in_=ps_dt[h][:], func=AF.Relu, bias=bdt_sb[:, c:c + 1]), waits=[])
                        prev_psum_reader[h] = wr2
                        we = r.op("scalar", lambda e: e.activation(out=t_abs[:], in_=t_abs[:], func=AF.Exp, scale=cons[:, C_NEG1:C_NEG1 + 1]), waits=[wa])
                        wl = r.op("scalar", lambda e: e.activation(out=t_abs[:], in_=t_abs[:], func=AF.Ln, bias=cons[:, C_ONE:C_ONE + 1]), waits=[we])
                        prev_vadd = r.op("vector", lambda e, c=c, sl=sl: e.tensor_add(
                            out=delta_sb[c][:, sl], in0=t_rel[:], in1=t_abs[:]), waits=[wl, wr2])
                    row = 1024 * (c // 4) + 128 * (c % 4)
                    du_st = r.dma("sync", du_cat[row:row + 128, :], delta_sb[c][:], waits=[prev_vadd])
                    du_st = r.dma("sync", du_cat[row + 512:row + 640, :], xl_tiles[c][:], waits=[])
                wag_du = r.collective("AllGather", ALU.bypass, GRP4, [du_cat.opt()], [du_ag.opt()], waits=[du_st])
                for nm in ag_store:
                    ag_waits[nm] = r.collective("AllGather", ALU.bypass, GRP8,
                                                [ag_in[nm].opt()], [ag_out[nm].opt()],
                                                waits=[ag_store[nm]])
            return wag_bc, wag_du
        if upto >= "A":
            wag_bc, wag_du = phase_A()

        def phase_B():
            r.wait("sync", r.fence())
            with contextlib.ExitStack() as bs:
                bcx = bs.enter_context
                dA = bcx(nc.sbuf_tensor("dA", [128, D_STATE * TC1], BF16))
                dBu = bcx(nc.sbuf_tensor("dBu", [128, D_STATE * TC1], BF16))
                hh = bcx(nc.sbuf_tensor("hh", [128, D_STATE * TC1], BF16))
                brep = bcx(nc.sbuf_tensor("brep", [128, D_STATE * TC], BF16))
                crep = bcx(nc.sbuf_tensor("crep", [128, D_STATE * TC], BF16))
                dlu8 = bcx(nc.sbuf_tensor("dlu8", [128, 8 * TPC], BF16))
                du_sb = bcx(nc.sbuf_tensor("du", [128, TC], BF16))
                y_sb = bcx(nc.sbuf_tensor("yS", [128, TC], BF16))
                carry = bcx(nc.sbuf_tensor("carry", [128, D_STATE * NDT], BF16))
                alog_sb = bcx(nc.sbuf_tensor("alogS", [128, 128], F32))
                a_sb = bcx(nc.sbuf_tensor("aS", [128, 128], F32))
                dpar_sb = bcx(nc.sbuf_tensor("dparS", [128, NDT], F32))

                dA3 = dA[:].rearrange("p (n t) -> p n t", t=TC1)
                dBu3 = dBu[:].rearrange("p (n t) -> p n t", t=TC1)
                hh3 = hh[:].rearrange("p (n t) -> p n t", t=TC1)
                brep3 = brep[:].rearrange("p (n t) -> p n t", t=TC)
                crep3 = crep[:].rearrange("p (n t) -> p n t", t=TC)

                wal = r.dma("sync", alog_sb[:], alog_d)
                wdp = r.dma("sync", dpar_sb[:], dpar_d)
                wexp0 = r.op("scalar", lambda e: e.activation(out=a_sb[:], in_=alog_sb[:], func=AF.Exp), waits=[wal])
                wA = r.op("vector", lambda e: e.tensor_scalar_mul(out=a_sb[:], in0=a_sb[:], scalar1=-1.0), waits=[wexp0])

                wpad1 = r.op("vector", lambda e: e.memset(dA3[:, :, TC + 1:TC + 2], 0.0), waits=[])
                wpad2 = r.op("vector", lambda e: e.memset(dBu3[:, :, TC + 1:TC + 2], 0.0), waits=[wpad1])
                wpad3 = r.op("vector", lambda e: e.memset(hh3[:, :, TC + 1:TC + 2], 0.0), waits=[wpad2])
                prev_ve = wpad3
                prev_sc = wexp0
                prev_ystore = None
                ds_ = bass.ds
                for ch in range(NCH):
                    tsl = slice(TC * ch, TC * (ch + 1))
                    rnk = ch // 2
                    csl = slice(512 * (ch % 2), 512 * (ch % 2 + 1))
                    wbr = r.dma("sync", brep3,
                                bc_ag[64 * rnk:64 * rnk + 32, csl].unsqueeze(0).broadcast_to([128, 32, TC]),
                                waits=[prev_ve, wag_bc])
                    wcr = r.dma("sync", crep3,
                                bc_ag[64 * rnk + 32:64 * rnk + 64, csl].unsqueeze(0).broadcast_to([128, 32, TC]),
                                waits=[])
                    if ch % 2 == 0:
                        def mk_dma(waits, rnk=rnk):
                            sem = "ld"
                            r.cnt[sem] += 16
                            tgt = r.cnt[sem]
                            waits2 = tuple((s, r.cnt[s]) if s in r.DMA_SEMS else (s, v) for (s, v) in waits)
                            def op(e, sems, waits2=waits2, tgt=tgt):
                                for (s, v) in waits2:
                                    e.wait_ge(sems[s], v)
                                off = dyn_expr(e, ("du", rnk), lambda e: e.scalar_reg_alu(
                                    ALU.add, e.scalar_reg_alu(ALU.mult, jc_of(e), 1024), 4096 * rnk))
                                e.dma_start(
                                    out=dlu8[:].rearrange("p (blk t) -> p blk t", t=TPC),
                                    in_=du_ag[ds_(off, 1024), :].rearrange("(blk p) t -> p blk t", p=128),
                                ).then_inc(sems[sem], 16)
                            r.ops["sync"].append(op)
                            return (sem, tgt)
                        wdu8 = mk_dma([prev_sc, prev_ve, wag_du])
                    for dt in range(NDT):
                        dl_sb = dlu8[:, dt * TPC + 512 * (ch % 2): dt * TPC + 512 * (ch % 2) + TC]
                        u_sb = dlu8[:, (4 + dt) * TPC + 512 * (ch % 2): (4 + dt) * TPC + 512 * (ch % 2) + TC]
                        wld = wdu8
                        wlu = wdu8
                        wexps = None
                        for n in range(D_STATE):
                            wexps = r.op("scalar", lambda e, n=n, dt=dt, dl_sb=dl_sb: e.activation(
                                out=dA[:, n * TC1 + 1: n * TC1 + 1 + TC], in_=dl_sb,
                                func=AF.Exp, scale=a_sb[:, dt * 32 + n: dt * 32 + n + 1]),
                                waits=[wld, prev_ve, wA] if n == 0 else [])
                        wms = r.op("vector", lambda e: e.memset(dA3[:, :, 0:1], 0.0), waits=[prev_ve] if ch + dt else [])
                        wdu = r.op("vector", lambda e, dl_sb=dl_sb, u_sb=u_sb: e.tensor_mul(
                            out=du_sb[:], in0=dl_sb, in1=u_sb), waits=[wld, wlu])
                        wdb = r.op("vector", lambda e: e.tensor_tensor(
                            out=dBu3[:, :, 1:TC + 1], in0=du_sb[:].unsqueeze(1).broadcast_to([128, 32, TC]),
                            in1=brep3, op=ALU.mult), waits=[wbr, wdu])
                        if ch == 0:
                            wcc = r.op("vector", lambda e, dt=dt: e.memset(dBu3[:, :, 0:1], 0.0), waits=[])
                        else:
                            wcc = r.op("vector", lambda e, dt=dt: e.tensor_copy(
                                out=dBu3[:, :, 0:1], in_=carry[:, dt * 32:(dt + 1) * 32].unsqueeze(2)), waits=[])
                        wsc_ = r.op("vector", lambda e: e.tensor_tensor_scan(
                            out=hh[:], data0=dA[:], data1=dBu[:], initial=0.0,
                            op0=ALU.mult, op1=ALU.add), waits=[wexps, wms, wdb, wcc])
                        wco = r.op("vector", lambda e, dt=dt: e.tensor_copy(
                            out=carry[:, dt * 32:(dt + 1) * 32].unsqueeze(2), in_=hh3[:, :, TC:TC + 1]), waits=[wsc_])
                        whc = r.op("vector", lambda e: e.tensor_tensor(
                            out=dA3[:, :, 1:TC + 1], in0=hh3[:, :, 1:TC + 1], in1=crep3, op=ALU.mult), waits=[wcr, wsc_])
                        wtr = whc
                        for half in (16, 8, 4, 2, 1):
                            wtr = r.op("vector", lambda e, half=half: e.tensor_tensor(
                                out=dA3[:, 0:half, 1:TC + 1], in0=dA3[:, 0:half, 1:TC + 1],
                                in1=dA3[:, half:2 * half, 1:TC + 1], op=ALU.add), waits=[wtr])
                        yw = [wdp, wtr] + ([prev_ystore] if prev_ystore else [])
                        wy = r.op("vector", lambda e, dt=dt, u_sb=u_sb: e.scalar_tensor_tensor(
                            out=y_sb[:], in0=u_sb, scalar=dpar_sb[:, dt:dt + 1],
                            in1=dA[:, 1:TC + 1], op0=ALU.mult, op1=ALU.add), waits=yw)
                        yrow = 512 * rnk + 128 * dt
                        prev_ystore = r.dma("sync", y_in[yrow:yrow + 128, csl], y_sb[:], waits=[wy])
                        prev_ve = wy
                        prev_sc = wexps
                wag_y = r.collective("AllGather", ALU.bypass, GRP4, [y_in.opt()], [y_ag.opt()], waits=[prev_ystore])

            return wag_y
        wstack.close()
        if upto >= "B":
            wag_y = phase_B()

        def phase_C():
            r.wait("sync", r.fence())
            with contextlib.ExitStack() as cs:
                cc = cs.enter_context
                x_tiles = [cc(nc.sbuf_tensor(f"cx{c}", [128, TPC], F32)) for c in range(DC)]
                xlbig = cc(nc.sbuf_tensor("cxlbig", [128, DC * TPC], BF16))
                xl_tiles = [xlbig[:, TPC * c:TPC * (c + 1)] for c in range(DC)]
                hid = [cc(nc.sbuf_tensor(f"chid{f}", [128, 512], BF16)) for f in range(NF0)]
                w1s = [cc(nc.sbuf_tensor(f"cw1s{p}", [128, D], BF16)) for p in range(2)]
                w2s = [cc(nc.sbuf_tensor(f"cw2s{p}", [128, D], BF16)) for p in range(2)]
                w3s = [cc(nc.sbuf_tensor(f"cw3s{p}", [128, H0], BF16)) for p in range(2)]
                sg_sb = cc(nc.sbuf_tensor("csg", [128, 512], BF16))
                px1, px2 = ps["p2"], ps["p3"]

                wx = [r.dma("sync", x_tiles[c][:], xT[128 * c:128 * (c + 1), :]) for c in range(DC)]
                wy_l = [None] * DC
                for i in range(4):
                    def mk_ydma(i=i):
                        sem = "ld"
                        r.cnt[sem] += 16
                        tgt = r.cnt[sem]
                        def op(e, sems, i=i, tgt=tgt):
                            e.wait_ge(sems["cc"], wag_y[1])
                            off = dyn_expr(e, ("y", i), lambda e: e.scalar_reg_alu(
                                ALU.add, e.scalar_reg_alu(ALU.mult, jc_of(e), 512), 2048 * i))
                            e.dma_start(
                                out=xlbig[:, 4 * TPC * i:4 * TPC * (i + 1)].rearrange("p (c t) -> p c t", t=TPC),
                                in_=y_ag[bass.ds(off, 512), :].rearrange("(c p) t -> p c t", p=128),
                            ).then_inc(sems[sem], 16)
                        r.ops["sync"].append(op)
                        return (sem, tgt)
                    w_ = mk_ydma()
                    for cdx in range(4 * i, 4 * (i + 1)):
                        wy_l[cdx] = w_
                wadd = None
                for c in range(DC):
                    wadd = r.op("vector", lambda e, c=c: e.tensor_tensor(
                        out=x_tiles[c][:], in0=x_tiles[c][:], in1=xl_tiles[c][:], op=ALU.add),
                        waits=[wx[c], wy_l[c]])
                wln = emit_ln(r, sb, ps, x_tiles, xl_tiles, TPC, deps=[wadd])

                rd1 = [None, None]
                rd2 = [None, None]
                rd3 = [None, None]
                prev_sig = None
                prev_hct = None
                prev_out_te = None
                prev_radd = None
                for h in range(2):
                    sl = slice(512 * h, 512 * (h + 1))
                    for f in range(NF0):
                        par = f % 2
                        wd1 = r.dma("sync", w1s[par][:], ag_out["w1f"][128 * f:128 * (f + 1), :],
                                    waits=[w for w in (rd1[par], ag_waits["w1f"]) if w], sem="t0" if par == 0 else "t1")
                        wd2 = r.dma("sync", w2s[par][:], ag_out["w2f"][128 * f:128 * (f + 1), :],
                                    waits=[w for w in (rd2[par], ag_waits["w2f"]) if w], sem="t2" if par == 0 else "t3")
                        w0 = [wd1, wln] + ([prev_sig] if prev_sig else [])
                        for c in range(DC):
                            r.op("tensor", lambda e, c=c, par=par, sl=sl: e.matmul(
                                px1[:], lhsT=w1s[par][:, 128 * c:128 * (c + 1)], rhs=xl_tiles[c][:, sl],
                                start=(c == 0), stop=(c == DC - 1)), waits=w0 if c == 0 else [])
                        rd1[par] = ("te", r.cnt["te"])
                        wte1 = rd1[par]
                        w0 = [wd2] + ([prev_hct] if prev_hct else [])
                        for c in range(DC):
                            r.op("tensor", lambda e, c=c, par=par, sl=sl: e.matmul(
                                px2[:], lhsT=w2s[par][:, 128 * c:128 * (c + 1)], rhs=xl_tiles[c][:, sl],
                                start=(c == 0), stop=(c == DC - 1)), waits=w0 if c == 0 else [])
                        rd2[par] = ("te", r.cnt["te"])
                        wte2 = rd2[par]
                        sgw = [wte1] + ([prev_hct] if prev_hct else [])
                        prev_sig = r.op("scalar", lambda e: e.activation(
                            out=sg_sb[:], in_=px1[:], func=AF.Sigmoid), waits=sgw)
                        hw = [wte2, prev_sig] + ([prev_out_te] if (prev_out_te and h == 1) else [])
                        prev_hct = r.op("vector", lambda e, f=f: e.tensor_tensor(
                            out=hid[f][:], in0=px2[:], in1=sg_sb[:], op=ALU.mult), waits=hw)
                    whid = prev_hct
                    for dci in range(DC):
                        par = dci % 2
                        wd3 = r.dma("sync", w3s[par][:], ag_out["w3f"][128 * dci:128 * (dci + 1), :],
                                    waits=[w for w in (rd3[par], ag_waits["w3f"]) if w],
                                    sem="t4" if par == 0 else "t5")
                        w0 = [wd3, whid, prev_sig] + ([prev_radd] if prev_radd else [])
                        for k in range(NF0):
                            r.op("tensor", lambda e, k=k, par=par: e.matmul(
                                px1[:], lhsT=w3s[par][:, 128 * k:128 * (k + 1)], rhs=hid[k][:],
                                start=(k == 0), stop=(k == NF0 - 1)), waits=w0 if k == 0 else [])
                        rd3[par] = ("te", r.cnt["te"])
                        prev_out_te = rd3[par]
                        prev_radd = r.op("vector", lambda e, dci=dci, sl=sl: e.tensor_tensor(
                            out=x_tiles[dci][:, sl], in0=x_tiles[dci][:, sl], in1=px1[:], op=ALU.add),
                            waits=[rd3[par]])
                        prev_sig = prev_radd
                x2st = None
                for c in range(DC):
                    x2st = r.dma("sync", x2_loc[128 * c:128 * (c + 1), :], x_tiles[c][:],
                                 waits=[prev_radd] if c == 0 else [])
            return x2st
        if upto >= "C":
            x2st = phase_C()

        def phase_D():
            r.wait("sync", r.fence())
            with contextlib.ExitStack() as dsx:
                dcx = dsx.enter_context
                xl_tiles = [dcx(nc.sbuf_tensor(f"dxl{c}", [128, TPC], BF16)) for c in range(DC)]
                wsl = [dcx(nc.sbuf_tensor(f"dwsl{p}", [128, D], BF16)) for p in range(2)]
                bias_sb = {nm: dcx(nc.sbuf_tensor(nm + "S", [128, DC], F32)) for nm in ("bq", "bk")}
                pq = ps["p2"]

                with contextlib.ExitStack() as dxs:
                    x_tiles = [dxs.enter_context(nc.sbuf_tensor(f"dx{c}", [128, TPC], F32)) for c in range(DC)]
                    wx = [r.dma("sync", x_tiles[c][:], x2_loc[128 * c:128 * (c + 1), :], waits=[x2st]) for c in range(DC)]
                    wln = emit_ln(r, sb, ps, x_tiles, xl_tiles, TPC, deps=[wx[DC - 1], waux])

                av_res = [dcx(nc.sbuf_tensor(f"dav{c}", [128, D], BF16)) for c in range(DC)]
                pt = dcx(nc.sbuf_tensor("dpt", [128, TPC], BF16))
                rt = dcx(nc.sbuf_tensor("drt", [128, TPC], BF16))
                rt2 = dcx(nc.sbuf_tensor("drt2", [128, TPC], BF16))
                vt_t = dcx(nc.sbuf_tensor("dvt", [128, D], BF16))
                bvr = dcx(nc.sbuf_tensor("dbvr", [128, D], BF16))
                bvrow_sb = dcx(nc.sbuf_tensor("dbvrow", [1, D], F32))
                tabs = {nm: dcx(nc.sbuf_tensor(nm + "S", [128, TPC], BF16))
                        for nm in ("cosq", "sinq", "cosk", "sink")}
                for nm, d_ in (("cosq", rope_d["cosq"]), ("sinq", rope_d["sinq"]),
                               ("cosk", rope_d["cosk"]), ("sink", rope_d["sink"])):
                    r.dma("sync", tabs[nm][:], d_, waits=[wln])
                r.dma("sync", bias_sb["bq"][:], bq_d)
                r.dma("sync", bias_sb["bk"][:], bk_d)
                wbv = r.dma("sync", bvrow_sb[:], bvrow_d)
                wavl = [r.dma("sync", av_res[c][:], ag_out["avf"][128 * c:128 * (c + 1), :],
                              waits=[ag_waits["avf"]] if c == 0 else []) for c in range(DC)]
                wtabs2 = ("ld", r.cnt["ld"])
                wbvb = None
                for j in range(4):
                    wmmb = r.op("tensor", lambda e, j=j: e.matmul(
                        ps["p3"][:], lhsT=sb["ones1"][:], rhs=bvrow_sb[0:1, 512 * j:512 * (j + 1)],
                        start=True, stop=True), waits=[wbv, waux] + ([wbvb] if wbvb else []))
                    wbvb = r.op("vector", lambda e, j=j: e.tensor_copy(
                        out=bvr[:, 512 * j:512 * (j + 1)], in_=ps["p3"][:]), waits=[wmmb])
                wln = wtabs2

                rdw = [None, None]
                prev_ev = wbvb
                prev_store = None
                projs = [("q", "aqf", "bq", q_loc, "cosq", "sinq"),
                         ("k", "akf", "bk", k_in, "cosk", "sink")]
                fidx = 0
                for (pname, wnm, bnm, odst, cosn, sinn) in projs:
                    for f in range(DC):
                        par = fidx % 2
                        wd = r.dma("sync", wsl[par][:], ag_out[wnm][128 * f:128 * (f + 1), :],
                                   waits=[w for w in (rdw[par], ag_waits[wnm]) if w],
                                   sem="t0" if par == 0 else "t1")
                        for h in range(2):
                            sl = slice(512 * h, 512 * (h + 1))
                            w0 = [wd, wtabs2] + ([prev_ev] if prev_ev else [])
                            for c in range(DC):
                                r.op("tensor", lambda e, c=c, par=par, sl=sl: e.matmul(
                                    pq[:], lhsT=wsl[par][:, 128 * c:128 * (c + 1)], rhs=xl_tiles[c][:, sl],
                                    start=(c == 0), stop=(c == DC - 1)), waits=w0 if c == 0 else [])
                            rdw[par] = ("te", r.cnt["te"])
                            evw = [rdw[par]] + ([prev_store] if prev_store else [])
                            prev_ev = r.op("scalar", lambda e, sl=sl, bnm=bnm, f=f: e.activation(
                                out=pt[:, sl], in_=pq[:], func=AF.Identity,
                                bias=bias_sb[bnm][:, f:f + 1]), waits=evw)
                        wc1 = r.op("vector", lambda e: e.tensor_copy(out=rt[0:64, :], in_=pt[64:128, :]), waits=[prev_ev])
                        wc2 = r.op("vector", lambda e: e.tensor_copy(out=rt[64:128, :], in_=pt[0:64, :]), waits=[prev_ev])
                        wm1 = r.op("vector", lambda e, cosn=cosn: e.tensor_tensor(
                            out=pt[:], in0=pt[:], in1=tabs[cosn][:], op=ALU.mult), waits=[wc1, wc2])
                        wm2 = r.op("vector", lambda e, sinn=sinn: e.tensor_tensor(
                            out=rt2[:], in0=rt[:], in1=tabs[sinn][:], op=ALU.mult), waits=[wm1])
                        wad = r.op("vector", lambda e: e.tensor_tensor(
                            out=pt[:], in0=pt[:], in1=rt2[:], op=ALU.add), waits=[wm2])
                        prev_ev = wad
                        prev_store = r.dma("sync", odst[128 * f:128 * (f + 1), :], pt[:], waits=[prev_ev])
                        fidx += 1
                wag_k = r.collective("AllGather", ALU.bypass, GRP4, [k_in.opt()], [k_ag.opt()], waits=[prev_store])
                v_view = v_in.rearrange("(h p) (kt i) -> p h kt i", p=128, i=128)
                prev_vst = None
                for tc in range(8):
                    for j in range(4):
                        jsl = slice(512 * j, 512 * (j + 1))
                        w0 = [wavl[DC - 1]] + ([prev_ev] if prev_ev else [])
                        for c in range(DC):
                            r.op("tensor", lambda e, c=c, tc=tc, jsl=jsl: e.matmul(
                                pq[:], lhsT=xl_tiles[c][:, 128 * tc:128 * (tc + 1)], rhs=av_res[c][:, jsl],
                                start=(c == 0), stop=(c == DC - 1)), waits=w0 if c == 0 else [])
                        wte_ = ("te", r.cnt["te"])
                        vw = [wte_, wbvb] + ([prev_vst] if prev_vst else [])
                        prev_ev = r.op("vector", lambda e, jsl=jsl: e.tensor_tensor(
                            out=vt_t[:, jsl], in0=pq[:], in1=bvr[:, jsl], op=ALU.add), waits=vw)
                    wvst = r.dma("sync", v_view[:, :, tc, :],
                                 vt_t[:].rearrange("p (h i) -> p h i", i=128), waits=[prev_ev])
                    prev_vst = wvst
                wag_v = r.collective("AllGather", ALU.bypass, GRP4, [v_in.opt()], [v_ag.opt()], waits=[wvst])
            return wag_k, wag_v
        if upto >= "D":
            wag_k, wag_v = phase_D()

        def phase_E():
            r.wait("sync", r.fence())
            with contextlib.ExitStack() as es:
                ecx = es.enter_context
                mg = [ecx(nc.sbuf_tensor(f"emg{c}", [128, TPC], BF16)) for c in range(NH)]
                kvh = [ecx(nc.sbuf_tensor(f"ekvh{p}", [128, SEQ], BF16)) for p in range(2)]
                s_sb = ecx(nc.sbuf_tensor("esS", [128, 512], BF16))
                binv = ecx(nc.sbuf_tensor("ebinv", [128, 512], BF16))
                srow = ecx(nc.sbuf_tensor("esrow", [1, 512], F32))
                bo_sb = ecx(nc.sbuf_tensor("eboS", [128, DC], F32))
                eqs = contextlib.ExitStack()
                qf = [eqs.enter_context(nc.sbuf_tensor(f"eqf{c}", [128, TPC], BF16)) for c in range(DC)]
                m8 = eqs.enter_context(nc.sbuf_tensor("em8", [128, NKT * TPC], mybir.dt.int8))
                vvh = [eqs.enter_context(nc.sbuf_tensor(f"evvh{p}", [128, SEQ], BF16)) for p in range(2)]
                ps_s = [ps["p0"], ps["p1"]]
                ps_ctx = ps["p2"]
                ps_b = ps["p3"]
                ps_sum = ps["row"]

                wq = [r.dma("sync", qf[c][:], q_loc[128 * c:128 * (c + 1), :],
                            waits=[wag_k] if c == 0 else []) for c in range(DC)]
                wm8 = r.dma("sync", m8[:].rearrange("p (kt t) -> p kt t", t=TPC),
                            mask_d.rearrange("kt p t -> p kt t"))
                wbo = r.dma("sync", bo_sb[:], bo_d)
                m83 = m8[:].rearrange("p (kt t) -> p kt t", t=TPC)
                wall = ("ld", r.cnt["ld"])

                rdk = [None, None]
                prev_exp = None
                prev_pv = None
                prev_ctxev = None
                prev_sumev = None
                prev_bev = None
                for h in range(NH):
                    par = h % 2
                    psem = "t0" if par == 0 else "t1"
                    kw = [w for w in (rdk[par], wag_k) if w]
                    wkh = None
                    for rr in range(4):
                        wkh = r.dma("sync", kvh[par][:, 1024 * rr:1024 * (rr + 1)],
                                    k_ag[2048 * rr + 128 * h:2048 * rr + 128 * (h + 1), :],
                                    waits=kw if rr == 0 else [], sem=psem)
                    wvh = None
                    for rr in range(4):
                        wvh = r.dma("sync", vvh[par][:, 1024 * rr:1024 * (rr + 1)],
                                    v_ag[2048 * rr + 128 * h:2048 * rr + 128 * (h + 1), :],
                                    waits=[wag_v] if rr == 0 else [], sem=psem)
                    for qc in range(2):
                        sl = slice(512 * qc, 512 * (qc + 1))
                        for kt in range(NKT):
                            pp = kt % 2
                            w0 = [wkh, wall] + ([prev_exp] if prev_exp else [])
                            wmm = r.op("tensor", lambda e, h=h, kt=kt, par=par, sl=sl, pp=pp: e.matmul(
                                ps_s[pp][:], lhsT=kvh[par][:, 128 * kt:128 * (kt + 1)], rhs=qf[h][:, sl],
                                start=True, stop=True), waits=w0)
                            evw = [wmm] + ([prev_exp] if prev_exp else []) + ([prev_pv] if prev_pv else [])
                            wev = r.op("vector", lambda e, kt=kt, sl=sl, pp=pp: e.scalar_tensor_tensor(
                                out=s_sb[:], in0=m83[:, kt, sl], scalar=MASK_NEG, in1=ps_s[pp][:],
                                op0=ALU.mult, op1=ALU.add), waits=evw)
                            wexp = r.op("scalar", lambda e: e.activation(
                                out=s_sb[:], in_=s_sb[:], func=AF.Exp,
                                bias=cons[:, C_EXPB:C_EXPB + 1]), waits=[wev])
                            pvw = [wexp, wvh]
                            if kt == 0 and prev_ctxev:
                                pvw.append(prev_ctxev)
                            r.op("tensor", lambda e, kt=kt, par=par: e.matmul(
                                ps_ctx[:], lhsT=vvh[par][:, 128 * kt:128 * (kt + 1)], rhs=s_sb[:],
                                start=(kt == 0), stop=(kt == NKT - 1)), waits=pvw)
                            smw = []
                            if kt == 0 and prev_sumev:
                                smw.append(prev_sumev)
                            prev_pv = r.op("tensor", lambda e, kt=kt: e.matmul(
                                ps_sum[:], lhsT=sb["ones_bf"][:], rhs=s_sb[:],
                                start=(kt == 0), stop=(kt == NKT - 1)), waits=smw)
                            prev_exp = wexp
                        wte = ("te", r.cnt["te"])
                        prev_sumev = r.op("scalar", lambda e: e.copy(out=srow[:], in_=ps_sum[:]), waits=[wte])
                        wrc = r.op("vector", lambda e: e.reciprocal(out=srow[:], in_=srow[:]), waits=[prev_sumev])
                        wbc2 = r.op("tensor", lambda e: e.matmul(
                            ps_b[:], lhsT=sb["ones1"][:], rhs=srow[:], start=True, stop=True),
                            waits=[wrc, waux] + ([prev_bev] if prev_bev else []))
                        prev_bev = r.op("vector", lambda e: e.tensor_copy(out=binv[:], in_=ps_b[:]), waits=[wbc2])
                        prev_ctxev = r.op("vector", lambda e, h=h, sl=sl: e.tensor_tensor(
                            out=mg[h][:, sl], in0=ps_ctx[:], in1=binv[:], op=ALU.mult),
                            waits=[prev_bev, wte])
                    rdk[par] = ("te", r.cnt["te"])
                eqs.close()
                x_tiles = [ecx(nc.sbuf_tensor(f"ex{c}", [128, TPC], F32)) for c in range(DC)]
                wx = [r.dma("sync", x_tiles[c][:], x2_loc[128 * c:128 * (c + 1), :],
                            waits=[x2st, prev_ctxev] if c == 0 else []) for c in range(DC)]
                rdw = [prev_ctxev, prev_ctxev]
                prev_radd = None
                for dci in range(DC):
                    par = dci % 2
                    wd = r.dma("sync", kvh[par][:, 0:D], ag_out["aof"][128 * dci:128 * (dci + 1), :],
                               waits=[w for w in (rdw[par], ag_waits["aof"]) if w],
                               sem="t2" if par == 0 else "t3")
                    for hf in range(2):
                        sl = slice(512 * hf, 512 * (hf + 1))
                        w0 = [wd, prev_ctxev, wbo]
                        if prev_radd:
                            w0.append(prev_radd)
                        for k in range(DC):
                            r.op("tensor", lambda e, k=k, par=par, sl=sl: e.matmul(
                                ps_b[:], lhsT=kvh[par][:, 0:D][:, 128 * k:128 * (k + 1)], rhs=mg[k][:, sl],
                                start=(k == 0), stop=(k == DC - 1)), waits=w0 if k == 0 else [])
                        rdw[par] = ("te", r.cnt["te"])
                        prev_radd = r.op("vector", lambda e, dci=dci, sl=sl: e.scalar_tensor_tensor(
                            out=x_tiles[dci][:, sl], in0=ps_b[:], scalar=bo_sb[:, dci:dci + 1],
                            in1=x_tiles[dci][:, sl], op0=ALU.add, op1=ALU.add),
                            waits=[rdw[par], wx[dci]])
                x3st = None
                for c in range(DC):
                    x3st = r.dma("sync", x3_loc[128 * c:128 * (c + 1), :], x_tiles[c][:],
                                 waits=[prev_radd] if c == 0 else [])
            return x3st
        if upto >= "E":
            x3st = phase_E()

        def phase_F():
            r.wait("sync", r.fence())
            with contextlib.ExitStack() as fs:
                fcx = fs.enter_context
                x_tiles = [fcx(nc.sbuf_tensor(f"fx{c}", [128, TPC], F32)) for c in range(DC)]
                xl_tiles = [fcx(nc.sbuf_tensor(f"fxl{c}", [128, TPC], BF16)) for c in range(DC)]
                hid = [fcx(nc.sbuf_tensor(f"fhid{f}", [128, 512], BF16)) for f in range(NF1)]
                w1s = [fcx(nc.sbuf_tensor(f"fw1s{p}", [128, D], BF16)) for p in range(2)]
                w2s = fcx(nc.sbuf_tensor("fw2s", [128, H1], BF16))
                b1_sb = fcx(nc.sbuf_tensor("fb1S", [128, NF1], F32))
                b2_sb = fcx(nc.sbuf_tensor("fb2S", [128, DC], F32))
                px1 = ps["p2"]

                wx = [r.dma("sync", x_tiles[c][:], x3_loc[128 * c:128 * (c + 1), :], waits=[x3st]) for c in range(DC)]
                wb1 = r.dma("sync", b1_sb[:], b1_d)
                wb2 = r.dma("sync", b2_sb[:], b2_d)
                wln = emit_ln(r, sb, ps, x_tiles, xl_tiles, TPC, deps=[wx[DC - 1], wb1, wb2])

                rd1 = [None, None]
                rd2 = None
                prev_gel = None
                prev_out_te = None
                prev_radd = None
                for h in range(2):
                    sl = slice(512 * h, 512 * (h + 1))
                    for f in range(NF1):
                        par = f % 2
                        wd1 = r.dma("sync", w1s[par][:], ag_out["f1f"][128 * f:128 * (f + 1), :],
                                    waits=[w for w in (rd1[par], ag_waits["f1f"]) if w],
                                    sem="t0" if par == 0 else "t1")
                        w0 = [wd1, wln] + ([prev_gel] if prev_gel else [])
                        for c in range(DC):
                            r.op("tensor", lambda e, c=c, par=par, sl=sl: e.matmul(
                                px1[:], lhsT=w1s[par][:, 128 * c:128 * (c + 1)], rhs=xl_tiles[c][:, sl],
                                start=(c == 0), stop=(c == DC - 1)), waits=w0 if c == 0 else [])
                        rd1[par] = ("te", r.cnt["te"])
                        gw = [rd1[par]]
                        if h == 1 and prev_out_te:
                            gw.append(prev_out_te)
                        prev_gel = r.op("scalar", lambda e, f=f: e.activation(
                            out=hid[f][:], in_=px1[:], func=AF.Gelu, bias=b1_sb[:, f:f + 1]), waits=gw)
                    whid = prev_gel
                    for dci in range(DC):
                        wd2 = r.dma("sync", w2s[:], ag_out["f2f"][128 * dci:128 * (dci + 1), :],
                                    waits=[w for w in (rd2, ag_waits["f2f"]) if w], sem="t2")
                        w0 = [wd2, whid, prev_gel] + ([prev_radd] if prev_radd else [])
                        for k in range(NF1):
                            r.op("tensor", lambda e, k=k: e.matmul(
                                px1[:], lhsT=w2s[:, 128 * k:128 * (k + 1)], rhs=hid[k][:],
                                start=(k == 0), stop=(k == NF1 - 1)), waits=w0 if k == 0 else [])
                        rd2 = ("te", r.cnt["te"])
                        prev_out_te = rd2
                        prev_radd = r.op("vector", lambda e, dci=dci, sl=sl: e.scalar_tensor_tensor(
                            out=x_tiles[dci][:, sl], in0=px1[:], scalar=b2_sb[:, dci:dci + 1],
                            in1=x_tiles[dci][:, sl], op0=ALU.add, op1=ALU.add),
                            waits=[rd2])
                        prev_gel = prev_radd
                for c in range(DC):
                    r.dma("sync", o_out[128 * c:128 * (c + 1), :], x_tiles[c][:],
                          waits=[prev_radd] if c == 0 else [])
                r.wait("sync", [("ld", r.cnt["ld"])])
            return None
        if upto >= "F":
            phase_F()

        if upto < "F":
            r.wait("sync", r.fence())
        with nc.Block() as block:
            r.replay(block, sems)
    return nc


def host_prep_fused(inputs):
    bf = BF
    aux = _aux()
    wbcT = np.ascontiguousarray(inputs["L0_attn_W_bc"].T).astype(np.float32)
    wdtT = np.ascontiguousarray(inputs["L0_attn_W_dt"].T).astype(np.float32)
    bdt = np.ascontiguousarray(inputs["L0_attn_b_dt"].reshape(DC, 128).T).astype(np.float32)
    alog = np.asarray(inputs["L0_attn_A_log"], dtype=np.float32)
    dpar = np.asarray(inputs["L0_attn_D_param"], dtype=np.float32)
    flats = {
        "w1f": slabify(np.ascontiguousarray(inputs["L0_ffn_w1"].T).astype(np.float32), NF0).reshape(H0, D),
        "w2f": slabify(np.ascontiguousarray(inputs["L0_ffn_w2"].T).astype(np.float32), NF0).reshape(H0, D),
        "w3f": slabify(np.ascontiguousarray(inputs["L0_ffn_w3"].T).astype(np.float32), DC).reshape(D, H0),
        "f1f": slabify(np.ascontiguousarray(inputs["L1_ffn_w1"].T).astype(np.float32), NF1).reshape(H1, D),
        "f2f": slabify(np.ascontiguousarray(inputs["L1_ffn_w2"].T).astype(np.float32), DC).reshape(D, H1),
        "aqf": slabify(np.ascontiguousarray(inputs["L1_attn_Wq"].T).astype(np.float32), DC).reshape(D, D),
        "akf": slabify(np.ascontiguousarray(inputs["L1_attn_Wk"].T).astype(np.float32), DC).reshape(D, D),
        "avf": np.ascontiguousarray(inputs["L1_attn_Wv"].T).astype(np.float32),
        "aof": slabify(np.ascontiguousarray(inputs["L1_attn_Wo"].T).astype(np.float32), DC).reshape(D, D),
    }
    invcnt = np.zeros((1, 8), dtype=np.float32)
    for i, c in enumerate(QCOUNT):
        invcnt[0, i] = 1.0 / c
    sel = np.zeros((1, 8), dtype=np.float32)
    sel[0, 0:3] = 1.0
    bq = np.ascontiguousarray(inputs["L1_attn_bq"].reshape(DC, 128).T).astype(np.float32)
    bk = np.ascontiguousarray(inputs["L1_attn_bk"].reshape(DC, 128).T).astype(np.float32)
    bvrow = np.ascontiguousarray(inputs["L1_attn_bv"].reshape(1, D)).astype(np.float32)
    bo = np.ascontiguousarray(inputs["L1_attn_bo"].reshape(DC, 128).T).astype(np.float32)
    b1 = np.ascontiguousarray(inputs["L1_ffn_b1"].reshape(NF1, 128).T).astype(np.float32)
    b2 = np.ascontiguousarray(inputs["L1_ffn_b2"].reshape(DC, 128).T).astype(np.float32)
    inv_freq = 1.0 / (10000.0 ** (np.arange(0, DH, 2, dtype=np.float64) / DH))
    x = np.asarray(inputs["L0_attn_x"], dtype=np.float32)
    xt = x.reshape(B_SZ * SEQ, D)
    maps = []
    for c in range(NCORE):
        t0 = (TPC * c) % SEQ
        t = np.arange(t0, t0 + TPC, dtype=np.float64)
        fr = t[:, None] * inv_freq[None, :]
        cosT = np.cos(fr).T
        sinT = np.sin(fr).T
        sc = 1.0 / np.sqrt(DH)
        kpos = np.arange(SEQ).reshape(NKT, 128, 1)
        qpos = (t0 + np.arange(TPC)).reshape(1, 1, TPC)
        mask8 = (kpos > qpos).astype(np.int8)
        j = c % 4
        rows = slice(512 * j, 512 * (j + 1))
        al = np.ascontiguousarray(alog[rows].reshape(4, 128, 32).transpose(1, 0, 2).reshape(128, 128))
        dp = np.ascontiguousarray(dpar[rows.start:rows.stop].reshape(4, 128).T)
        m = dict(
            xT=np.ascontiguousarray(xt[TPC * c:TPC * (c + 1), :].T),
            wbcT=wbcT, wdtT=wdtT, bdt=bdt, alog=al, dpar=dp,
            invcnt=invcnt, sel=sel, bq=bq, bk=bk, bvrow=bvrow, bo=bo, b1=b1, b2=b2,
            cosq=(np.concatenate([cosT, cosT], 0) * sc).astype(bf),
            sinq=(np.concatenate([-sinT, sinT], 0) * sc).astype(bf),
            cosk=np.concatenate([cosT, cosT], 0).astype(bf),
            sink=np.concatenate([-sinT, sinT], 0).astype(bf),
            mask8=mask8, ones_bf=aux["ones_bf"], ones1=aux["ones1"], consts=aux["consts"],
        )
        for nm, arr in flats.items():
            rr = arr.shape[0] // NCORE
            m[nm] = np.ascontiguousarray(arr[rr * c:rr * (c + 1)])
        maps.append(m)
    return maps


def kernel_fused(**inputs):
    inputs = {k: np.asarray(v) for k, v in inputs.items()}
    maps = host_prep_fused(inputs)
    res = _run("FUSED", build_fused, maps)
    out = np.empty((B_SZ * SEQ, D), dtype=np.float32)
    for c in range(NCORE):
        out[TPC * c:TPC * (c + 1), :] = res[c]["o_out"].T
    return out.reshape(B_SZ, SEQ, D)


_CACHE = {}


def _run(name, builder, in_maps):
    if name not in _CACHE:
        _CACHE[name] = builder()
    res = run_bass_kernel_spmd(_CACHE[name], in_maps, core_ids=list(range(NCORE)))
    return res.results


def _aux():
    return dict(
        ones_bf=np.ones((128, 1), dtype=BF),
        ones1=np.ones((1, 128), dtype=np.float32),
        consts=np.tile(np.asarray([CONST_VALS], dtype=np.float32), (128, 1)),
    )


def host_prep_A(inputs, xs_T):
    wbcT = np.ascontiguousarray(inputs["L0_attn_W_bc"].T).astype(np.float32)
    wdtT = np.ascontiguousarray(inputs["L0_attn_W_dt"].T).astype(np.float32)
    bdt = np.ascontiguousarray(inputs["L0_attn_b_dt"].reshape(DC, 128).T).astype(np.float32)
    aux = _aux()
    return [dict(xT=xs_T[c], wbcT=wbcT, wdtT=wdtT, bdt=bdt, ones_bf=aux["ones_bf"],
                 ones1=aux["ones1"], consts=aux["consts"]) for c in range(NCORE)]


def host_prep_D(x2s, qj, inputs):
    s = dict(qaq=slabify(qj["aq"], DC), qak=slabify(qj["ak"], DC), qav=slabify(qj["av"], DC))
    bq = np.ascontiguousarray(inputs["L1_attn_bq"].reshape(DC, 128).T).astype(np.float32)
    bk = np.ascontiguousarray(inputs["L1_attn_bk"].reshape(DC, 128).T).astype(np.float32)
    bv = np.ascontiguousarray(inputs["L1_attn_bv"].reshape(DC, 128).T).astype(np.float32)
    inv_freq = 1.0 / (10000.0 ** (np.arange(0, DH, 2, dtype=np.float64) / DH))
    aux = _aux()
    maps = []
    for c in range(NCORE):
        t0 = (TPC * c) % SEQ
        t = np.arange(t0, t0 + TPC, dtype=np.float64)
        fr = t[:, None] * inv_freq[None, :]
        cosT = np.cos(fr).T
        sinT = np.sin(fr).T
        sc = 1.0 / np.sqrt(DH)
        cosq = np.concatenate([cosT, cosT], 0) * sc
        sinq = np.concatenate([-sinT, sinT], 0) * sc
        cosk = np.concatenate([cosT, cosT], 0)
        sink = np.concatenate([-sinT, sinT], 0)
        maps.append(dict(
            xT=x2s[c], qaq=s["qaq"], qak=s["qak"], qav=s["qav"],
            bq=bq, bk=bk, bv=bv,
            cosq=cosq.astype(BF), sinq=sinq.astype(BF),
            cosk=cosk.astype(BF), sink=sink.astype(BF),
            ones_bf=aux["ones_bf"], ones1=aux["ones1"], consts=aux["consts"]))
    return maps


def host_prep_E(x2s, resD, qj, inputs):
    qao = slabify(qj["ao"], DC)
    bo = np.ascontiguousarray(inputs["L1_attn_bo"].reshape(DC, 128).T).astype(np.float32)
    aux = _aux()
    kf_sl = {}
    vt_sl = {}
    for b in range(B_SZ):
        kfull = np.concatenate([resD[4 * b + j]["o_k"] for j in range(4)], axis=1)
        kf_sl[b] = np.ascontiguousarray(kfull.reshape(NH, 128, SEQ))
        vfull = np.concatenate([resD[4 * b + j]["o_v"] for j in range(4)], axis=1)
        vtm = np.ascontiguousarray(vfull.T)
        vt_sl[b] = np.ascontiguousarray(
            vtm.reshape(NKT, 128, NH, 128).transpose(2, 1, 0, 3).reshape(NH, 128, SEQ))
    maps = []
    for c in range(NCORE):
        b = c // 4
        t0 = (TPC * c) % SEQ
        kpos = np.arange(SEQ).reshape(NKT, 128, 1)
        qpos = (t0 + np.arange(TPC)).reshape(1, 1, TPC)
        mask8 = (kpos > qpos).astype(np.int8)
        maps.append(dict(
            xT=x2s[c], qf=resD[c]["o_q"], kf=kf_sl[b], vt=vt_sl[b], mask8=mask8,
            qao=qao, bo=bo, ones_bf=aux["ones_bf"], ones1=aux["ones1"], consts=aux["consts"]))
    return maps


def host_prep_F(x3s, qj, inputs):
    s1 = slabify(qj["f1s"], NF1)
    s2 = slabify(qj["f2s"], DC)
    b1 = np.ascontiguousarray(inputs["L1_ffn_b1"].reshape(NF1, 128).T).astype(np.float32)
    b2 = np.ascontiguousarray(inputs["L1_ffn_b2"].reshape(DC, 128).T).astype(np.float32)
    aux = _aux()
    return [dict(xT=x3s[c], qf1=s1, qf2=s2, b1=b1, b2=b2, ones_bf=aux["ones_bf"],
                 ones1=aux["ones1"], consts=aux["consts"]) for c in range(NCORE)]


def kernel(**inputs):
    return kernel_fused(**inputs)


# version 05 (final)
# speedup vs baseline: 1.1736x; 1.1736x over previous
import contextlib
import numpy as np
import ml_dtypes
import concourse.bass as bass
import concourse.mybir as mybir
from concourse.bass_utils import run_bass_kernel_spmd
from concourse.expressions import RuntimeValue

F32 = mybir.dt.float32
BF16 = mybir.dt.bfloat16
AF = mybir.ActivationFunctionType
ALU = mybir.AluOpType
BF = ml_dtypes.bfloat16

NCORE = 8
B_SZ = 2
SEQ = 4096
D = 2048
DC = D // 128
TPC = 1024
H0 = 5632
H1 = 8192
NH = 16
DH = 128
DT_RANK = 64
D_STATE = 32
LN_EPS = 1e-5
EXP_BIAS = -20.0
MASK_NEG = -30000.0

C_LNEPS, C_ONE, C_NEG1, C_NEGH, C_EXPB, C_HALF, C_EPS8, C_ZERO = range(8)
CONST_VALS = [LN_EPS, 1.0, -1.0, -0.5, EXP_BIAS, 0.5, 1e-8, 0.0]

ENGS = ("sync", "gpsimd", "scalar", "vector", "tensor")


class Rec:

    def __init__(self):
        self.ops = {k: [] for k in ENGS}
        self.cnt = {"ld": 0, "lde": 0, "ldo": 0, "g": 0, "sc": 0, "ve": 0, "te": 0, "cc": 0,
                    "t0": 0, "t1": 0, "t2": 0, "t3": 0, "t4": 0, "t5": 0}

    DMA_SEMS = ("ld", "lde", "ldo", "g", "cc", "t0", "t1", "t2", "t3", "t4", "t5")

    def _emit(self, eng, waits, fn, inc_sem=None, inc_by=0):
        waits = tuple(
            (s, self.cnt[s] - (inc_by if s == inc_sem else 0))
            if s in self.DMA_SEMS else (s, v) for (s, v) in waits
        )

        def op(e, sems):
            for (s, v) in waits:
                e.wait_ge(sems[s], v)
            ins = fn(e)
            if inc_sem is not None and ins is not None:
                ins.then_inc(sems[inc_sem], inc_by)

        self.ops[eng].append(op)

    def dma(self, q, out, in_, waits=(), sem=None):
        if sem is None:
            sem = {"sync": "ld", "gpsimd": "g"}[q]
        self.cnt[sem] += 16
        self._emit(q, waits, lambda e: e.dma_start(out=out, in_=in_), sem, 16)
        return (sem, self.cnt[sem])

    def op(self, eng, fn, waits=()):
        sem = {"scalar": "sc", "vector": "ve", "tensor": "te"}[eng]
        self.cnt[sem] += 1
        self._emit(eng, waits, fn, sem, 1)
        return (sem, self.cnt[sem])

    def wait(self, eng, waits):
        self._emit(eng, waits, lambda e: None)

    def fence(self):
        return [(s, self.cnt[s]) for s in self.cnt]

    def collective(self, kind, op, groups, ins, outs, waits=()):
        waits = list(waits) + ([("cc", self.cnt["cc"])] if self.cnt["cc"] else [])
        self.cnt["cc"] += 1
        self._emit("gpsimd", waits, lambda e: e.collective_compute(
            kind, op, replica_groups=groups, ins=ins, outs=outs), "cc", 1)
        return ("cc", self.cnt["cc"])

    def replay(self, block, sems):
        for eng in ENGS:
            if not self.ops[eng]:
                continue
            engops = list(self.ops[eng])

            def body(e, engops=engops):
                for f in engops:
                    f(e, sems)

            getattr(block, eng)(body)


def _mk_sems(nc, stack):
    return {n: stack.enter_context(nc.semaphore(n))
            for n in ("ld", "lde", "ldo", "g", "sc", "ve", "te", "cc",
                      "t0", "t1", "t2", "t3", "t4", "t5")}



def emit_ln(r, sb, ps, x_tiles, xl_tiles, n_tok, deps=()):
    stats = sb["stats"]
    sums = stats[:, 0:n_tok]
    sumsq = stats[:, n_tok:2 * n_tok]
    scr = stats[:, 2 * n_tok:3 * n_tok]
    cons = sb["consts"]
    nhalf = n_tok // 512

    casts = []
    for c in range(DC):
        casts.append(r.op("vector", lambda e, c=c: e.tensor_copy(out=xl_tiles[c][:], in_=x_tiles[c][:]), waits=deps))

    prev_copy_s = None
    prev_copy_q = None
    for h in range(nhalf):
        sl = slice(512 * h, 512 * (h + 1))
        prev_mmq = None
        for c in range(DC):
            sqw = [casts[c]] + ([prev_mmq] if prev_mmq else [])
            wsq = r.op("scalar", lambda e, c=c, sl=sl: e.activation(
                out=sb["sqt"][:], in_=xl_tiles[c][:, sl], func=AF.Square), waits=sqw)
            smw = [casts[c]]
            if c == 0 and prev_copy_s is not None:
                smw.append(prev_copy_s)
            r.op("tensor", lambda e, c=c, sl=sl: e.matmul(
                ps["sts"][:], lhsT=sb["ones_bf"][:], rhs=xl_tiles[c][:, sl],
                start=(c == 0), stop=(c == DC - 1)), waits=smw)
            qmw = [wsq]
            if c == 0 and prev_copy_q is not None:
                qmw.append(prev_copy_q)
            prev_mmq = r.op("tensor", lambda e, c=c: e.matmul(
                ps["stq"][:], lhsT=sb["ones_bf"][:], rhs=sb["sqt"][:],
                start=(c == 0), stop=(c == DC - 1)), waits=qmw)
        wt = ("te", r.cnt["te"])
        prev_copy_s = r.op("scalar", lambda e, sl=sl: e.copy(out=sums[:, sl], in_=ps["sts"][:]), waits=[wt])
        prev_copy_q = r.op("scalar", lambda e, sl=sl: e.copy(out=sumsq[:, sl], in_=ps["stq"][:]), waits=[wt])

    w1 = r.op("vector", lambda e: e.tensor_scalar_mul(out=sums, in0=sums, scalar1=1.0 / D), waits=[prev_copy_s, prev_copy_q])
    w2 = r.op("scalar", lambda e: e.activation(out=scr, in_=sums, func=AF.Square), waits=[w1])
    w3 = r.op("vector", lambda e: e.tensor_scalar_mul(out=sumsq, in0=sumsq, scalar1=1.0 / D), waits=[w1])
    w4 = r.op("vector", lambda e: e.tensor_sub(out=sumsq, in0=sumsq, in1=scr), waits=[w2, w3])
    w5 = r.op("scalar", lambda e: e.activation(out=sumsq, in_=sumsq, func=AF.Ln, bias=cons[0:1, C_LNEPS:C_LNEPS + 1]), waits=[w4])
    wrstd = r.op("scalar", lambda e: e.activation(out=sumsq, in_=sumsq, func=AF.Exp, scale=cons[0:1, C_NEGH:C_NEGH + 1]), waits=[w5])

    guard = wrstd
    wlast = None
    for h in range(nhalf):
        sl = slice(512 * h, 512 * (h + 1))
        wm = r.op("tensor", lambda e, sl=sl: e.matmul(ps["mb"][:], lhsT=sb["ones1"][:], rhs=sums[:, sl], start=True, stop=True), waits=[guard])
        wr = r.op("tensor", lambda e, sl=sl: e.matmul(ps["rb"][:], lhsT=sb["ones1"][:], rhs=sumsq[:, sl], start=True, stop=True), waits=[guard])
        for c in range(DC):
            waw = ([wm, wr] if c == 0 else []) + ([wlast] if wlast else [])
            wa = r.op("vector", lambda e, c=c, sl=sl: e.tensor_sub(out=sb["tmp"][:], in0=x_tiles[c][:, sl], in1=ps["mb"][:]), waits=waw)
            wlast = r.op("vector", lambda e, c=c, sl=sl: e.tensor_mul(out=xl_tiles[c][:, sl], in0=sb["tmp"][:], in1=ps["rb"][:]), waits=[wa])
        guard = wlast
    return wlast



def build_A():
    nc = bass.Bass("TRN2", target_bir_lowering=False, debug=False)
    xT = nc.dram_tensor("xT", [D, TPC], F32, kind="ExternalInput").ap()
    wbcT = nc.dram_tensor("wbcT", [D, 128], F32, kind="ExternalInput").ap()
    wdtT = nc.dram_tensor("wdtT", [DT_RANK, D], F32, kind="ExternalInput").ap()
    bdt = nc.dram_tensor("bdt", [128, DC], F32, kind="ExternalInput").ap()
    ones_bf_d = nc.dram_tensor("ones_bf", [128, 1], BF16, kind="ExternalInput").ap()
    ones1_d = nc.dram_tensor("ones1", [1, 128], F32, kind="ExternalInput").ap()
    consts_d = nc.dram_tensor("consts", [128, 8], F32, kind="ExternalInput").ap()

    o_delta = nc.dram_tensor("o_delta", [D, TPC], BF16, kind="ExternalOutput").ap()
    o_u = nc.dram_tensor("o_u", [D, TPC], BF16, kind="ExternalOutput").ap()
    o_bc = nc.dram_tensor("o_bc", [64, TPC], BF16, kind="ExternalOutput").ap()

    r = Rec()
    with contextlib.ExitStack() as stack:
        ec = stack.enter_context
        x_tiles = [ec(nc.sbuf_tensor(f"x{c}", [128, TPC], F32)) for c in range(DC)]
        xl_tiles = [ec(nc.sbuf_tensor(f"xl{c}", [128, TPC], BF16)) for c in range(DC)]
        sb = {
            "stats": ec(nc.sbuf_tensor("stats", [1, 3 * TPC], F32)),
            "sqt": ec(nc.sbuf_tensor("sqt", [128, 512], BF16)),
            "tmp": ec(nc.sbuf_tensor("tmp", [128, 512], F32)),
            "ones_bf": ec(nc.sbuf_tensor("onesbf", [128, 1], BF16)),
            "ones1": ec(nc.sbuf_tensor("ones1s", [1, 128], F32)),
            "consts": ec(nc.sbuf_tensor("constsS", [128, 8], F32)),
        }
        wbc_sb = ec(nc.sbuf_tensor("wbc", [128, DC * 128], BF16))
        wdt_sb = ec(nc.sbuf_tensor("wdt", [DT_RANK, D], BF16))
        bdt_sb = ec(nc.sbuf_tensor("bdtS", [128, DC], F32))
        dtr_sb = ec(nc.sbuf_tensor("dtr", [DT_RANK, TPC], BF16))
        bc_sb = ec(nc.sbuf_tensor("bcS", [64, TPC], BF16))
        delta_sb = [ec(nc.sbuf_tensor(f"dl{c}", [128, TPC], BF16)) for c in range(DC)]
        t_abs = ec(nc.sbuf_tensor("tabs", [128, 512], F32))
        t_rel = ec(nc.sbuf_tensor("trel", [128, 512], F32))

        ps = {
            "sts": ec(nc.psum_tensor("psts", [1, 512], F32)),
            "stq": ec(nc.psum_tensor("pstq", [1, 512], F32)),
            "mb": ec(nc.psum_tensor("pmb", [128, 512], F32)),
            "rb": ec(nc.psum_tensor("prb", [128, 512], F32)),
        }
        ps_db = [ec(nc.psum_tensor(f"pdb{h}", [128, 512], F32)) for h in range(2)]
        ps_dt = [ec(nc.psum_tensor(f"pdt{h}", [128, 512], F32)) for h in range(2)]

        sems = _mk_sems(nc, stack)

        wx = [r.dma("sync", x_tiles[c][:], xT[128 * c:128 * (c + 1), :]) for c in range(DC)]
        ww1 = r.dma("gpsimd", wbc_sb[:].rearrange("p (c o) -> p c o", o=128),
                    wbcT.rearrange("(c p) o -> p c o", p=128))
        ww2 = r.dma("gpsimd", wdt_sb[:], wdtT)
        wb = r.dma("sync", bdt_sb[:], bdt)
        wo1 = r.dma("sync", sb["ones_bf"][:], ones_bf_d)
        wo2 = r.dma("sync", sb["ones1"][:], ones1_d)
        wcn = r.dma("sync", sb["consts"][:], consts_d)

        wln = emit_ln(r, sb, ps, x_tiles, xl_tiles, TPC,
                      deps=[wx[DC - 1], wo1, wo2, wcn])

        ev_dtr = [None, None]
        ev_bc = [None, None]
        for h in range(2):
            sl = slice(512 * h, 512 * (h + 1))
            for c in range(DC):
                r.op("tensor", lambda e, c=c, h=h, sl=sl: e.matmul(
                    ps_db[h][:], lhsT=wbc_sb[:, 128 * c:128 * (c + 1)], rhs=xl_tiles[c][:, sl],
                    start=(c == 0), stop=(c == DC - 1)),
                    waits=[wln, ww1] if (h == 0 and c == 0) else [])
            wt = ("te", r.cnt["te"])
            ev_dtr[h] = r.op("scalar", lambda e, h=h, sl=sl: e.copy(out=dtr_sb[:, sl], in_=ps_db[h][0:64, :]), waits=[wt])
            ev_bc[h] = r.op("scalar", lambda e, h=h, sl=sl: e.copy(out=bc_sb[:, sl], in_=ps_db[h][64:128, :]), waits=[wt])

        prev_psum_reader = [None, None]
        prev_vadd = None
        for c in range(DC):
            for h in range(2):
                sl = slice(512 * h, 512 * (h + 1))
                w0 = [ev_dtr[h], ww2, wb]
                if prev_psum_reader[h] is not None:
                    w0.append(prev_psum_reader[h])
                wmm = r.op("tensor", lambda e, c=c, h=h, sl=sl: e.matmul(
                    ps_dt[h][:], lhsT=wdt_sb[:, 128 * c:128 * (c + 1)], rhs=dtr_sb[:, sl],
                    start=True, stop=True), waits=w0)
                wa_w = [wmm] + ([prev_vadd] if prev_vadd is not None else [])
                wa = r.op("scalar", lambda e, c=c, h=h: e.activation(
                    out=t_abs[:], in_=ps_dt[h][:], func=AF.Abs, bias=bdt_sb[:, c:c + 1]), waits=wa_w)
                wr2 = r.op("scalar", lambda e, c=c, h=h: e.activation(
                    out=t_rel[:], in_=ps_dt[h][:], func=AF.Relu, bias=bdt_sb[:, c:c + 1]), waits=[])
                prev_psum_reader[h] = wr2
                we = r.op("scalar", lambda e: e.activation(out=t_abs[:], in_=t_abs[:], func=AF.Exp, scale=sb["consts"][:, C_NEG1:C_NEG1 + 1]), waits=[])
                wl = r.op("scalar", lambda e: e.activation(out=t_abs[:], in_=t_abs[:], func=AF.Ln, bias=sb["consts"][:, C_ONE:C_ONE + 1]), waits=[])
                prev_vadd = r.op("vector", lambda e, c=c, sl=sl: e.tensor_add(
                    out=delta_sb[c][:, sl], in0=t_rel[:], in1=t_abs[:]), waits=[wl])

        wve = ("ve", r.cnt["ve"])
        wsc = ("sc", r.cnt["sc"])
        for c in range(DC):
            r.dma("sync", o_delta[128 * c:128 * (c + 1), :], delta_sb[c][:], waits=[wve] if c == 0 else [])
            r.dma("sync", o_u[128 * c:128 * (c + 1), :], xl_tiles[c][:], waits=[])
        r.dma("sync", o_bc, bc_sb[:], waits=[wsc])
        final = r.cnt["ld"]
        r.wait("sync", [("ld", final)])

        with nc.Block() as block:
            r.replay(block, sems)
    return nc



TC = 512
NCH = SEQ // TC
NDT = 4
TC1 = TC + 2


def build_B():
    nc = bass.Bass("TRN2", target_bir_lowering=False, debug=False)
    delta_d = nc.dram_tensor("delta", [512, SEQ], BF16, kind="ExternalInput").ap()
    u_d = nc.dram_tensor("u", [512, SEQ], BF16, kind="ExternalInput").ap()
    bc_d = nc.dram_tensor("bc", [64, SEQ], BF16, kind="ExternalInput").ap()
    alog_d = nc.dram_tensor("alog", [128, 128], F32, kind="ExternalInput").ap()
    dpar_d = nc.dram_tensor("dpar", [128, NDT], F32, kind="ExternalInput").ap()
    o_y = nc.dram_tensor("o_y", [512, SEQ], BF16, kind="ExternalOutput").ap()

    r = Rec()
    with contextlib.ExitStack() as stack:
        ec = stack.enter_context
        dA = ec(nc.sbuf_tensor("dA", [128, D_STATE * TC1], BF16))
        dBu = ec(nc.sbuf_tensor("dBu", [128, D_STATE * TC1], BF16))
        hh = ec(nc.sbuf_tensor("hh", [128, D_STATE * TC1], BF16))
        brep = ec(nc.sbuf_tensor("brep", [128, D_STATE * TC], BF16))
        crep = ec(nc.sbuf_tensor("crep", [128, D_STATE * TC], BF16))
        dl_sb = ec(nc.sbuf_tensor("dl", [128, TC], BF16))
        u_sb = ec(nc.sbuf_tensor("uS", [128, TC], BF16))
        du_sb = ec(nc.sbuf_tensor("du", [128, TC], BF16))
        y_sb = ec(nc.sbuf_tensor("yS", [128, TC], BF16))
        carry = ec(nc.sbuf_tensor("carry", [128, D_STATE * NDT], BF16))
        alog_sb = ec(nc.sbuf_tensor("alogS", [128, 128], F32))
        a_sb = ec(nc.sbuf_tensor("aS", [128, 128], F32))
        dpar_sb = ec(nc.sbuf_tensor("dparS", [128, NDT], F32))

        sems = _mk_sems(nc, stack)

        dA3 = dA[:].rearrange("p (n t) -> p n t", t=TC1)
        dBu3 = dBu[:].rearrange("p (n t) -> p n t", t=TC1)
        hh3 = hh[:].rearrange("p (n t) -> p n t", t=TC1)
        brep3 = brep[:].rearrange("p (n t) -> p n t", t=TC)
        crep3 = crep[:].rearrange("p (n t) -> p n t", t=TC)

        wal = r.dma("sync", alog_sb[:], alog_d)
        wdp = r.dma("sync", dpar_sb[:], dpar_d)
        wexp = r.op("scalar", lambda e: e.activation(out=a_sb[:], in_=alog_sb[:], func=AF.Exp), waits=[wal])
        wA = r.op("vector", lambda e: e.tensor_scalar_mul(out=a_sb[:], in0=a_sb[:], scalar1=-1.0), waits=[wexp])

        prev_ve = wA
        prev_sc = wexp
        prev_ystore = None
        for ch in range(NCH):
            tsl = slice(TC * ch, TC * (ch + 1))
            wbr = r.dma("sync", brep3, bc_d[0:32, tsl].unsqueeze(0).broadcast_to([128, 32, TC]), waits=[prev_ve])
            wcr = r.dma("sync", crep3, bc_d[32:64, tsl].unsqueeze(0).broadcast_to([128, 32, TC]), waits=[])
            for dt in range(NDT):
                psl = slice(128 * dt, 128 * (dt + 1))
                wld = r.dma("sync", dl_sb[:], delta_d[psl, tsl], waits=[prev_sc, prev_ve])
                wlu = r.dma("sync", u_sb[:], u_d[psl, tsl], waits=[])
                wexps = None
                for n in range(D_STATE):
                    wexps = r.op("scalar", lambda e, n=n, dt=dt: e.activation(
                        out=dA[:, n * TC1 + 1: (n + 1) * TC1], in_=dl_sb[:],
                        func=AF.Exp, scale=a_sb[:, dt * 32 + n: dt * 32 + n + 1]),
                        waits=[wld, prev_ve, wA] if n == 0 else [])
                wms = r.op("vector", lambda e: e.memset(dA3[:, :, 0:1], 0.0), waits=[prev_ve] if ch + dt else [])
                wdu = r.op("vector", lambda e: e.tensor_mul(out=du_sb[:], in0=dl_sb[:], in1=u_sb[:]), waits=[wld, wlu])
                wdb = r.op("vector", lambda e: e.tensor_tensor(
                    out=dBu3[:, :, 1:TC + 1], in0=du_sb[:].unsqueeze(1).broadcast_to([128, 32, TC]),
                    in1=brep3, op=ALU.mult), waits=[wbr, wdu])
                if ch == 0:
                    wcc = r.op("vector", lambda e, dt=dt: e.memset(dBu3[:, :, 0:1], 0.0), waits=[])
                else:
                    wcc = r.op("vector", lambda e, dt=dt: e.tensor_copy(
                        out=dBu3[:, :, 0:1], in_=carry[:, dt * 32:(dt + 1) * 32].unsqueeze(2)), waits=[])
                wsc_ = r.op("vector", lambda e: e.tensor_tensor_scan(
                    out=hh[:], data0=dA[:], data1=dBu[:], initial=0.0,
                    op0=ALU.mult, op1=ALU.add), waits=[wexps, wms, wdb, wcc])
                wco = r.op("vector", lambda e, dt=dt: e.tensor_copy(
                    out=carry[:, dt * 32:(dt + 1) * 32].unsqueeze(2), in_=hh3[:, :, TC:TC + 1]), waits=[wsc_])
                whc = r.op("vector", lambda e: e.tensor_tensor(
                    out=dA3[:, :, 1:TC + 1], in0=hh3[:, :, 1:TC + 1], in1=crep3, op=ALU.mult), waits=[wcr, wsc_])
                wtr = whc
                for half in (16, 8, 4, 2, 1):
                    wtr = r.op("vector", lambda e, half=half: e.tensor_tensor(
                        out=dA3[:, 0:half, 1:TC + 1], in0=dA3[:, 0:half, 1:TC + 1],
                        in1=dA3[:, half:2 * half, 1:TC + 1], op=ALU.add), waits=[wtr])
                yw = [wdp, wtr] + ([prev_ystore] if prev_ystore else [])
                wy = r.op("vector", lambda e, dt=dt: e.scalar_tensor_tensor(
                    out=y_sb[:], in0=u_sb[:], scalar=dpar_sb[:, dt:dt + 1],
                    in1=dA[:, 1:TC + 1], op0=ALU.mult, op1=ALU.add), waits=yw)
                prev_ystore = r.dma("sync", o_y[psl, tsl], y_sb[:], waits=[wy])
                prev_ve = wy
                prev_sc = wexps
        final = r.cnt["ld"]
        r.wait("sync", [("ld", final)])

        with nc.Block() as block:
            r.replay(block, sems)
    return nc


def host_prep_B(inputs, resA):
    dseq = {b: np.concatenate([resA[4 * b + j]["o_delta"] for j in range(4)], axis=1) for b in range(B_SZ)}
    useq = {b: np.concatenate([resA[4 * b + j]["o_u"] for j in range(4)], axis=1) for b in range(B_SZ)}
    bcseq = {b: np.ascontiguousarray(np.concatenate([resA[4 * b + j]["o_bc"] for j in range(4)], axis=1)) for b in range(B_SZ)}
    alog = np.asarray(inputs["L0_attn_A_log"], dtype=np.float32)
    dpar = np.asarray(inputs["L0_attn_D_param"], dtype=np.float32)
    maps = []
    for c in range(NCORE):
        b, j = c // 4, c % 4
        rows = slice(512 * j, 512 * (j + 1))
        al = np.ascontiguousarray(
            alog[rows].reshape(4, 128, 32).transpose(1, 0, 2).reshape(128, 128))
        dp = np.ascontiguousarray(dpar[rows.start:rows.stop].reshape(4, 128).T)
        maps.append(dict(
            delta=np.ascontiguousarray(dseq[b][rows.start:rows.stop]),
            u=np.ascontiguousarray(useq[b][rows.start:rows.stop]),
            bc=bcseq[b], alog=al, dpar=dp))
    return maps



QSPEC = [
    ("w1s", 256, H0, 0),
    ("w2s", 256, H0, 0),
    ("w3s", 704, D, 0),
    ("f1s", 256, H1, 1),
    ("f2s", 1024, D, 1),
]
CSPEC = [("aq", 256, D), ("ak", 256, D), ("av", 256, D), ("ao", 256, D)]
QCOUNT = [H0 * D, H0 * D, H0 * D, H1 * D, H1 * D]


def build_Q():
    nc = bass.Bass("TRN2", target_bir_lowering=False, debug=False)
    ins = {}
    outs = {}
    for (nm, rows, cols, _) in QSPEC:
        ins[nm] = nc.dram_tensor(nm, [rows, cols], F32, kind="ExternalInput").ap()
        outs[nm] = nc.dram_tensor("q" + nm, [rows, cols], BF16, kind="ExternalOutput").ap()
    for (nm, rows, cols) in CSPEC:
        ins[nm] = nc.dram_tensor(nm, [rows, cols], F32, kind="ExternalInput").ap()
        outs[nm] = nc.dram_tensor("q" + nm, [rows, cols], BF16, kind="ExternalOutput").ap()
    ones1_d = nc.dram_tensor("ones1", [1, 128], F32, kind="ExternalInput").ap()
    invcnt_d = nc.dram_tensor("invcnt", [1, 8], F32, kind="ExternalInput").ap()
    sel_d = nc.dram_tensor("sel", [1, 8], F32, kind="ExternalInput").ap()

    g_b = nc.dram_tensor("g_bounce", [1, 8], F32)
    g_ar = nc.dram_tensor("g_ar", [1, 8], F32)

    r = Rec()
    with contextlib.ExitStack() as stack:
        ec = stack.enter_context
        wt = ec(nc.sbuf_tensor("wt", [128, H1], F32))
        qt = ec(nc.sbuf_tensor("qt", [128, H1], BF16))
        q2 = ec(nc.sbuf_tensor("q2", [128, H1], BF16))
        pacc = ec(nc.sbuf_tensor("pacc", [128, 16], F32))
        redc = ec(nc.sbuf_tensor("redc", [128, 1], F32))
        grow = ec(nc.sbuf_tensor("grow", [1, 16], F32))
        thb = ec(nc.sbuf_tensor("thb", [128, 8], F32))
        nthb = ec(nc.sbuf_tensor("nthb", [128, 8], F32))
        ones1 = ec(nc.sbuf_tensor("ones1s", [1, 128], F32))
        onescol = ec(nc.sbuf_tensor("onescol", [128, 1], F32))
        ps1 = ec(nc.psum_tensor("ps1", [1, 8], F32))
        psb = ec(nc.psum_tensor("psb", [128, 8], F32))
        sems = _mk_sems(nc, stack)

        w_on = r.dma("sync", ones1[:], ones1_d)
        wic = r.dma("sync", grow[0:1, 8:16], invcnt_d)
        wsel = r.dma("sync", thb[0:1, 0:8], sel_d)
        wmo = r.op("vector", lambda e: e.memset(pacc[:], 0.0), waits=[])
        woc = r.op("vector", lambda e: e.memset(onescol[:], 1.0), waits=[])

        def tiles_of(rows):
            out = []
            p = 0
            while p < rows:
                out.append((p, min(128, rows - p)))
                p += 128
            return out

        prev = wmo
        wmm_dep = []
        for wi, (nm, rows, cols, _) in enumerate(QSPEC):
            accs = []
            for ti, (p0, np_) in enumerate(tiles_of(rows)):
                wl = r.dma("sync", wt[0:np_, 0:cols], ins[nm][p0:p0 + np_, :], waits=[prev])
                prev = r.op("vector", lambda e, np_=np_, cols=cols, ti=ti, wi=wi: e.reduce_sum(
                    out=pacc[0:np_, ti + 1:ti + 2], in_=wt[0:np_, 0:cols],
                    axis=mybir.AxisListType.X, apply_absolute_value=True), waits=[wl])
                accs.append(ti)
            prev = r.op("vector", lambda e, n=len(accs): e.reduce_sum(
                out=redc[:], in_=pacc[:, 0:16], axis=mybir.AxisListType.X), waits=[prev])
            wq = r.op("tensor", lambda e, wi=wi: e.matmul(
                ps1[0:1, wi:wi + 1], lhsT=onescol[:], rhs=redc[:],
                start=True, stop=True), waits=[prev, woc])
            wmm_dep.append(wq)
            prev = r.op("vector", lambda e: e.memset(pacc[:], 0.0), waits=[wq])
        wg = r.op("scalar", lambda e: e.copy(out=grow[0:1, 0:8], in_=ps1[0:1, 0:8]), waits=wmm_dep)
        wgd = r.dma("gpsimd", g_b.ap(), grow[0:1, 0:8], waits=[wg])
        war = r.collective("AllReduce", ALU.add, [list(range(NCORE))],
                           [g_b.ap().opt()], [g_ar.ap().opt()], waits=[wgd])
        wgb = r.dma("sync", grow[0:1, 0:8], g_ar.ap(), waits=[war])

        wm1 = r.op("vector", lambda e: e.tensor_mul(out=grow[0:1, 0:8], in0=grow[0:1, 0:8], in1=grow[0:1, 8:16]), waits=[wgb, wic])
        wa_ = r.op("vector", lambda e: e.tensor_scalar_add(out=grow[0:1, 8:16], in0=grow[0:1, 0:8], scalar1=1e-8), waits=[wm1])
        wb_ = r.op("vector", lambda e: e.tensor_scalar_max(out=grow[0:1, 0:8], in0=grow[0:1, 0:8], scalar1=1e-8), waits=[wa_])
        wbl1 = r.op("vector", lambda e: e.tensor_sub(out=grow[0:1, 8:16], in0=grow[0:1, 8:16], in1=grow[0:1, 0:8]), waits=[wa_, wb_])
        wbl2 = r.op("vector", lambda e: e.tensor_mul(out=grow[0:1, 8:16], in0=grow[0:1, 8:16], in1=thb[0:1, 0:8]), waits=[wbl1, wsel])
        wbl3 = r.op("vector", lambda e: e.tensor_add(out=grow[0:1, 0:8], in0=grow[0:1, 0:8], in1=grow[0:1, 8:16]), waits=[wbl2])
        wth = r.op("vector", lambda e: e.tensor_scalar_mul(out=grow[0:1, 0:8], in0=grow[0:1, 0:8], scalar1=0.5), waits=[wbl3])
        wtb = r.op("tensor", lambda e: e.matmul(psb[:], lhsT=ones1[:], rhs=grow[0:1, 0:8], start=True, stop=True), waits=[wth, w_on])
        wtc = r.op("vector", lambda e: e.tensor_copy(out=thb[:], in_=psb[:]), waits=[wtb])
        wtn = r.op("vector", lambda e: e.tensor_scalar_mul(out=nthb[:], in0=thb[:], scalar1=-1.0), waits=[wtc])

        prev_q = wtn
        prev_out = None
        for wi, (nm, rows, cols, _) in enumerate(QSPEC):
            for (p0, np_) in tiles_of(rows):
                wl = r.dma("sync", wt[0:np_, 0:cols], ins[nm][p0:p0 + np_, :], waits=[prev_q])
                w1w = [wl] + ([prev_out] if prev_out else [])
                w1_ = r.op("vector", lambda e, np_=np_, cols=cols, wi=wi: e.tensor_scalar(
                    out=qt[0:np_, 0:cols], in0=wt[0:np_, 0:cols], scalar1=thb[0:np_, wi:wi + 1],
                    scalar2=None, op0=ALU.is_ge), waits=w1w)
                w2_ = r.op("vector", lambda e, np_=np_, cols=cols, wi=wi: e.tensor_scalar(
                    out=q2[0:np_, 0:cols], in0=wt[0:np_, 0:cols], scalar1=nthb[0:np_, wi:wi + 1],
                    scalar2=None, op0=ALU.is_le), waits=[])
                w3_ = r.op("vector", lambda e, np_=np_, cols=cols: e.tensor_sub(
                    out=qt[0:np_, 0:cols], in0=qt[0:np_, 0:cols], in1=q2[0:np_, 0:cols]), waits=[w1_, w2_])
                prev_out = r.dma("sync", outs[nm][p0:p0 + np_, :], qt[0:np_, 0:cols], waits=[w3_])
                prev_q = w3_
        for (nm, rows, cols) in CSPEC:
            for (p0, np_) in tiles_of(rows):
                wl = r.dma("sync", wt[0:np_, 0:cols], ins[nm][p0:p0 + np_, :], waits=[prev_q])
                cw = [wl] + ([prev_out] if prev_out else [])
                prev_q = r.op("vector", lambda e, np_=np_, cols=cols: e.tensor_copy(
                    out=qt[0:np_, 0:cols], in_=wt[0:np_, 0:cols]), waits=cw)
                prev_out = r.dma("sync", outs[nm][p0:p0 + np_, :], qt[0:np_, 0:cols], waits=[prev_q])
        final = r.cnt["ld"]
        r.wait("sync", [("ld", final)])
        with nc.Block() as block:
            r.replay(block, sems)
    return nc


def host_prep_Q(inputs):
    w1T = np.ascontiguousarray(inputs["L0_ffn_w1"].T).astype(np.float32)
    w2T = np.ascontiguousarray(inputs["L0_ffn_w2"].T).astype(np.float32)
    w3T = np.ascontiguousarray(inputs["L0_ffn_w3"].T).astype(np.float32)
    f1T = np.ascontiguousarray(inputs["L1_ffn_w1"].T).astype(np.float32)
    f2T = np.ascontiguousarray(inputs["L1_ffn_w2"].T).astype(np.float32)
    aqT = np.ascontiguousarray(inputs["L1_attn_Wq"].T).astype(np.float32)
    akT = np.ascontiguousarray(inputs["L1_attn_Wk"].T).astype(np.float32)
    avT = np.ascontiguousarray(inputs["L1_attn_Wv"].T).astype(np.float32)
    aoT = np.ascontiguousarray(inputs["L1_attn_Wo"].T).astype(np.float32)
    full = dict(w1s=w1T, w2s=w2T, w3s=w3T, f1s=f1T, f2s=f2T, aq=aqT, ak=akT, av=avT, ao=aoT)
    shard_rows = dict(w1s=256, w2s=256, w3s=704, f1s=256, f2s=1024, aq=256, ak=256, av=256, ao=256)
    ones1 = np.ones((1, 128), dtype=np.float32)
    invcnt = np.zeros((1, 8), dtype=np.float32)
    for i, c in enumerate(QCOUNT):
        invcnt[0, i] = 1.0 / c
    sel = np.zeros((1, 8), dtype=np.float32)
    sel[0, 0:3] = 1.0
    maps = []
    for c in range(NCORE):
        m = dict(ones1=ones1, invcnt=invcnt, sel=sel)
        for nm, arr in full.items():
            rr = shard_rows[nm]
            m[nm] = np.ascontiguousarray(arr[rr * c:rr * (c + 1)])
        maps.append(m)
    return maps


def host_join_Q(resQ):
    out = {}
    for nm in ("w1s", "w2s", "w3s", "f1s", "f2s", "aq", "ak", "av", "ao"):
        out[nm] = np.concatenate([resQ[c]["q" + nm] for c in range(NCORE)], axis=0)
    return out



NF0 = H0 // 128


def build_C():
    nc = bass.Bass("TRN2", target_bir_lowering=False, debug=False)
    xT = nc.dram_tensor("xT", [D, TPC], F32, kind="ExternalInput").ap()
    yT = nc.dram_tensor("yT", [D, TPC], BF16, kind="ExternalInput").ap()
    qw1 = nc.dram_tensor("qw1", [NF0, 128, D], BF16, kind="ExternalInput").ap()
    qw2 = nc.dram_tensor("qw2", [NF0, 128, D], BF16, kind="ExternalInput").ap()
    qw3 = nc.dram_tensor("qw3", [DC, 128, H0], BF16, kind="ExternalInput").ap()
    ones_bf_d = nc.dram_tensor("ones_bf", [128, 1], BF16, kind="ExternalInput").ap()
    ones1_d = nc.dram_tensor("ones1", [1, 128], F32, kind="ExternalInput").ap()
    consts_d = nc.dram_tensor("consts", [128, 8], F32, kind="ExternalInput").ap()
    o_x2 = nc.dram_tensor("o_x2", [D, TPC], F32, kind="ExternalOutput").ap()

    r = Rec()
    with contextlib.ExitStack() as stack:
        ec = stack.enter_context
        x_tiles = [ec(nc.sbuf_tensor(f"x{c}", [128, TPC], F32)) for c in range(DC)]
        xl_tiles = [ec(nc.sbuf_tensor(f"xl{c}", [128, TPC], BF16)) for c in range(DC)]
        hid = [ec(nc.sbuf_tensor(f"hid{f}", [128, 512], BF16)) for f in range(NF0)]
        w1s = [ec(nc.sbuf_tensor(f"w1s{p}", [128, D], BF16)) for p in range(2)]
        w2s = [ec(nc.sbuf_tensor(f"w2s{p}", [128, D], BF16)) for p in range(2)]
        w3s = [ec(nc.sbuf_tensor(f"w3s{p}", [128, H0], BF16)) for p in range(2)]
        sg_sb = ec(nc.sbuf_tensor("sg", [128, 512], BF16))
        sb = {
            "stats": ec(nc.sbuf_tensor("stats", [1, 3 * TPC], F32)),
            "sqt": ec(nc.sbuf_tensor("sqt", [128, 512], BF16)),
            "tmp": ec(nc.sbuf_tensor("tmp", [128, 512], F32)),
            "ones_bf": ec(nc.sbuf_tensor("onesbf", [128, 1], BF16)),
            "ones1": ec(nc.sbuf_tensor("ones1s", [1, 128], F32)),
            "consts": ec(nc.sbuf_tensor("constsS", [128, 8], F32)),
        }
        ps = {
            "sts": ec(nc.psum_tensor("psts", [1, 512], F32)),
            "stq": ec(nc.psum_tensor("pstq", [1, 512], F32)),
            "mb": ec(nc.psum_tensor("pmb", [128, 512], F32)),
            "rb": ec(nc.psum_tensor("prb", [128, 512], F32)),
        }
        px1 = ec(nc.psum_tensor("px1", [128, 512], F32))
        px2 = ec(nc.psum_tensor("px2", [128, 512], F32))
        sems = _mk_sems(nc, stack)

        wx = [r.dma("sync", x_tiles[c][:], xT[128 * c:128 * (c + 1), :]) for c in range(DC)]
        wy = [r.dma("sync", xl_tiles[c][:], yT[128 * c:128 * (c + 1), :]) for c in range(DC)]
        wo1 = r.dma("sync", sb["ones_bf"][:], ones_bf_d)
        wo2 = r.dma("sync", sb["ones1"][:], ones1_d)
        wcn = r.dma("sync", sb["consts"][:], consts_d)

        wadd = None
        for c in range(DC):
            wadd = r.op("vector", lambda e, c=c: e.tensor_tensor(
                out=x_tiles[c][:], in0=x_tiles[c][:], in1=xl_tiles[c][:], op=ALU.add),
                waits=[wx[c], wy[c]])

        wln = emit_ln(r, sb, ps, x_tiles, xl_tiles, TPC, deps=[wadd, wo1, wo2, wcn])

        rd1 = [None, None]
        rd2 = [None, None]
        rd3 = [None, None]
        prev_sig = None
        prev_hct = None
        prev_out_te = None
        prev_radd = None
        for h in range(2):
            sl = slice(512 * h, 512 * (h + 1))
            for f in range(NF0):
                par = f % 2
                psem = "lde" if par == 0 else "ldo"
                wd1 = r.dma("sync", w1s[par][:], qw1[f], waits=[w for w in (rd1[par],) if w], sem=psem)
                wd2 = r.dma("sync", w2s[par][:], qw2[f], waits=[w for w in (rd2[par],) if w], sem=psem)
                w0 = [wd1, wln] + ([prev_sig] if prev_sig else [])
                for c in range(DC):
                    r.op("tensor", lambda e, c=c, f=f, par=par, sl=sl: e.matmul(
                        px1[:], lhsT=w1s[par][:, 128 * c:128 * (c + 1)], rhs=xl_tiles[c][:, sl],
                        start=(c == 0), stop=(c == DC - 1)), waits=w0 if c == 0 else [])
                rd1[par] = ("te", r.cnt["te"])
                wte1 = rd1[par]
                w0 = [wd2] + ([prev_hct] if prev_hct else [])
                for c in range(DC):
                    r.op("tensor", lambda e, c=c, f=f, par=par, sl=sl: e.matmul(
                        px2[:], lhsT=w2s[par][:, 128 * c:128 * (c + 1)], rhs=xl_tiles[c][:, sl],
                        start=(c == 0), stop=(c == DC - 1)), waits=w0 if c == 0 else [])
                rd2[par] = ("te", r.cnt["te"])
                wte2 = rd2[par]
                sgw = [wte1] + ([prev_hct] if prev_hct else [])
                prev_sig = r.op("scalar", lambda e: e.activation(
                    out=sg_sb[:], in_=px1[:], func=AF.Sigmoid), waits=sgw)
                hw = [wte2, prev_sig] + ([prev_out_te] if (prev_out_te and h == 1) else [])
                prev_hct = r.op("vector", lambda e, f=f: e.tensor_tensor(
                    out=hid[f][:], in0=px2[:], in1=sg_sb[:], op=ALU.mult), waits=hw)
            whid = prev_hct
            for dci in range(DC):
                par = dci % 2
                wd3 = r.dma("sync", w3s[par][:], qw3[dci], waits=[w for w in (rd3[par],) if w],
                            sem="lde" if par == 0 else "ldo")
                w0 = [wd3, whid, prev_sig] + ([prev_radd] if prev_radd else [])
                for k in range(NF0):
                    r.op("tensor", lambda e, k=k, par=par: e.matmul(
                        px1[:], lhsT=w3s[par][:, 128 * k:128 * (k + 1)], rhs=hid[k][:],
                        start=(k == 0), stop=(k == NF0 - 1)), waits=w0 if k == 0 else [])
                rd3[par] = ("te", r.cnt["te"])
                prev_out_te = rd3[par]
                prev_radd = r.op("vector", lambda e, dci=dci, sl=sl: e.tensor_tensor(
                    out=x_tiles[dci][:, sl], in0=x_tiles[dci][:, sl], in1=px1[:], op=ALU.add),
                    waits=[rd3[par]])
                prev_sig = prev_radd
        for c in range(DC):
            r.dma("sync", o_x2[128 * c:128 * (c + 1), :], x_tiles[c][:], waits=[prev_radd] if c == 0 else [])
        final = r.cnt["ld"]
        r.wait("sync", [("ld", final)])
        with nc.Block() as block:
            r.replay(block, sems)
    return nc


def slabify(wT, fchunks):
    K, F = wT.shape
    kc = K // 128
    assert F == fchunks * 128
    return np.ascontiguousarray(
        wT.reshape(kc, 128, fchunks, 128).transpose(2, 1, 0, 3).reshape(fchunks, 128, K))


def host_prep_C(resA_or_x, yseq, qj):
    s1 = slabify(qj["w1s"], NF0)
    s2 = slabify(qj["w2s"], NF0)
    s3 = slabify(qj["w3s"], DC)
    ones_bf = np.ones((128, 1), dtype=BF)
    ones1 = np.ones((1, 128), dtype=np.float32)
    consts = np.tile(np.asarray([CONST_VALS], dtype=np.float32), (128, 1))
    return [
        dict(xT=resA_or_x[c], yT=yseq[c], qw1=s1, qw2=s2, qw3=s3,
             ones_bf=ones_bf, ones1=ones1, consts=consts)
        for c in range(NCORE)
    ]


def host_y_from_B(resB):
    ys = []
    for c in range(NCORE):
        b = c // 4
        yb = np.concatenate([resB[4 * b + j]["o_y"] for j in range(4)], axis=0)
        tloc = (c % 4)
        ys.append(np.ascontiguousarray(yb[:, TPC * tloc:TPC * (tloc + 1)]))
    return ys




def build_D():
    nc = bass.Bass("TRN2", target_bir_lowering=False, debug=False)
    xT = nc.dram_tensor("xT", [D, TPC], F32, kind="ExternalInput").ap()
    qaq = nc.dram_tensor("qaq", [DC, 128, D], BF16, kind="ExternalInput").ap()
    qak = nc.dram_tensor("qak", [DC, 128, D], BF16, kind="ExternalInput").ap()
    qav = nc.dram_tensor("qav", [DC, 128, D], BF16, kind="ExternalInput").ap()
    bq_d = nc.dram_tensor("bq", [128, DC], F32, kind="ExternalInput").ap()
    bk_d = nc.dram_tensor("bk", [128, DC], F32, kind="ExternalInput").ap()
    bv_d = nc.dram_tensor("bv", [128, DC], F32, kind="ExternalInput").ap()
    cosq_d = nc.dram_tensor("cosq", [128, TPC], BF16, kind="ExternalInput").ap()
    sinq_d = nc.dram_tensor("sinq", [128, TPC], BF16, kind="ExternalInput").ap()
    cosk_d = nc.dram_tensor("cosk", [128, TPC], BF16, kind="ExternalInput").ap()
    sink_d = nc.dram_tensor("sink", [128, TPC], BF16, kind="ExternalInput").ap()
    ones_bf_d = nc.dram_tensor("ones_bf", [128, 1], BF16, kind="ExternalInput").ap()
    ones1_d = nc.dram_tensor("ones1", [1, 128], F32, kind="ExternalInput").ap()
    consts_d = nc.dram_tensor("consts", [128, 8], F32, kind="ExternalInput").ap()
    o_q = nc.dram_tensor("o_q", [D, TPC], BF16, kind="ExternalOutput").ap()
    o_k = nc.dram_tensor("o_k", [D, TPC], BF16, kind="ExternalOutput").ap()
    o_v = nc.dram_tensor("o_v", [D, TPC], BF16, kind="ExternalOutput").ap()

    r = Rec()
    with contextlib.ExitStack() as stack:
        ec = stack.enter_context
        x_tiles = [ec(nc.sbuf_tensor(f"x{c}", [128, TPC], F32)) for c in range(DC)]
        xl_tiles = [ec(nc.sbuf_tensor(f"xl{c}", [128, TPC], BF16)) for c in range(DC)]
        wsl = [ec(nc.sbuf_tensor(f"wsl{p}", [128, D], BF16)) for p in range(2)]
        pt = ec(nc.sbuf_tensor("pt", [128, TPC], BF16))
        rt = ec(nc.sbuf_tensor("rt", [128, TPC], BF16))
        rt2 = ec(nc.sbuf_tensor("rt2", [128, TPC], BF16))
        tabs = {nm: ec(nc.sbuf_tensor(nm + "S", [128, TPC], BF16))
                for nm in ("cosq", "sinq", "cosk", "sink")}
        bias_sb = {nm: ec(nc.sbuf_tensor(nm + "S", [128, DC], F32))
                   for nm in ("bq", "bk", "bv")}
        sb = {
            "stats": ec(nc.sbuf_tensor("stats", [1, 3 * TPC], F32)),
            "sqt": ec(nc.sbuf_tensor("sqt", [128, 512], BF16)),
            "tmp": ec(nc.sbuf_tensor("tmp", [128, 512], F32)),
            "ones_bf": ec(nc.sbuf_tensor("onesbf", [128, 1], BF16)),
            "ones1": ec(nc.sbuf_tensor("ones1s", [1, 128], F32)),
            "consts": ec(nc.sbuf_tensor("constsS", [128, 8], F32)),
        }
        ps = {
            "sts": ec(nc.psum_tensor("psts", [1, 512], F32)),
            "stq": ec(nc.psum_tensor("pstq", [1, 512], F32)),
            "mb": ec(nc.psum_tensor("pmb", [128, 512], F32)),
            "rb": ec(nc.psum_tensor("prb", [128, 512], F32)),
        }
        pq = ec(nc.psum_tensor("pq", [128, 512], F32))
        sems = _mk_sems(nc, stack)

        wx = [r.dma("sync", x_tiles[c][:], xT[128 * c:128 * (c + 1), :]) for c in range(DC)]
        wo1 = r.dma("sync", sb["ones_bf"][:], ones_bf_d)
        wo2 = r.dma("sync", sb["ones1"][:], ones1_d)
        wcn = r.dma("sync", sb["consts"][:], consts_d)
        for nm, d_ in (("cosq", cosq_d), ("sinq", sinq_d), ("cosk", cosk_d), ("sink", sink_d)):
            r.dma("sync", tabs[nm][:], d_)
        for nm, d_ in (("bq", bq_d), ("bk", bk_d), ("bv", bv_d)):
            r.dma("sync", bias_sb[nm][:], d_)
        wtabs = ("ld", r.cnt["ld"])

        wln = emit_ln(r, sb, ps, x_tiles, xl_tiles, TPC, deps=[wx[DC - 1], wo1, wo2, wcn, wtabs])

        rdw = [None, None]
        prev_ev = None
        prev_store = None
        projs = [("q", qaq, "bq", o_q, "cosq", "sinq"),
                 ("k", qak, "bk", o_k, "cosk", "sink"),
                 ("v", qav, "bv", o_v, None, None)]
        fidx = 0
        for (pname, slabs, bnm, odst, cosn, sinn) in projs:
            for f in range(DC):
                par = fidx % 2
                psem = "lde" if par == 0 else "ldo"
                wd = r.dma("sync", wsl[par][:], slabs[f], waits=[w for w in (rdw[par],) if w], sem=psem)
                for h in range(2):
                    sl = slice(512 * h, 512 * (h + 1))
                    w0 = [wd, wln] + ([prev_ev] if prev_ev else [])
                    for c in range(DC):
                        r.op("tensor", lambda e, c=c, par=par, sl=sl: e.matmul(
                            pq[:], lhsT=wsl[par][:, 128 * c:128 * (c + 1)], rhs=xl_tiles[c][:, sl],
                            start=(c == 0), stop=(c == DC - 1)), waits=w0 if c == 0 else [])
                    rdw[par] = ("te", r.cnt["te"])
                    evw = [rdw[par]] + ([prev_store] if prev_store else [])
                    prev_ev = r.op("scalar", lambda e, sl=sl, bnm=bnm, f=f: e.activation(
                        out=pt[:, sl], in_=pq[:], func=AF.Identity,
                        bias=bias_sb[bnm][:, f:f + 1]), waits=evw)
                if cosn is not None:
                    wc1 = r.op("vector", lambda e: e.tensor_copy(out=rt[0:64, :], in_=pt[64:128, :]), waits=[prev_ev])
                    wc2 = r.op("vector", lambda e: e.tensor_copy(out=rt[64:128, :], in_=pt[0:64, :]), waits=[prev_ev])
                    wm1 = r.op("vector", lambda e, cosn=cosn: e.tensor_tensor(
                        out=pt[:], in0=pt[:], in1=tabs[cosn][:], op=ALU.mult), waits=[wc1, wc2])
                    wm2 = r.op("vector", lambda e, sinn=sinn: e.tensor_tensor(
                        out=rt2[:], in0=rt[:], in1=tabs[sinn][:], op=ALU.mult), waits=[wm1])
                    wad = r.op("vector", lambda e: e.tensor_tensor(
                        out=pt[:], in0=pt[:], in1=rt2[:], op=ALU.add), waits=[wm2])
                    prev_ev = wad
                prev_store = r.dma("sync", odst[128 * f:128 * (f + 1), :], pt[:], waits=[prev_ev])
                fidx += 1
        r.wait("sync", [("ld", r.cnt["ld"])])
        with nc.Block() as block:
            r.replay(block, sems)
    return nc



NKT = SEQ // 128


def build_E():
    nc = bass.Bass("TRN2", target_bir_lowering=False, debug=False)
    xT = nc.dram_tensor("xT", [D, TPC], F32, kind="ExternalInput").ap()
    qf_d = nc.dram_tensor("qf", [D, TPC], BF16, kind="ExternalInput").ap()
    kf_d = nc.dram_tensor("kf", [NH, 128, SEQ], BF16, kind="ExternalInput").ap()
    vt_d = nc.dram_tensor("vt", [NH, 128, SEQ], BF16, kind="ExternalInput").ap()
    mask_d = nc.dram_tensor("mask8", [NKT, 128, TPC], mybir.dt.int8, kind="ExternalInput").ap()
    qao = nc.dram_tensor("qao", [DC, 128, D], BF16, kind="ExternalInput").ap()
    bo_d = nc.dram_tensor("bo", [128, DC], F32, kind="ExternalInput").ap()
    ones_bf_d = nc.dram_tensor("ones_bf", [128, 1], BF16, kind="ExternalInput").ap()
    ones1_d = nc.dram_tensor("ones1", [1, 128], F32, kind="ExternalInput").ap()
    consts_d = nc.dram_tensor("consts", [128, 8], F32, kind="ExternalInput").ap()
    o_x3 = nc.dram_tensor("o_x3", [D, TPC], F32, kind="ExternalOutput").ap()

    r = Rec()
    with contextlib.ExitStack() as stack:
        ec = stack.enter_context
        x_tiles = [ec(nc.sbuf_tensor(f"x{c}", [128, TPC], F32)) for c in range(DC)]
        qf = [ec(nc.sbuf_tensor(f"qf{c}", [128, TPC], BF16)) for c in range(DC)]
        mg = [ec(nc.sbuf_tensor(f"mg{c}", [128, TPC], BF16)) for c in range(NH)]
        m8 = ec(nc.sbuf_tensor("m8", [128, NKT * TPC], mybir.dt.int8))
        kvh = [ec(nc.sbuf_tensor(f"kvh{p}", [128, SEQ], BF16)) for p in range(2)]
        vvh = [ec(nc.sbuf_tensor(f"vvh{p}", [128, SEQ], BF16)) for p in range(2)]
        s_sb = ec(nc.sbuf_tensor("sS", [128, 512], BF16))
        binv = ec(nc.sbuf_tensor("binv", [128, 512], BF16))
        srow = ec(nc.sbuf_tensor("srow", [1, 512], F32))
        sb = {
            "ones_bf": ec(nc.sbuf_tensor("onesbf", [128, 1], BF16)),
            "ones1": ec(nc.sbuf_tensor("ones1s", [1, 128], F32)),
            "consts": ec(nc.sbuf_tensor("constsS", [128, 8], F32)),
        }
        bo_sb = ec(nc.sbuf_tensor("boS", [128, DC], F32))
        ps_s = [ec(nc.psum_tensor(f"pss{i}", [128, 512], F32)) for i in range(2)]
        ps_ctx = ec(nc.psum_tensor("psctx", [128, 512], F32))
        ps_sum = ec(nc.psum_tensor("pssum", [1, 512], F32))
        ps_b = ec(nc.psum_tensor("psb", [128, 512], F32))
        sems = _mk_sems(nc, stack)

        wx = [r.dma("sync", x_tiles[c][:], xT[128 * c:128 * (c + 1), :]) for c in range(DC)]
        wq = [r.dma("sync", qf[c][:], qf_d[128 * c:128 * (c + 1), :]) for c in range(DC)]
        wm8 = r.dma("sync", m8[:].rearrange("p (kt t) -> p kt t", t=TPC),
                    mask_d.rearrange("kt p t -> p kt t"))
        wo1 = r.dma("sync", sb["ones_bf"][:], ones_bf_d)
        wo2 = r.dma("sync", sb["ones1"][:], ones1_d)
        wcn = r.dma("sync", sb["consts"][:], consts_d)
        wbo = r.dma("sync", bo_sb[:], bo_d)
        m83 = m8[:].rearrange("p (kt t) -> p kt t", t=TPC)
        wall = ("ld", r.cnt["ld"])

        rdk = [None, None]
        prev_exp = None
        prev_pv = None
        prev_evs = None
        prev_ctxev = None
        prev_sumev = None
        prev_bev = None
        cons = sb["consts"]
        for h in range(NH):
            par = h % 2
            psem = "lde" if par == 0 else "ldo"
            wkh = r.dma("sync", kvh[par][:], kf_d[h], waits=[w for w in (rdk[par],) if w], sem=psem)
            wvh = r.dma("sync", vvh[par][:], vt_d[h], waits=[], sem=psem)
            for qc in range(2):
                sl = slice(512 * qc, 512 * (qc + 1))
                for kt in range(NKT):
                    pp = kt % 2
                    w0 = [wkh, wall] + ([prev_exp] if prev_exp else [])
                    wmm = r.op("tensor", lambda e, h=h, kt=kt, par=par, sl=sl, pp=pp: e.matmul(
                        ps_s[pp][:], lhsT=kvh[par][:, 128 * kt:128 * (kt + 1)], rhs=qf[h][:, sl],
                        start=True, stop=True), waits=w0)
                    evw = [wmm] + ([prev_exp] if prev_exp else []) + ([prev_pv] if prev_pv else [])
                    wev = r.op("vector", lambda e, kt=kt, sl=sl, pp=pp: e.scalar_tensor_tensor(
                        out=s_sb[:], in0=m83[:, kt, sl], scalar=MASK_NEG, in1=ps_s[pp][:],
                        op0=ALU.mult, op1=ALU.add), waits=evw)
                    wexp = r.op("scalar", lambda e: e.activation(
                        out=s_sb[:], in_=s_sb[:], func=AF.Exp,
                        bias=cons[:, C_EXPB:C_EXPB + 1]), waits=[wev])
                    prev_evs = wev
                    pvw = [wexp, wvh]
                    if kt == 0 and prev_ctxev:
                        pvw.append(prev_ctxev)
                    r.op("tensor", lambda e, kt=kt, par=par: e.matmul(
                        ps_ctx[:], lhsT=vvh[par][:, 128 * kt:128 * (kt + 1)], rhs=s_sb[:],
                        start=(kt == 0), stop=(kt == NKT - 1)), waits=pvw)
                    smw = []
                    if kt == 0 and prev_sumev:
                        smw.append(prev_sumev)
                    prev_pv = r.op("tensor", lambda e, kt=kt: e.matmul(
                        ps_sum[:], lhsT=sb["ones_bf"][:], rhs=s_sb[:],
                        start=(kt == 0), stop=(kt == NKT - 1)), waits=smw)
                    prev_exp = wexp
                wte = ("te", r.cnt["te"])
                prev_sumev = r.op("scalar", lambda e: e.copy(out=srow[:], in_=ps_sum[:]), waits=[wte])
                wrc = r.op("vector", lambda e: e.reciprocal(out=srow[:], in_=srow[:]), waits=[prev_sumev])
                wbc = r.op("tensor", lambda e: e.matmul(
                    ps_b[:], lhsT=sb["ones1"][:], rhs=srow[:], start=True, stop=True),
                    waits=[wrc] + ([prev_bev] if prev_bev else []))
                prev_bev = r.op("vector", lambda e: e.tensor_copy(out=binv[:], in_=ps_b[:]), waits=[wbc])
                prev_ctxev = r.op("vector", lambda e, h=h, sl=sl: e.tensor_tensor(
                    out=mg[h][:, sl], in0=ps_ctx[:], in1=binv[:], op=ALU.mult),
                    waits=[prev_bev, wte])
            rdk[par] = ("te", r.cnt["te"])
        rdw = [prev_ctxev, prev_ctxev]
        prev_radd = None
        wsl = kvh
        for dci in range(DC):
            par = dci % 2
            psem = "lde" if par == 0 else "ldo"
            wd = r.dma("sync", wsl[par][:, 0:D], qao[dci], waits=[w for w in (rdw[par],) if w], sem=psem)
            for hf in range(2):
                sl = slice(512 * hf, 512 * (hf + 1))
                w0 = [wd, prev_ctxev]
                if prev_radd:
                    w0.append(prev_radd)
                for k in range(DC):
                    r.op("tensor", lambda e, k=k, par=par, sl=sl: e.matmul(
                        ps_b[:], lhsT=wsl[par][:, 0:D][:, 128 * k:128 * (k + 1)], rhs=mg[k][:, sl],
                        start=(k == 0), stop=(k == DC - 1)), waits=w0 if k == 0 else [])
                rdw[par] = ("te", r.cnt["te"])
                prev_radd = r.op("vector", lambda e, dci=dci, sl=sl: e.scalar_tensor_tensor(
                    out=x_tiles[dci][:, sl], in0=ps_b[:], scalar=bo_sb[:, dci:dci + 1],
                    in1=x_tiles[dci][:, sl], op0=ALU.add, op1=ALU.add),
                    waits=[rdw[par], wx[dci]])
        for c in range(DC):
            r.dma("sync", o_x3[128 * c:128 * (c + 1), :], x_tiles[c][:], waits=[prev_radd] if c == 0 else [])
        r.wait("sync", [("ld", r.cnt["ld"])])
        with nc.Block() as block:
            r.replay(block, sems)
    return nc



NF1 = H1 // 128


def build_F():
    nc = bass.Bass("TRN2", target_bir_lowering=False, debug=False)
    xT = nc.dram_tensor("xT", [D, TPC], F32, kind="ExternalInput").ap()
    qf1 = nc.dram_tensor("qf1", [NF1, 128, D], BF16, kind="ExternalInput").ap()
    qf2 = nc.dram_tensor("qf2", [DC, 128, H1], BF16, kind="ExternalInput").ap()
    b1_d = nc.dram_tensor("b1", [128, NF1], F32, kind="ExternalInput").ap()
    b2_d = nc.dram_tensor("b2", [128, DC], F32, kind="ExternalInput").ap()
    ones_bf_d = nc.dram_tensor("ones_bf", [128, 1], BF16, kind="ExternalInput").ap()
    ones1_d = nc.dram_tensor("ones1", [1, 128], F32, kind="ExternalInput").ap()
    consts_d = nc.dram_tensor("consts", [128, 8], F32, kind="ExternalInput").ap()
    o_xo = nc.dram_tensor("o_xo", [D, TPC], F32, kind="ExternalOutput").ap()

    r = Rec()
    with contextlib.ExitStack() as stack:
        ec = stack.enter_context
        x_tiles = [ec(nc.sbuf_tensor(f"x{c}", [128, TPC], F32)) for c in range(DC)]
        xl_tiles = [ec(nc.sbuf_tensor(f"xl{c}", [128, TPC], BF16)) for c in range(DC)]
        hid = [ec(nc.sbuf_tensor(f"hid{f}", [128, 512], BF16)) for f in range(NF1)]
        w1s = [ec(nc.sbuf_tensor(f"w1s{p}", [128, D], BF16)) for p in range(2)]
        w2s = ec(nc.sbuf_tensor("w2s", [128, H1], BF16))
        b1_sb = ec(nc.sbuf_tensor("b1S", [128, NF1], F32))
        b2_sb = ec(nc.sbuf_tensor("b2S", [128, DC], F32))
        sb = {
            "stats": ec(nc.sbuf_tensor("stats", [1, 3 * TPC], F32)),
            "sqt": ec(nc.sbuf_tensor("sqt", [128, 512], BF16)),
            "tmp": ec(nc.sbuf_tensor("tmp", [128, 512], F32)),
            "ones_bf": ec(nc.sbuf_tensor("onesbf", [128, 1], BF16)),
            "ones1": ec(nc.sbuf_tensor("ones1s", [1, 128], F32)),
            "consts": ec(nc.sbuf_tensor("constsS", [128, 8], F32)),
        }
        ps = {
            "sts": ec(nc.psum_tensor("psts", [1, 512], F32)),
            "stq": ec(nc.psum_tensor("pstq", [1, 512], F32)),
            "mb": ec(nc.psum_tensor("pmb", [128, 512], F32)),
            "rb": ec(nc.psum_tensor("prb", [128, 512], F32)),
        }
        px1 = ec(nc.psum_tensor("px1", [128, 512], F32))
        sems = _mk_sems(nc, stack)

        wx = [r.dma("sync", x_tiles[c][:], xT[128 * c:128 * (c + 1), :]) for c in range(DC)]
        wo1 = r.dma("sync", sb["ones_bf"][:], ones_bf_d)
        wo2 = r.dma("sync", sb["ones1"][:], ones1_d)
        wcn = r.dma("sync", sb["consts"][:], consts_d)
        wb1 = r.dma("sync", b1_sb[:], b1_d)
        wb2 = r.dma("sync", b2_sb[:], b2_d)

        wln = emit_ln(r, sb, ps, x_tiles, xl_tiles, TPC, deps=[wx[DC - 1], wo1, wo2, wcn, wb1, wb2])

        rd1 = [None, None]
        rd2 = None
        prev_gel = None
        prev_out_te = None
        prev_radd = None
        for h in range(2):
            sl = slice(512 * h, 512 * (h + 1))
            for f in range(NF1):
                par = f % 2
                psem = "lde" if par == 0 else "ldo"
                wd1 = r.dma("sync", w1s[par][:], qf1[f], waits=[w for w in (rd1[par],) if w], sem=psem)
                w0 = [wd1, wln] + ([prev_gel] if prev_gel else [])
                for c in range(DC):
                    r.op("tensor", lambda e, c=c, par=par, sl=sl: e.matmul(
                        px1[:], lhsT=w1s[par][:, 128 * c:128 * (c + 1)], rhs=xl_tiles[c][:, sl],
                        start=(c == 0), stop=(c == DC - 1)), waits=w0 if c == 0 else [])
                rd1[par] = ("te", r.cnt["te"])
                gw = [rd1[par]]
                if h == 1 and prev_out_te:
                    gw.append(prev_out_te)
                prev_gel = r.op("scalar", lambda e, f=f: e.activation(
                    out=hid[f][:], in_=px1[:], func=AF.Gelu, bias=b1_sb[:, f:f + 1]), waits=gw)
            whid = prev_gel
            for dci in range(DC):
                wd2 = r.dma("sync", w2s[:], qf2[dci], waits=[w for w in (rd2,) if w], sem="lde")
                w0 = [wd2, whid, prev_gel] + ([prev_radd] if prev_radd else [])
                for k in range(NF1):
                    r.op("tensor", lambda e, k=k: e.matmul(
                        px1[:], lhsT=w2s[:, 128 * k:128 * (k + 1)], rhs=hid[k][:],
                        start=(k == 0), stop=(k == NF1 - 1)), waits=w0 if k == 0 else [])
                rd2 = ("te", r.cnt["te"])
                prev_out_te = rd2
                prev_radd = r.op("vector", lambda e, dci=dci, sl=sl: e.scalar_tensor_tensor(
                    out=x_tiles[dci][:, sl], in0=px1[:], scalar=b2_sb[:, dci:dci + 1],
                    in1=x_tiles[dci][:, sl], op0=ALU.add, op1=ALU.add),
                    waits=[rd2])
                prev_gel = prev_radd
        for c in range(DC):
            r.dma("sync", o_xo[128 * c:128 * (c + 1), :], x_tiles[c][:], waits=[prev_radd] if c == 0 else [])
        r.wait("sync", [("ld", r.cnt["ld"])])
        with nc.Block() as block:
            r.replay(block, sems)
    return nc



QSPEC_F = [
    ("w1f", 704, D, 0),
    ("w2f", 704, D, 0),
    ("w3f", 256, H0, 0),
    ("f1f", 1024, D, 1),
    ("f2f", 256, H1, 1),
]
CSPEC_F = [("aqf", 256, D), ("akf", 256, D), ("avf", 256, D), ("aof", 256, D)]
GRP4 = [[0, 1, 2, 3], [4, 5, 6, 7]]
GRP8 = [list(range(NCORE))]


def build_fused(upto="F"):
    nc = bass.Bass("TRN2", target_bir_lowering=False, debug=False)
    ap = lambda t: t.ap()
    xT = nc.dram_tensor("xT", [D, TPC], F32, kind="ExternalInput").ap()
    wbcT = nc.dram_tensor("wbcT", [D, 128], F32, kind="ExternalInput").ap()
    wdtT = nc.dram_tensor("wdtT", [DT_RANK, D], F32, kind="ExternalInput").ap()
    bdt = nc.dram_tensor("bdt", [128, DC], F32, kind="ExternalInput").ap()
    alog_d = nc.dram_tensor("alog", [128, 128], F32, kind="ExternalInput").ap()
    dpar_d = nc.dram_tensor("dpar", [128, NDT], F32, kind="ExternalInput").ap()
    wsh_in = {}
    for (nm, rows, cols, _) in QSPEC_F:
        wsh_in[nm] = nc.dram_tensor(nm, [rows, cols], F32, kind="ExternalInput").ap()
    for (nm, rows, cols) in CSPEC_F:
        wsh_in[nm] = nc.dram_tensor(nm, [rows, cols], F32, kind="ExternalInput").ap()
    invcnt_d = nc.dram_tensor("invcnt", [1, 8], F32, kind="ExternalInput").ap()
    sel_d = nc.dram_tensor("sel", [1, 8], F32, kind="ExternalInput").ap()
    bq_d = nc.dram_tensor("bq", [128, DC], F32, kind="ExternalInput").ap()
    bk_d = nc.dram_tensor("bk", [128, DC], F32, kind="ExternalInput").ap()
    bvrow_d = nc.dram_tensor("bvrow", [1, D], F32, kind="ExternalInput").ap()
    bo_d = nc.dram_tensor("bo", [128, DC], F32, kind="ExternalInput").ap()
    b1_d = nc.dram_tensor("b1", [128, NF1], F32, kind="ExternalInput").ap()
    b2_d = nc.dram_tensor("b2", [128, DC], F32, kind="ExternalInput").ap()
    rope_d = {nm: nc.dram_tensor(nm, [128, TPC], BF16, kind="ExternalInput").ap()
              for nm in ("cosq", "sinq", "cosk", "sink")}
    mask_d = nc.dram_tensor("mask8", [NKT, 128, TPC], mybir.dt.int8, kind="ExternalInput").ap()
    ones_bf_d = nc.dram_tensor("ones_bf", [128, 1], BF16, kind="ExternalInput").ap()
    ones1_d = nc.dram_tensor("ones1", [1, 128], F32, kind="ExternalInput").ap()
    consts_d = nc.dram_tensor("consts", [128, 8], F32, kind="ExternalInput").ap()
    o_out = nc.dram_tensor("o_out", [D, TPC], F32, kind="ExternalOutput").ap()

    g_b = nc.dram_tensor("g_b", [1, 8], F32).ap()
    g_ar = nc.dram_tensor("g_ar", [1, 8], F32).ap()
    ag_in = {}
    ag_out = {}
    for (nm, rows, cols, _) in QSPEC_F:
        ag_in[nm] = nc.dram_tensor("agi_" + nm, [rows, cols], BF16).ap()
        ag_out[nm] = nc.dram_tensor("ago_" + nm, [NCORE * rows, cols], BF16).ap()
    for (nm, rows, cols) in CSPEC_F:
        ag_in[nm] = nc.dram_tensor("agi_" + nm, [rows, cols], BF16).ap()
        ag_out[nm] = nc.dram_tensor("ago_" + nm, [NCORE * rows, cols], BF16).ap()
    du_cat = nc.dram_tensor("du_cat", [4 * TPC, TPC], BF16).ap()
    du_ag = nc.dram_tensor("du_ag", [16 * TPC, TPC], BF16).ap()
    bc_in = nc.dram_tensor("bc_in", [64, TPC], BF16).ap()
    bc_ag = nc.dram_tensor("bc_ag", [256, TPC], BF16).ap()
    y_in = nc.dram_tensor("y_in", [D, TPC], BF16).ap()
    y_ag = nc.dram_tensor("y_ag", [4 * D, TPC], BF16).ap()
    q_loc = nc.dram_tensor("q_loc", [D, TPC], BF16).ap()
    k_in = nc.dram_tensor("k_in", [D, TPC], BF16).ap()
    k_ag = nc.dram_tensor("k_ag", [4 * D, TPC], BF16).ap()
    v_in = nc.dram_tensor("v_in", [D, TPC], BF16).ap()
    v_ag = nc.dram_tensor("v_ag", [4 * D, TPC], BF16).ap()
    x2_loc = nc.dram_tensor("x2_loc", [D, TPC], F32).ap()
    x3_loc = nc.dram_tensor("x3_loc", [D, TPC], F32).ap()

    r = Rec()
    rst = {}

    def dyn_expr(e, key, build):
        k = (e.engine, key)
        if k not in rst:
            rst[k] = build(e)
        return rst[k]

    def jc_of(e):
        return dyn_expr(e, "jc", lambda e: e.scalar_reg_alu(
            ALU.mod, e.partition_id(), 4))

    with contextlib.ExitStack() as stack:
        ec = stack.enter_context
        sb = {
            "stats": ec(nc.sbuf_tensor("stats", [1, 3 * TPC], F32)),
            "sqt": ec(nc.sbuf_tensor("sqt", [128, 512], BF16)),
            "tmp": ec(nc.sbuf_tensor("tmpG", [128, 512], F32)),
            "ones_bf": ec(nc.sbuf_tensor("onesbf", [128, 1], BF16)),
            "ones1": ec(nc.sbuf_tensor("ones1s", [1, 128], F32)),
            "consts": ec(nc.sbuf_tensor("constsS", [128, 8], F32)),
        }
        ps = {
            "sts": ec(nc.psum_tensor("psts", [1, 512], F32)),
            "stq": ec(nc.psum_tensor("pstq", [1, 512], F32)),
            "p0": ec(nc.psum_tensor("p0", [128, 512], F32)),
            "p1": ec(nc.psum_tensor("p1", [128, 512], F32)),
            "p2": ec(nc.psum_tensor("p2", [128, 512], F32)),
            "p3": ec(nc.psum_tensor("p3", [128, 512], F32)),
            "row": ec(nc.psum_tensor("prow", [1, 512], F32)),
        }
        ps["mb"] = ps["p0"]
        ps["rb"] = ps["p1"]
        sems = _mk_sems(nc, stack)
        cons = sb["consts"]

        wstack = contextlib.ExitStack()
        wbc_sb = wstack.enter_context(nc.sbuf_tensor("wbc", [128, DC * 128], BF16))
        wdt_sb = wstack.enter_context(nc.sbuf_tensor("wdt", [DT_RANK, D], BF16))
        wo1 = r.dma("sync", sb["ones_bf"][:], ones_bf_d)
        wo2 = r.dma("sync", sb["ones1"][:], ones1_d)
        wcn = r.dma("sync", sb["consts"][:], consts_d)
        waux = ("ld", r.cnt["ld"])
        ww1 = r.dma("gpsimd", wbc_sb[:].rearrange("p (c o) -> p c o", o=128),
                    wbcT.rearrange("(c p) o -> p c o", p=128))
        ww2 = r.dma("gpsimd", wdt_sb[:], wdtT)

        def phase_Q():
            with contextlib.ExitStack() as qs:
                qc = qs.enter_context
                wt = qc(nc.sbuf_tensor("wt", [128, H1], F32))
                qt = qc(nc.sbuf_tensor("qt", [128, H1], BF16))
                q2 = qc(nc.sbuf_tensor("q2", [128, H1], BF16))
                pacc = qc(nc.sbuf_tensor("pacc", [128, 16], F32))
                redc = qc(nc.sbuf_tensor("redc", [128, 1], F32))
                grow = qc(nc.sbuf_tensor("grow", [1, 16], F32))
                thb = qc(nc.sbuf_tensor("thb", [128, 8], F32))
                nthb = qc(nc.sbuf_tensor("nthb", [128, 8], F32))
                onescol = qc(nc.sbuf_tensor("onescol", [128, 1], F32))

                wic = r.dma("sync", grow[0:1, 8:16], invcnt_d)
                wsel = r.dma("sync", thb[0:1, 0:8], sel_d)
                wmo = r.op("vector", lambda e: e.memset(pacc[:], 0.0), waits=[])
                woc = r.op("vector", lambda e: e.memset(onescol[:], 1.0), waits=[])

                def tiles_of(rows):
                    out_ = []
                    p = 0
                    while p < rows:
                        out_.append((p, min(128, rows - p)))
                        p += 128
                    return out_

                prev = wmo
                wmm_dep = []
                for wi, (nm, rows, cols, _) in enumerate(QSPEC_F):
                    for ti, (p0_, np_) in enumerate(tiles_of(rows)):
                        wl = r.dma("sync", wt[0:np_, 0:cols], wsh_in[nm][p0_:p0_ + np_, :], waits=[prev])
                        prev = r.op("vector", lambda e, np_=np_, cols=cols, ti=ti: e.reduce_sum(
                            out=pacc[0:np_, ti + 1:ti + 2], in_=wt[0:np_, 0:cols],
                            axis=mybir.AxisListType.X, apply_absolute_value=True), waits=[wl])
                    prev = r.op("vector", lambda e: e.reduce_sum(
                        out=redc[:], in_=pacc[:, 0:16], axis=mybir.AxisListType.X), waits=[prev])
                    wq = r.op("tensor", lambda e, wi=wi: e.matmul(
                        ps["row"][0:1, wi:wi + 1], lhsT=onescol[:], rhs=redc[:],
                        start=True, stop=True), waits=[prev, woc])
                    wmm_dep.append(wq)
                    prev = r.op("vector", lambda e: e.memset(pacc[:], 0.0), waits=[wq])
                wmg = r.op("vector", lambda e: e.memset(grow[0:1, 0:8], 1.0), waits=[])
                wg = r.op("scalar", lambda e: e.copy(out=grow[0:1, 0:5], in_=ps["row"][0:1, 0:5]), waits=wmm_dep + [wmg])
                wgd = r.dma("gpsimd", g_b, grow[0:1, 0:8], waits=[wg])
                war = r.collective("AllReduce", ALU.add, GRP8, [g_b.opt()], [g_ar.opt()], waits=[wgd])
                wgb = r.dma("sync", grow[0:1, 0:8], g_ar, waits=[war])

                wm1 = r.op("vector", lambda e: e.tensor_mul(out=grow[0:1, 0:8], in0=grow[0:1, 0:8], in1=grow[0:1, 8:16]), waits=[wgb, wic])
                wa_ = r.op("vector", lambda e: e.tensor_scalar_add(out=grow[0:1, 8:16], in0=grow[0:1, 0:8], scalar1=1e-8), waits=[wm1])
                wb_ = r.op("vector", lambda e: e.tensor_scalar_max(out=grow[0:1, 0:8], in0=grow[0:1, 0:8], scalar1=1e-8), waits=[wa_])
                wbl1 = r.op("vector", lambda e: e.tensor_sub(out=grow[0:1, 8:16], in0=grow[0:1, 8:16], in1=grow[0:1, 0:8]), waits=[wa_, wb_])
                wbl2 = r.op("vector", lambda e: e.tensor_mul(out=grow[0:1, 8:16], in0=grow[0:1, 8:16], in1=thb[0:1, 0:8]), waits=[wbl1, wsel])
                wbl3 = r.op("vector", lambda e: e.tensor_add(out=grow[0:1, 0:8], in0=grow[0:1, 0:8], in1=grow[0:1, 8:16]), waits=[wbl2])
                wth = r.op("vector", lambda e: e.tensor_scalar_mul(out=grow[0:1, 0:8], in0=grow[0:1, 0:8], scalar1=0.5), waits=[wbl3])
                wtb = r.op("tensor", lambda e: e.matmul(ps["p2"][0:128, 0:8], lhsT=sb["ones1"][:], rhs=grow[0:1, 0:8], start=True, stop=True), waits=[wth, waux])
                wtc = r.op("vector", lambda e: e.tensor_copy(out=thb[:], in_=ps["p2"][0:128, 0:8]), waits=[wtb])
                wtn = r.op("vector", lambda e: e.tensor_scalar_mul(out=nthb[:], in0=thb[:], scalar1=-1.0), waits=[wtc])

                prev_q = wtn
                prev_out = None
                ag_store = {}
                for wi, (nm, rows, cols, _) in enumerate(QSPEC_F):
                    for (p0_, np_) in tiles_of(rows):
                        wl = r.dma("sync", wt[0:np_, 0:cols], wsh_in[nm][p0_:p0_ + np_, :],
                                   waits=[prev_q] + ([prev_out] if prev_out else []))
                        w1w = [wl]
                        w1_ = r.op("vector", lambda e, np_=np_, cols=cols, wi=wi: e.tensor_scalar(
                            out=qt[0:np_, 0:cols], in0=wt[0:np_, 0:cols], scalar1=thb[0:np_, wi:wi + 1],
                            scalar2=None, op0=ALU.is_ge), waits=w1w)
                        w2_ = r.op("vector", lambda e, np_=np_, cols=cols, wi=wi: e.tensor_scalar(
                            out=q2[0:np_, 0:cols], in0=wt[0:np_, 0:cols], scalar1=nthb[0:np_, wi:wi + 1],
                            scalar2=None, op0=ALU.is_le), waits=[])
                        w3_ = r.op("vector", lambda e, np_=np_, cols=cols: e.tensor_sub(
                            out=qt[0:np_, 0:cols], in0=qt[0:np_, 0:cols], in1=q2[0:np_, 0:cols]), waits=[w1_, w2_])
                        prev_out = r.dma("sync", ag_in[nm][p0_:p0_ + np_, :], qt[0:np_, 0:cols], waits=[w3_])
                        prev_q = w3_
                    ag_store[nm] = prev_out
                for (nm, rows, cols) in CSPEC_F:
                    for (p0_, np_) in tiles_of(rows):
                        wl = r.dma("sync", wt[0:np_, 0:cols], wsh_in[nm][p0_:p0_ + np_, :],
                                   waits=[prev_q] + ([prev_out] if prev_out else []))
                        cw = [wl]
                        prev_q = r.op("vector", lambda e, np_=np_, cols=cols: e.tensor_copy(
                            out=qt[0:np_, 0:cols], in_=wt[0:np_, 0:cols]), waits=cw)
                        prev_out = r.dma("sync", ag_in[nm][p0_:p0_ + np_, :], qt[0:np_, 0:cols], waits=[prev_q])
                    ag_store[nm] = prev_out
            return ag_store
        ag_store = phase_Q()
        ag_waits = {}

        def phase_A():
            r.wait("sync", r.fence())
            with contextlib.ExitStack() as asx:
                ac = asx.enter_context
                x_tiles = [ac(nc.sbuf_tensor(f"ax{c}", [128, TPC], F32)) for c in range(DC)]
                xl_tiles = [ac(nc.sbuf_tensor(f"axl{c}", [128, TPC], BF16)) for c in range(DC)]
                bdt_sb = ac(nc.sbuf_tensor("bdtS", [128, DC], F32))
                dtr_sb = ac(nc.sbuf_tensor("dtr", [DT_RANK, TPC], BF16))
                bc_sb = ac(nc.sbuf_tensor("bcS", [64, TPC], BF16))
                delta_sb = [ac(nc.sbuf_tensor(f"adl{c}", [128, TPC], BF16)) for c in range(DC)]
                t_abs = ac(nc.sbuf_tensor("tabs", [128, 512], F32))
                t_rel = ac(nc.sbuf_tensor("trel", [128, 512], F32))
                ps_db = [ps["p2"], ps["p3"]]
                ps_dt = [ps["p0"], ps["p1"]]

                wx = [r.dma("sync", x_tiles[c][:], xT[128 * c:128 * (c + 1), :]) for c in range(DC)]
                wb = r.dma("sync", bdt_sb[:], bdt)

                wln = emit_ln(r, sb, ps, x_tiles, xl_tiles, TPC, deps=[wx[DC - 1], waux])

                ev_dtr = [None, None]
                for h in range(2):
                    sl = slice(512 * h, 512 * (h + 1))
                    for c in range(DC):
                        r.op("tensor", lambda e, c=c, h=h, sl=sl: e.matmul(
                            ps_db[h][:], lhsT=wbc_sb[:, 128 * c:128 * (c + 1)], rhs=xl_tiles[c][:, sl],
                            start=(c == 0), stop=(c == DC - 1)),
                            waits=[wln, ww1] if c == 0 else [])
                    wt_ = ("te", r.cnt["te"])
                    ev_dtr[h] = r.op("scalar", lambda e, h=h, sl=sl: e.copy(out=dtr_sb[:, sl], in_=ps_db[h][0:64, :]), waits=[wt_])
                    r.op("scalar", lambda e, h=h, sl=sl: e.copy(out=bc_sb[:, sl], in_=ps_db[h][64:128, :]), waits=[])
                wbc_done = ("sc", r.cnt["sc"])
                wbcst = r.dma("sync", bc_in, bc_sb[:], waits=[wbc_done])
                wag_bc = r.collective("AllGather", ALU.bypass, GRP4, [bc_in.opt()], [bc_ag.opt()], waits=[wbcst])

                prev_psum_reader = [None, None]
                prev_vadd = None
                du_st = None
                for c in range(DC):
                    for h in range(2):
                        sl = slice(512 * h, 512 * (h + 1))
                        w0 = [ev_dtr[h], ww2, wb]
                        if prev_psum_reader[h] is not None:
                            w0.append(prev_psum_reader[h])
                        wmm = r.op("tensor", lambda e, c=c, h=h, sl=sl: e.matmul(
                            ps_dt[h][:], lhsT=wdt_sb[:, 128 * c:128 * (c + 1)], rhs=dtr_sb[:, sl],
                            start=True, stop=True), waits=w0)
                        wa_w = [wmm] + ([prev_vadd] if prev_vadd is not None else [])
                        wa = r.op("scalar", lambda e, c=c, h=h: e.activation(
                            out=t_abs[:], in_=ps_dt[h][:], func=AF.Abs, bias=bdt_sb[:, c:c + 1]), waits=wa_w)
                        wr2 = r.op("scalar", lambda e, c=c, h=h: e.activation(
                            out=t_rel[:], in_=ps_dt[h][:], func=AF.Relu, bias=bdt_sb[:, c:c + 1]), waits=[])
                        prev_psum_reader[h] = wr2
                        we = r.op("scalar", lambda e: e.activation(out=t_abs[:], in_=t_abs[:], func=AF.Exp, scale=cons[:, C_NEG1:C_NEG1 + 1]), waits=[wa])
                        wl = r.op("scalar", lambda e: e.activation(out=t_abs[:], in_=t_abs[:], func=AF.Ln, bias=cons[:, C_ONE:C_ONE + 1]), waits=[we])
                        prev_vadd = r.op("vector", lambda e, c=c, sl=sl: e.tensor_add(
                            out=delta_sb[c][:, sl], in0=t_rel[:], in1=t_abs[:]), waits=[wl, wr2])
                    row = 1024 * (c // 4) + 128 * (c % 4)
                    du_st = r.dma("sync", du_cat[row:row + 128, :], delta_sb[c][:], waits=[prev_vadd])
                    du_st = r.dma("sync", du_cat[row + 512:row + 640, :], xl_tiles[c][:], waits=[])
                wag_du = r.collective("AllGather", ALU.bypass, GRP4, [du_cat.opt()], [du_ag.opt()], waits=[du_st])
                for nm in ag_store:
                    ag_waits[nm] = r.collective("AllGather", ALU.bypass, GRP8,
                                                [ag_in[nm].opt()], [ag_out[nm].opt()],
                                                waits=[ag_store[nm]])
            return wag_bc, wag_du
        if upto >= "A":
            wag_bc, wag_du = phase_A()

        def phase_B():
            r.wait("sync", r.fence())
            with contextlib.ExitStack() as bs:
                bcx = bs.enter_context
                dA = bcx(nc.sbuf_tensor("dA", [128, D_STATE * TC1], BF16))
                dBu = bcx(nc.sbuf_tensor("dBu", [128, D_STATE * TC1], BF16))
                hh = bcx(nc.sbuf_tensor("hh", [128, D_STATE * TC1], BF16))
                brep = bcx(nc.sbuf_tensor("brep", [128, D_STATE * TC], BF16))
                crep = bcx(nc.sbuf_tensor("crep", [128, D_STATE * TC], BF16))
                dlu8 = bcx(nc.sbuf_tensor("dlu8", [128, 8 * TPC], BF16))
                du_sb = bcx(nc.sbuf_tensor("du", [128, TC], BF16))
                y_sb = bcx(nc.sbuf_tensor("yS", [128, TC], BF16))
                carry = bcx(nc.sbuf_tensor("carry", [128, D_STATE * NDT], BF16))
                alog_sb = bcx(nc.sbuf_tensor("alogS", [128, 128], F32))
                a_sb = bcx(nc.sbuf_tensor("aS", [128, 128], F32))
                dpar_sb = bcx(nc.sbuf_tensor("dparS", [128, NDT], F32))

                dA3 = dA[:].rearrange("p (n t) -> p n t", t=TC1)
                dBu3 = dBu[:].rearrange("p (n t) -> p n t", t=TC1)
                hh3 = hh[:].rearrange("p (n t) -> p n t", t=TC1)
                brep3 = brep[:].rearrange("p (n t) -> p n t", t=TC)
                crep3 = crep[:].rearrange("p (n t) -> p n t", t=TC)

                wal = r.dma("sync", alog_sb[:], alog_d)
                wdp = r.dma("sync", dpar_sb[:], dpar_d)
                wexp0 = r.op("scalar", lambda e: e.activation(out=a_sb[:], in_=alog_sb[:], func=AF.Exp), waits=[wal])
                wA = r.op("vector", lambda e: e.tensor_scalar_mul(out=a_sb[:], in0=a_sb[:], scalar1=-1.0), waits=[wexp0])

                wpad1 = r.op("vector", lambda e: e.memset(dA3[:, :, TC + 1:TC + 2], 0.0), waits=[])
                wpad2 = r.op("vector", lambda e: e.memset(dBu3[:, :, TC + 1:TC + 2], 0.0), waits=[wpad1])
                wpad3 = r.op("vector", lambda e: e.memset(hh3[:, :, TC + 1:TC + 2], 0.0), waits=[wpad2])
                prev_ve = wpad3
                prev_sc = wexp0
                prev_ystore = None
                ds_ = bass.ds
                for ch in range(NCH):
                    tsl = slice(TC * ch, TC * (ch + 1))
                    rnk = ch // 2
                    csl = slice(512 * (ch % 2), 512 * (ch % 2 + 1))
                    wbr = r.dma("sync", brep3,
                                bc_ag[64 * rnk:64 * rnk + 32, csl].unsqueeze(0).broadcast_to([128, 32, TC]),
                                waits=[prev_ve, wag_bc])
                    wcr = r.dma("sync", crep3,
                                bc_ag[64 * rnk + 32:64 * rnk + 64, csl].unsqueeze(0).broadcast_to([128, 32, TC]),
                                waits=[])
                    if ch % 2 == 0:
                        def mk_dma(waits, rnk=rnk):
                            sem = "ld"
                            r.cnt[sem] += 16
                            tgt = r.cnt[sem]
                            waits2 = tuple((s, r.cnt[s]) if s in r.DMA_SEMS else (s, v) for (s, v) in waits)
                            def op(e, sems, waits2=waits2, tgt=tgt):
                                for (s, v) in waits2:
                                    e.wait_ge(sems[s], v)
                                off = dyn_expr(e, ("du", rnk), lambda e: e.scalar_reg_alu(
                                    ALU.add, e.scalar_reg_alu(ALU.mult, jc_of(e), 1024), 4096 * rnk))
                                e.dma_start(
                                    out=dlu8[:].rearrange("p (blk t) -> p blk t", t=TPC),
                                    in_=du_ag[ds_(off, 1024), :].rearrange("(blk p) t -> p blk t", p=128),
                                ).then_inc(sems[sem], 16)
                            r.ops["sync"].append(op)
                            return (sem, tgt)
                        wdu8 = mk_dma([prev_sc, prev_ve, wag_du])
                    for dt in range(NDT):
                        dl_sb = dlu8[:, dt * TPC + 512 * (ch % 2): dt * TPC + 512 * (ch % 2) + TC]
                        u_sb = dlu8[:, (4 + dt) * TPC + 512 * (ch % 2): (4 + dt) * TPC + 512 * (ch % 2) + TC]
                        wld = wdu8
                        wlu = wdu8
                        wexps = None
                        for n in range(D_STATE):
                            wexps = r.op("scalar", lambda e, n=n, dt=dt, dl_sb=dl_sb: e.activation(
                                out=dA[:, n * TC1 + 1: n * TC1 + 1 + TC], in_=dl_sb,
                                func=AF.Exp, scale=a_sb[:, dt * 32 + n: dt * 32 + n + 1]),
                                waits=[wld, prev_ve, wA] if n == 0 else [])
                        wms = r.op("vector", lambda e: e.memset(dA3[:, :, 0:1], 0.0), waits=[prev_ve] if ch + dt else [])
                        wdu = r.op("vector", lambda e, dl_sb=dl_sb, u_sb=u_sb: e.tensor_mul(
                            out=du_sb[:], in0=dl_sb, in1=u_sb), waits=[wld, wlu])
                        wdb = r.op("vector", lambda e: e.tensor_tensor(
                            out=dBu3[:, :, 1:TC + 1], in0=du_sb[:].unsqueeze(1).broadcast_to([128, 32, TC]),
                            in1=brep3, op=ALU.mult), waits=[wbr, wdu])
                        if ch == 0:
                            wcc = r.op("vector", lambda e, dt=dt: e.memset(dBu3[:, :, 0:1], 0.0), waits=[])
                        else:
                            wcc = r.op("vector", lambda e, dt=dt: e.tensor_copy(
                                out=dBu3[:, :, 0:1], in_=carry[:, dt * 32:(dt + 1) * 32].unsqueeze(2)), waits=[])
                        wsc_ = r.op("vector", lambda e: e.tensor_tensor_scan(
                            out=hh[:], data0=dA[:], data1=dBu[:], initial=0.0,
                            op0=ALU.mult, op1=ALU.add), waits=[wexps, wms, wdb, wcc])
                        wco = r.op("vector", lambda e, dt=dt: e.tensor_copy(
                            out=carry[:, dt * 32:(dt + 1) * 32].unsqueeze(2), in_=hh3[:, :, TC:TC + 1]), waits=[wsc_])
                        whc = r.op("vector", lambda e: e.tensor_tensor(
                            out=dA3[:, :, 1:TC + 1], in0=hh3[:, :, 1:TC + 1], in1=crep3, op=ALU.mult), waits=[wcr, wsc_])
                        wtr = whc
                        for half in (16, 8, 4, 2, 1):
                            wtr = r.op("vector", lambda e, half=half: e.tensor_tensor(
                                out=dA3[:, 0:half, 1:TC + 1], in0=dA3[:, 0:half, 1:TC + 1],
                                in1=dA3[:, half:2 * half, 1:TC + 1], op=ALU.add), waits=[wtr])
                        yw = [wdp, wtr] + ([prev_ystore] if prev_ystore else [])
                        wy = r.op("vector", lambda e, dt=dt, u_sb=u_sb: e.scalar_tensor_tensor(
                            out=y_sb[:], in0=u_sb, scalar=dpar_sb[:, dt:dt + 1],
                            in1=dA[:, 1:TC + 1], op0=ALU.mult, op1=ALU.add), waits=yw)
                        yrow = 512 * rnk + 128 * dt
                        prev_ystore = r.dma("sync", y_in[yrow:yrow + 128, csl], y_sb[:], waits=[wy])
                        prev_ve = wy
                        prev_sc = wexps
                wag_y = r.collective("AllGather", ALU.bypass, GRP4, [y_in.opt()], [y_ag.opt()], waits=[prev_ystore])

            return wag_y
        wstack.close()
        if upto >= "B":
            wag_y = phase_B()

        def phase_C():
            r.wait("sync", r.fence())
            with contextlib.ExitStack() as cs:
                cc = cs.enter_context
                x_tiles = [cc(nc.sbuf_tensor(f"cx{c}", [128, TPC], F32)) for c in range(DC)]
                xlbig = cc(nc.sbuf_tensor("cxlbig", [128, DC * TPC], BF16))
                xl_tiles = [xlbig[:, TPC * c:TPC * (c + 1)] for c in range(DC)]
                hid = [cc(nc.sbuf_tensor(f"chid{f}", [128, 512], BF16)) for f in range(NF0)]
                w1s = [cc(nc.sbuf_tensor(f"cw1s{p}", [128, D], BF16)) for p in range(2)]
                w2s = [cc(nc.sbuf_tensor(f"cw2s{p}", [128, D], BF16)) for p in range(2)]
                w3s = [cc(nc.sbuf_tensor(f"cw3s{p}", [128, H0], BF16)) for p in range(2)]
                sg2 = [cc(nc.sbuf_tensor(f"csg{p}", [128, 512], BF16)) for p in range(2)]
                px1b = [ps["p2"], ps["p0"]]
                px2b = [ps["p3"], ps["p1"]]

                wx = [r.dma("sync", x_tiles[c][:], xT[128 * c:128 * (c + 1), :]) for c in range(DC)]
                wy_l = [None] * DC
                for i in range(4):
                    def mk_ydma(i=i):
                        sem = "ld"
                        r.cnt[sem] += 16
                        tgt = r.cnt[sem]
                        def op(e, sems, i=i, tgt=tgt):
                            e.wait_ge(sems["cc"], wag_y[1])
                            off = dyn_expr(e, ("y", i), lambda e: e.scalar_reg_alu(
                                ALU.add, e.scalar_reg_alu(ALU.mult, jc_of(e), 512), 2048 * i))
                            e.dma_start(
                                out=xlbig[:, 4 * TPC * i:4 * TPC * (i + 1)].rearrange("p (c t) -> p c t", t=TPC),
                                in_=y_ag[bass.ds(off, 512), :].rearrange("(c p) t -> p c t", p=128),
                            ).then_inc(sems[sem], 16)
                        r.ops["sync"].append(op)
                        return (sem, tgt)
                    w_ = mk_ydma()
                    for cdx in range(4 * i, 4 * (i + 1)):
                        wy_l[cdx] = w_
                wadd = None
                for c in range(DC):
                    wadd = r.op("vector", lambda e, c=c: e.tensor_tensor(
                        out=x_tiles[c][:], in0=x_tiles[c][:], in1=xl_tiles[c][:], op=ALU.add),
                        waits=[wx[c], wy_l[c]])
                wln = emit_ln(r, sb, ps, x_tiles, xl_tiles, TPC, deps=[wadd])

                rd1 = [None, None]
                rd2 = [None, None]
                rd3 = [None, None]
                sig_hist = [None, None]
                hct_hist = [None, None]
                radd_hist = [None, None]
                prev_sig = None
                prev_hct = None
                prev_out_te = None
                prev_radd = None
                for h in range(2):
                    sl = slice(512 * h, 512 * (h + 1))
                    for f in range(NF0):
                        par = f % 2
                        wd1 = r.dma("sync", w1s[par][:], ag_out["w1f"][128 * f:128 * (f + 1), :],
                                    waits=[w for w in (rd1[par], ag_waits["w1f"]) if w], sem="t0" if par == 0 else "t1")
                        wd2 = r.dma("sync", w2s[par][:], ag_out["w2f"][128 * f:128 * (f + 1), :],
                                    waits=[w for w in (rd2[par], ag_waits["w2f"]) if w], sem="t2" if par == 0 else "t3")
                        w0 = [wd1, wln] + ([sig_hist[par]] if sig_hist[par] else []) + ([radd_hist[par]] if radd_hist[par] else [])
                        for c in range(DC):
                            r.op("tensor", lambda e, c=c, par=par, sl=sl: e.matmul(
                                px1b[par][:], lhsT=w1s[par][:, 128 * c:128 * (c + 1)], rhs=xl_tiles[c][:, sl],
                                start=(c == 0), stop=(c == DC - 1)), waits=w0 if c == 0 else [])
                        rd1[par] = ("te", r.cnt["te"])
                        wte1 = rd1[par]
                        w0 = [wd2] + ([hct_hist[par]] if hct_hist[par] else [])
                        for c in range(DC):
                            r.op("tensor", lambda e, c=c, par=par, sl=sl: e.matmul(
                                px2b[par][:], lhsT=w2s[par][:, 128 * c:128 * (c + 1)], rhs=xl_tiles[c][:, sl],
                                start=(c == 0), stop=(c == DC - 1)), waits=w0 if c == 0 else [])
                        rd2[par] = ("te", r.cnt["te"])
                        wte2 = rd2[par]
                        sgw = [wte1] + ([hct_hist[par]] if hct_hist[par] else [])
                        prev_sig = r.op("scalar", lambda e, par=par: e.activation(
                            out=sg2[par][:], in_=px1b[par][:], func=AF.Sigmoid), waits=sgw)
                        sig_hist[par] = prev_sig
                        hw = [wte2, prev_sig] + ([prev_out_te] if (prev_out_te and h == 1) else [])
                        prev_hct = r.op("vector", lambda e, f=f, par=par: e.tensor_tensor(
                            out=hid[f][:], in0=px2b[par][:], in1=sg2[par][:], op=ALU.mult), waits=hw)
                        hct_hist[par] = prev_hct
                    whid = prev_hct
                    for dci in range(DC):
                        par = dci % 2
                        wd3 = r.dma("sync", w3s[par][:], ag_out["w3f"][128 * dci:128 * (dci + 1), :],
                                    waits=[w for w in (rd3[par], ag_waits["w3f"]) if w],
                                    sem="t4" if par == 0 else "t5")
                        w0 = [wd3, whid] + ([sig_hist[par]] if sig_hist[par] else []) + ([radd_hist[par]] if radd_hist[par] else [])
                        for k in range(NF0):
                            r.op("tensor", lambda e, k=k, par=par: e.matmul(
                                px1b[par][:], lhsT=w3s[par][:, 128 * k:128 * (k + 1)], rhs=hid[k][:],
                                start=(k == 0), stop=(k == NF0 - 1)), waits=w0 if k == 0 else [])
                        rd3[par] = ("te", r.cnt["te"])
                        prev_out_te = rd3[par]
                        prev_radd = r.op("vector", lambda e, dci=dci, sl=sl, par=par: e.tensor_tensor(
                            out=x_tiles[dci][:, sl], in0=x_tiles[dci][:, sl], in1=px1b[par][:], op=ALU.add),
                            waits=[rd3[par]])
                        radd_hist[par] = prev_radd
                        sig_hist[par] = prev_radd
                x2st = None
                for c in range(DC):
                    x2st = r.dma("sync", x2_loc[128 * c:128 * (c + 1), :], x_tiles[c][:],
                                 waits=[prev_radd] if c == 0 else [])
            return x2st
        if upto >= "C":
            x2st = phase_C()

        def phase_D():
            r.wait("sync", r.fence())
            with contextlib.ExitStack() as dsx:
                dcx = dsx.enter_context
                xl_tiles = [dcx(nc.sbuf_tensor(f"dxl{c}", [128, TPC], BF16)) for c in range(DC)]
                wsl = [dcx(nc.sbuf_tensor(f"dwsl{p}", [128, D], BF16)) for p in range(2)]
                bias_sb = {nm: dcx(nc.sbuf_tensor(nm + "S", [128, DC], F32)) for nm in ("bq", "bk")}
                pqb = [ps["p2"], ps["p0"]]

                with contextlib.ExitStack() as dxs:
                    x_tiles = [dxs.enter_context(nc.sbuf_tensor(f"dx{c}", [128, TPC], F32)) for c in range(DC)]
                    wx = [r.dma("sync", x_tiles[c][:], x2_loc[128 * c:128 * (c + 1), :], waits=[x2st]) for c in range(DC)]
                    wln = emit_ln(r, sb, ps, x_tiles, xl_tiles, TPC, deps=[wx[DC - 1], waux])

                av_res = [dcx(nc.sbuf_tensor(f"dav{c}", [128, D], BF16)) for c in range(DC)]
                pt = dcx(nc.sbuf_tensor("dpt", [128, TPC], BF16))
                rt = dcx(nc.sbuf_tensor("drt", [128, TPC], BF16))
                rt2 = dcx(nc.sbuf_tensor("drt2", [128, TPC], BF16))
                vt_t = dcx(nc.sbuf_tensor("dvt", [128, D], BF16))
                bvr = dcx(nc.sbuf_tensor("dbvr", [128, D], BF16))
                bvrow_sb = dcx(nc.sbuf_tensor("dbvrow", [1, D], F32))
                tabs = {nm: dcx(nc.sbuf_tensor(nm + "S", [128, TPC], BF16))
                        for nm in ("cosq", "sinq", "cosk", "sink")}
                for nm, d_ in (("cosq", rope_d["cosq"]), ("sinq", rope_d["sinq"]),
                               ("cosk", rope_d["cosk"]), ("sink", rope_d["sink"])):
                    r.dma("sync", tabs[nm][:], d_, waits=[wln])
                r.dma("sync", bias_sb["bq"][:], bq_d)
                r.dma("sync", bias_sb["bk"][:], bk_d)
                wbv = r.dma("sync", bvrow_sb[:], bvrow_d)
                wavl = [r.dma("sync", av_res[c][:], ag_out["avf"][128 * c:128 * (c + 1), :],
                              waits=[ag_waits["avf"]] if c == 0 else []) for c in range(DC)]
                wtabs2 = ("ld", r.cnt["ld"])
                wbvb = None
                for j in range(4):
                    wmmb = r.op("tensor", lambda e, j=j: e.matmul(
                        ps["p3"][:], lhsT=sb["ones1"][:], rhs=bvrow_sb[0:1, 512 * j:512 * (j + 1)],
                        start=True, stop=True), waits=[wbv, waux] + ([wbvb] if wbvb else []))
                    wbvb = r.op("vector", lambda e, j=j: e.tensor_copy(
                        out=bvr[:, 512 * j:512 * (j + 1)], in_=ps["p3"][:]), waits=[wmmb])
                wln = wtabs2

                rdw = [None, None]
                ev_hist = [wbvb, None]
                prev_ev = wbvb
                prev_store = None
                projs = [("q", "aqf", "bq", q_loc, "cosq", "sinq"),
                         ("k", "akf", "bk", k_in, "cosk", "sink")]
                fidx = 0
                gidx = 0
                for (pname, wnm, bnm, odst, cosn, sinn) in projs:
                    for f in range(DC):
                        par = fidx % 2
                        wd = r.dma("sync", wsl[par][:], ag_out[wnm][128 * f:128 * (f + 1), :],
                                   waits=[w for w in (rdw[par], ag_waits[wnm]) if w],
                                   sem="t0" if par == 0 else "t1")
                        for h in range(2):
                            bpar = gidx % 2
                            gidx += 1
                            sl = slice(512 * h, 512 * (h + 1))
                            w0 = [wd, wtabs2] + ([ev_hist[bpar]] if ev_hist[bpar] else [])
                            for c in range(DC):
                                r.op("tensor", lambda e, c=c, par=par, sl=sl, bpar=bpar: e.matmul(
                                    pqb[bpar][:], lhsT=wsl[par][:, 128 * c:128 * (c + 1)], rhs=xl_tiles[c][:, sl],
                                    start=(c == 0), stop=(c == DC - 1)), waits=w0 if c == 0 else [])
                            rdw[par] = ("te", r.cnt["te"])
                            evw = [rdw[par]] + ([prev_store] if prev_store else [])
                            prev_ev = r.op("scalar", lambda e, sl=sl, bnm=bnm, f=f, bpar=bpar: e.activation(
                                out=pt[:, sl], in_=pqb[bpar][:], func=AF.Identity,
                                bias=bias_sb[bnm][:, f:f + 1]), waits=evw)
                            ev_hist[bpar] = prev_ev
                        wc1 = r.op("vector", lambda e: e.tensor_copy(out=rt[0:64, :], in_=pt[64:128, :]), waits=[prev_ev])
                        wc2 = r.op("vector", lambda e: e.tensor_copy(out=rt[64:128, :], in_=pt[0:64, :]), waits=[prev_ev])
                        wm1 = r.op("vector", lambda e, cosn=cosn: e.tensor_tensor(
                            out=pt[:], in0=pt[:], in1=tabs[cosn][:], op=ALU.mult), waits=[wc1, wc2])
                        wm2 = r.op("vector", lambda e, sinn=sinn: e.tensor_tensor(
                            out=rt2[:], in0=rt[:], in1=tabs[sinn][:], op=ALU.mult), waits=[wm1])
                        wad = r.op("vector", lambda e: e.tensor_tensor(
                            out=pt[:], in0=pt[:], in1=rt2[:], op=ALU.add), waits=[wm2])
                        prev_ev = wad
                        prev_store = r.dma("sync", odst[128 * f:128 * (f + 1), :], pt[:], waits=[prev_ev])
                        fidx += 1
                wag_k = r.collective("AllGather", ALU.bypass, GRP4, [k_in.opt()], [k_ag.opt()], waits=[prev_store])
                v_view = v_in.rearrange("(h p) (kt i) -> p h kt i", p=128, i=128)
                prev_vst = None
                for tc in range(8):
                    for j in range(4):
                        bpar = gidx % 2
                        gidx += 1
                        jsl = slice(512 * j, 512 * (j + 1))
                        w0 = [wavl[DC - 1]] + ([ev_hist[bpar]] if ev_hist[bpar] else [])
                        if tc == 0 and j == 0 and prev_ev:
                            w0.append(prev_ev)
                        for c in range(DC):
                            r.op("tensor", lambda e, c=c, tc=tc, jsl=jsl, bpar=bpar: e.matmul(
                                pqb[bpar][:], lhsT=xl_tiles[c][:, 128 * tc:128 * (tc + 1)], rhs=av_res[c][:, jsl],
                                start=(c == 0), stop=(c == DC - 1)), waits=w0 if c == 0 else [])
                        wte_ = ("te", r.cnt["te"])
                        vw = [wte_, wbvb] + ([prev_vst] if prev_vst else [])
                        prev_ev = r.op("vector", lambda e, jsl=jsl, bpar=bpar: e.tensor_tensor(
                            out=vt_t[:, jsl], in0=pqb[bpar][:], in1=bvr[:, jsl], op=ALU.add), waits=vw)
                        ev_hist[bpar] = prev_ev
                    wvst = r.dma("sync", v_view[:, :, tc, :],
                                 vt_t[:].rearrange("p (h i) -> p h i", i=128), waits=[prev_ev])
                    prev_vst = wvst
                wag_v = r.collective("AllGather", ALU.bypass, GRP4, [v_in.opt()], [v_ag.opt()], waits=[wvst])
            return wag_k, wag_v
        if upto >= "D":
            wag_k, wag_v = phase_D()

        def phase_E():
            r.wait("sync", r.fence())
            with contextlib.ExitStack() as es:
                ecx = es.enter_context
                mg = [ecx(nc.sbuf_tensor(f"emg{c}", [128, TPC], BF16)) for c in range(NH)]
                kvh = [ecx(nc.sbuf_tensor(f"ekvh{p}", [128, SEQ], BF16)) for p in range(2)]
                s2 = [ecx(nc.sbuf_tensor(f"esS{p}", [128, 512], BF16)) for p in range(2)]
                binv = ecx(nc.sbuf_tensor("ebinv", [128, 512], BF16))
                srow = ecx(nc.sbuf_tensor("esrow", [1, 512], F32))
                bo_sb = ecx(nc.sbuf_tensor("eboS", [128, DC], F32))
                eqs = contextlib.ExitStack()
                qf = [eqs.enter_context(nc.sbuf_tensor(f"eqf{c}", [128, TPC], BF16)) for c in range(DC)]
                m8 = eqs.enter_context(nc.sbuf_tensor("em8", [128, NKT * TPC], mybir.dt.int8))
                vvh = [eqs.enter_context(nc.sbuf_tensor(f"evvh{p}", [128, SEQ], BF16)) for p in range(2)]
                ps_s = [ps["p0"], ps["p1"]]
                ps_ctx = ps["p2"]
                ps_b = ps["p3"]
                ps_sum = ps["row"]

                wq = [r.dma("sync", qf[c][:], q_loc[128 * c:128 * (c + 1), :],
                            waits=[wag_k] if c == 0 else []) for c in range(DC)]
                wm8 = r.dma("sync", m8[:].rearrange("p (kt t) -> p kt t", t=TPC),
                            mask_d.rearrange("kt p t -> p kt t"))
                wbo = r.dma("sync", bo_sb[:], bo_d)
                m83 = m8[:].rearrange("p (kt t) -> p kt t", t=TPC)
                wall = ("ld", r.cnt["ld"])

                rdk = [None, None]
                prev_exp = None
                prev_pv = None
                wev_hist = [None, None]
                wexp_hist = [None, None]
                wpv_hist = [None, None]
                prev_ctxev = None
                prev_sumev = None
                prev_bev = None
                for h in range(NH):
                    par = h % 2
                    psem = "t0" if par == 0 else "t1"
                    kw = [w for w in (rdk[par], wag_k) if w]
                    wkh = None
                    for rr in range(4):
                        wkh = r.dma("sync", kvh[par][:, 1024 * rr:1024 * (rr + 1)],
                                    k_ag[2048 * rr + 128 * h:2048 * rr + 128 * (h + 1), :],
                                    waits=kw if rr == 0 else [], sem=psem)
                    wvh = None
                    for rr in range(4):
                        wvh = r.dma("sync", vvh[par][:, 1024 * rr:1024 * (rr + 1)],
                                    v_ag[2048 * rr + 128 * h:2048 * rr + 128 * (h + 1), :],
                                    waits=[wag_v] if rr == 0 else [], sem=psem)
                    for qc in range(2):
                        sl = slice(512 * qc, 512 * (qc + 1))
                        wev_hist = [None, None]
                        wexp_hist = [None, None]
                        wpv_hist = [None, None]

                        qk_te = {}

                        def emit_qk(kt):
                            pp = kt % 2
                            w0 = [wkh, wall]
                            if wev_hist[pp]:
                                w0.append(wev_hist[pp])
                            qk_te[kt] = r.op("tensor", lambda e, h=h, kt=kt, par=par, sl=sl, pp=pp: e.matmul(
                                ps_s[pp][:], lhsT=kvh[par][:, 128 * kt:128 * (kt + 1)], rhs=qf[h][:, sl],
                                start=True, stop=True), waits=w0)

                        emit_qk(0)
                        emit_qk(1)
                        for kt in range(NKT):
                            pp = kt % 2
                            wmm = qk_te[kt]
                            evw = [wmm]
                            if wexp_hist[pp]:
                                evw.append(wexp_hist[pp])
                            if wpv_hist[pp]:
                                evw.append(wpv_hist[pp])
                            wev = r.op("vector", lambda e, kt=kt, sl=sl, pp=pp: e.scalar_tensor_tensor(
                                out=s2[pp][:], in0=m83[:, kt, sl], scalar=MASK_NEG, in1=ps_s[pp][:],
                                op0=ALU.mult, op1=ALU.add), waits=evw)
                            wev_hist[pp] = wev
                            wexp = r.op("scalar", lambda e, pp=pp: e.activation(
                                out=s2[pp][:], in_=s2[pp][:], func=AF.Exp,
                                bias=cons[:, C_EXPB:C_EXPB + 1]), waits=[wev])
                            wexp_hist[pp] = wexp
                            pvw = [wexp, wvh]
                            if kt == 0 and prev_ctxev:
                                pvw.append(prev_ctxev)
                            r.op("tensor", lambda e, kt=kt, par=par, pp=pp: e.matmul(
                                ps_ctx[:], lhsT=vvh[par][:, 128 * kt:128 * (kt + 1)], rhs=s2[pp][:],
                                start=(kt == 0), stop=(kt == NKT - 1)), waits=pvw)
                            smw = []
                            if kt == 0 and prev_sumev:
                                smw.append(prev_sumev)
                            prev_pv = r.op("tensor", lambda e, kt=kt, pp=pp: e.matmul(
                                ps_sum[:], lhsT=sb["ones_bf"][:], rhs=s2[pp][:],
                                start=(kt == 0), stop=(kt == NKT - 1)), waits=smw)
                            wpv_hist[pp] = prev_pv
                            prev_exp = wexp
                            if kt + 2 < NKT:
                                emit_qk(kt + 2)
                        wte = ("te", r.cnt["te"])
                        prev_sumev = r.op("scalar", lambda e: e.copy(out=srow[:], in_=ps_sum[:]), waits=[wte])
                        wrc = r.op("vector", lambda e: e.reciprocal(out=srow[:], in_=srow[:]), waits=[prev_sumev])
                        wbc2 = r.op("tensor", lambda e: e.matmul(
                            ps_b[:], lhsT=sb["ones1"][:], rhs=srow[:], start=True, stop=True),
                            waits=[wrc, waux] + ([prev_bev] if prev_bev else []))
                        prev_bev = r.op("vector", lambda e: e.tensor_copy(out=binv[:], in_=ps_b[:]), waits=[wbc2])
                        prev_ctxev = r.op("vector", lambda e, h=h, sl=sl: e.tensor_tensor(
                            out=mg[h][:, sl], in0=ps_ctx[:], in1=binv[:], op=ALU.mult),
                            waits=[prev_bev, wte])
                    rdk[par] = ("te", r.cnt["te"])
                eqs.close()
                x_tiles = [ecx(nc.sbuf_tensor(f"ex{c}", [128, TPC], F32)) for c in range(DC)]
                wx = [r.dma("sync", x_tiles[c][:], x2_loc[128 * c:128 * (c + 1), :],
                            waits=[x2st, prev_ctxev] if c == 0 else []) for c in range(DC)]
                rdw = [prev_ctxev, prev_ctxev]
                prev_radd = None
                for dci in range(DC):
                    par = dci % 2
                    wd = r.dma("sync", kvh[par][:, 0:D], ag_out["aof"][128 * dci:128 * (dci + 1), :],
                               waits=[w for w in (rdw[par], ag_waits["aof"]) if w],
                               sem="t2" if par == 0 else "t3")
                    for hf in range(2):
                        sl = slice(512 * hf, 512 * (hf + 1))
                        w0 = [wd, prev_ctxev, wbo]
                        if prev_radd:
                            w0.append(prev_radd)
                        for k in range(DC):
                            r.op("tensor", lambda e, k=k, par=par, sl=sl: e.matmul(
                                ps_b[:], lhsT=kvh[par][:, 0:D][:, 128 * k:128 * (k + 1)], rhs=mg[k][:, sl],
                                start=(k == 0), stop=(k == DC - 1)), waits=w0 if k == 0 else [])
                        rdw[par] = ("te", r.cnt["te"])
                        prev_radd = r.op("vector", lambda e, dci=dci, sl=sl: e.scalar_tensor_tensor(
                            out=x_tiles[dci][:, sl], in0=ps_b[:], scalar=bo_sb[:, dci:dci + 1],
                            in1=x_tiles[dci][:, sl], op0=ALU.add, op1=ALU.add),
                            waits=[rdw[par], wx[dci]])
                x3st = None
                for c in range(DC):
                    x3st = r.dma("sync", x3_loc[128 * c:128 * (c + 1), :], x_tiles[c][:],
                                 waits=[prev_radd] if c == 0 else [])
            return x3st
        if upto >= "E":
            x3st = phase_E()

        def phase_F():
            r.wait("sync", r.fence())
            with contextlib.ExitStack() as fs:
                fcx = fs.enter_context
                x_tiles = [fcx(nc.sbuf_tensor(f"fx{c}", [128, TPC], F32)) for c in range(DC)]
                xl_tiles = [fcx(nc.sbuf_tensor(f"fxl{c}", [128, TPC], BF16)) for c in range(DC)]
                hid = [fcx(nc.sbuf_tensor(f"fhid{f}", [128, 512], BF16)) for f in range(NF1)]
                w1s = [fcx(nc.sbuf_tensor(f"fw1s{p}", [128, D], BF16)) for p in range(2)]
                w2s = fcx(nc.sbuf_tensor("fw2s", [128, H1], BF16))
                b1_sb = fcx(nc.sbuf_tensor("fb1S", [128, NF1], F32))
                b2_sb = fcx(nc.sbuf_tensor("fb2S", [128, DC], F32))
                pxb = [ps["p2"], ps["p0"]]

                wx = [r.dma("sync", x_tiles[c][:], x3_loc[128 * c:128 * (c + 1), :], waits=[x3st]) for c in range(DC)]
                wb1 = r.dma("sync", b1_sb[:], b1_d)
                wb2 = r.dma("sync", b2_sb[:], b2_d)
                wln = emit_ln(r, sb, ps, x_tiles, xl_tiles, TPC, deps=[wx[DC - 1], wb1, wb2])

                rd1 = [None, None]
                rd2 = None
                gel_hist = [None, None]
                radd_hist = [None, None]
                prev_gel = None
                prev_out_te = None
                prev_radd = None
                for h in range(2):
                    sl = slice(512 * h, 512 * (h + 1))
                    for f in range(NF1):
                        par = f % 2
                        wd1 = r.dma("sync", w1s[par][:], ag_out["f1f"][128 * f:128 * (f + 1), :],
                                    waits=[w for w in (rd1[par], ag_waits["f1f"]) if w],
                                    sem="t0" if par == 0 else "t1")
                        w0 = [wd1, wln] + ([gel_hist[par]] if gel_hist[par] else []) + ([radd_hist[par]] if radd_hist[par] else [])
                        for c in range(DC):
                            r.op("tensor", lambda e, c=c, par=par, sl=sl: e.matmul(
                                pxb[par][:], lhsT=w1s[par][:, 128 * c:128 * (c + 1)], rhs=xl_tiles[c][:, sl],
                                start=(c == 0), stop=(c == DC - 1)), waits=w0 if c == 0 else [])
                        rd1[par] = ("te", r.cnt["te"])
                        gw = [rd1[par]]
                        if h == 1 and prev_out_te:
                            gw.append(prev_out_te)
                        prev_gel = r.op("scalar", lambda e, f=f, par=par: e.activation(
                            out=hid[f][:], in_=pxb[par][:], func=AF.Gelu, bias=b1_sb[:, f:f + 1]), waits=gw)
                        gel_hist[par] = prev_gel
                    whid = prev_gel
                    for dci in range(DC):
                        par = dci % 2
                        wd2 = r.dma("sync", w2s[:], ag_out["f2f"][128 * dci:128 * (dci + 1), :],
                                    waits=[w for w in (rd2, ag_waits["f2f"]) if w], sem="t2")
                        w0 = [wd2, whid] + ([gel_hist[par]] if gel_hist[par] else []) + ([radd_hist[par]] if radd_hist[par] else [])
                        for k in range(NF1):
                            r.op("tensor", lambda e, k=k, par=par: e.matmul(
                                pxb[par][:], lhsT=w2s[:, 128 * k:128 * (k + 1)], rhs=hid[k][:],
                                start=(k == 0), stop=(k == NF1 - 1)), waits=w0 if k == 0 else [])
                        rd2 = ("te", r.cnt["te"])
                        prev_out_te = rd2
                        prev_radd = r.op("vector", lambda e, dci=dci, sl=sl, par=par: e.scalar_tensor_tensor(
                            out=x_tiles[dci][:, sl], in0=pxb[par][:], scalar=b2_sb[:, dci:dci + 1],
                            in1=x_tiles[dci][:, sl], op0=ALU.add, op1=ALU.add),
                            waits=[rd2])
                        radd_hist[par] = prev_radd
                        gel_hist[par] = prev_radd
                for c in range(DC):
                    r.dma("sync", o_out[128 * c:128 * (c + 1), :], x_tiles[c][:],
                          waits=[prev_radd] if c == 0 else [])
                r.wait("sync", [("ld", r.cnt["ld"])])
            return None
        if upto >= "F":
            phase_F()

        if upto < "F":
            r.wait("sync", r.fence())
        with nc.Block() as block:
            r.replay(block, sems)
    return nc


def host_prep_fused(inputs):
    bf = BF
    aux = _aux()
    wbcT = np.ascontiguousarray(inputs["L0_attn_W_bc"].T).astype(np.float32)
    wdtT = np.ascontiguousarray(inputs["L0_attn_W_dt"].T).astype(np.float32)
    bdt = np.ascontiguousarray(inputs["L0_attn_b_dt"].reshape(DC, 128).T).astype(np.float32)
    alog = np.asarray(inputs["L0_attn_A_log"], dtype=np.float32)
    dpar = np.asarray(inputs["L0_attn_D_param"], dtype=np.float32)
    flats = {
        "w1f": slabify(np.ascontiguousarray(inputs["L0_ffn_w1"].T).astype(np.float32), NF0).reshape(H0, D),
        "w2f": slabify(np.ascontiguousarray(inputs["L0_ffn_w2"].T).astype(np.float32), NF0).reshape(H0, D),
        "w3f": slabify(np.ascontiguousarray(inputs["L0_ffn_w3"].T).astype(np.float32), DC).reshape(D, H0),
        "f1f": slabify(np.ascontiguousarray(inputs["L1_ffn_w1"].T).astype(np.float32), NF1).reshape(H1, D),
        "f2f": slabify(np.ascontiguousarray(inputs["L1_ffn_w2"].T).astype(np.float32), DC).reshape(D, H1),
        "aqf": slabify(np.ascontiguousarray(inputs["L1_attn_Wq"].T).astype(np.float32), DC).reshape(D, D),
        "akf": slabify(np.ascontiguousarray(inputs["L1_attn_Wk"].T).astype(np.float32), DC).reshape(D, D),
        "avf": np.ascontiguousarray(inputs["L1_attn_Wv"].T).astype(np.float32),
        "aof": slabify(np.ascontiguousarray(inputs["L1_attn_Wo"].T).astype(np.float32), DC).reshape(D, D),
    }
    invcnt = np.zeros((1, 8), dtype=np.float32)
    for i, c in enumerate(QCOUNT):
        invcnt[0, i] = 1.0 / c
    sel = np.zeros((1, 8), dtype=np.float32)
    sel[0, 0:3] = 1.0
    bq = np.ascontiguousarray(inputs["L1_attn_bq"].reshape(DC, 128).T).astype(np.float32)
    bk = np.ascontiguousarray(inputs["L1_attn_bk"].reshape(DC, 128).T).astype(np.float32)
    bvrow = np.ascontiguousarray(inputs["L1_attn_bv"].reshape(1, D)).astype(np.float32)
    bo = np.ascontiguousarray(inputs["L1_attn_bo"].reshape(DC, 128).T).astype(np.float32)
    b1 = np.ascontiguousarray(inputs["L1_ffn_b1"].reshape(NF1, 128).T).astype(np.float32)
    b2 = np.ascontiguousarray(inputs["L1_ffn_b2"].reshape(DC, 128).T).astype(np.float32)
    inv_freq = 1.0 / (10000.0 ** (np.arange(0, DH, 2, dtype=np.float64) / DH))
    x = np.asarray(inputs["L0_attn_x"], dtype=np.float32)
    xt = x.reshape(B_SZ * SEQ, D)
    maps = []
    for c in range(NCORE):
        t0 = (TPC * c) % SEQ
        t = np.arange(t0, t0 + TPC, dtype=np.float64)
        fr = t[:, None] * inv_freq[None, :]
        cosT = np.cos(fr).T
        sinT = np.sin(fr).T
        sc = 1.0 / np.sqrt(DH)
        kpos = np.arange(SEQ).reshape(NKT, 128, 1)
        qpos = (t0 + np.arange(TPC)).reshape(1, 1, TPC)
        mask8 = (kpos > qpos).astype(np.int8)
        j = c % 4
        rows = slice(512 * j, 512 * (j + 1))
        al = np.ascontiguousarray(alog[rows].reshape(4, 128, 32).transpose(1, 0, 2).reshape(128, 128))
        dp = np.ascontiguousarray(dpar[rows.start:rows.stop].reshape(4, 128).T)
        m = dict(
            xT=np.ascontiguousarray(xt[TPC * c:TPC * (c + 1), :].T),
            wbcT=wbcT, wdtT=wdtT, bdt=bdt, alog=al, dpar=dp,
            invcnt=invcnt, sel=sel, bq=bq, bk=bk, bvrow=bvrow, bo=bo, b1=b1, b2=b2,
            cosq=(np.concatenate([cosT, cosT], 0) * sc).astype(bf),
            sinq=(np.concatenate([-sinT, sinT], 0) * sc).astype(bf),
            cosk=np.concatenate([cosT, cosT], 0).astype(bf),
            sink=np.concatenate([-sinT, sinT], 0).astype(bf),
            mask8=mask8, ones_bf=aux["ones_bf"], ones1=aux["ones1"], consts=aux["consts"],
        )
        for nm, arr in flats.items():
            rr = arr.shape[0] // NCORE
            m[nm] = np.ascontiguousarray(arr[rr * c:rr * (c + 1)])
        maps.append(m)
    return maps


def kernel_fused(**inputs):
    inputs = {k: np.asarray(v) for k, v in inputs.items()}
    maps = host_prep_fused(inputs)
    res = _run("FUSED", build_fused, maps)
    out = np.empty((B_SZ * SEQ, D), dtype=np.float32)
    for c in range(NCORE):
        out[TPC * c:TPC * (c + 1), :] = res[c]["o_out"].T
    return out.reshape(B_SZ, SEQ, D)


_CACHE = {}


def _run(name, builder, in_maps):
    if name not in _CACHE:
        _CACHE[name] = builder()
    res = run_bass_kernel_spmd(_CACHE[name], in_maps, core_ids=list(range(NCORE)))
    return res.results


def _aux():
    return dict(
        ones_bf=np.ones((128, 1), dtype=BF),
        ones1=np.ones((1, 128), dtype=np.float32),
        consts=np.tile(np.asarray([CONST_VALS], dtype=np.float32), (128, 1)),
    )


def host_prep_A(inputs, xs_T):
    wbcT = np.ascontiguousarray(inputs["L0_attn_W_bc"].T).astype(np.float32)
    wdtT = np.ascontiguousarray(inputs["L0_attn_W_dt"].T).astype(np.float32)
    bdt = np.ascontiguousarray(inputs["L0_attn_b_dt"].reshape(DC, 128).T).astype(np.float32)
    aux = _aux()
    return [dict(xT=xs_T[c], wbcT=wbcT, wdtT=wdtT, bdt=bdt, ones_bf=aux["ones_bf"],
                 ones1=aux["ones1"], consts=aux["consts"]) for c in range(NCORE)]


def host_prep_D(x2s, qj, inputs):
    s = dict(qaq=slabify(qj["aq"], DC), qak=slabify(qj["ak"], DC), qav=slabify(qj["av"], DC))
    bq = np.ascontiguousarray(inputs["L1_attn_bq"].reshape(DC, 128).T).astype(np.float32)
    bk = np.ascontiguousarray(inputs["L1_attn_bk"].reshape(DC, 128).T).astype(np.float32)
    bv = np.ascontiguousarray(inputs["L1_attn_bv"].reshape(DC, 128).T).astype(np.float32)
    inv_freq = 1.0 / (10000.0 ** (np.arange(0, DH, 2, dtype=np.float64) / DH))
    aux = _aux()
    maps = []
    for c in range(NCORE):
        t0 = (TPC * c) % SEQ
        t = np.arange(t0, t0 + TPC, dtype=np.float64)
        fr = t[:, None] * inv_freq[None, :]
        cosT = np.cos(fr).T
        sinT = np.sin(fr).T
        sc = 1.0 / np.sqrt(DH)
        cosq = np.concatenate([cosT, cosT], 0) * sc
        sinq = np.concatenate([-sinT, sinT], 0) * sc
        cosk = np.concatenate([cosT, cosT], 0)
        sink = np.concatenate([-sinT, sinT], 0)
        maps.append(dict(
            xT=x2s[c], qaq=s["qaq"], qak=s["qak"], qav=s["qav"],
            bq=bq, bk=bk, bv=bv,
            cosq=cosq.astype(BF), sinq=sinq.astype(BF),
            cosk=cosk.astype(BF), sink=sink.astype(BF),
            ones_bf=aux["ones_bf"], ones1=aux["ones1"], consts=aux["consts"]))
    return maps


def host_prep_E(x2s, resD, qj, inputs):
    qao = slabify(qj["ao"], DC)
    bo = np.ascontiguousarray(inputs["L1_attn_bo"].reshape(DC, 128).T).astype(np.float32)
    aux = _aux()
    kf_sl = {}
    vt_sl = {}
    for b in range(B_SZ):
        kfull = np.concatenate([resD[4 * b + j]["o_k"] for j in range(4)], axis=1)
        kf_sl[b] = np.ascontiguousarray(kfull.reshape(NH, 128, SEQ))
        vfull = np.concatenate([resD[4 * b + j]["o_v"] for j in range(4)], axis=1)
        vtm = np.ascontiguousarray(vfull.T)
        vt_sl[b] = np.ascontiguousarray(
            vtm.reshape(NKT, 128, NH, 128).transpose(2, 1, 0, 3).reshape(NH, 128, SEQ))
    maps = []
    for c in range(NCORE):
        b = c // 4
        t0 = (TPC * c) % SEQ
        kpos = np.arange(SEQ).reshape(NKT, 128, 1)
        qpos = (t0 + np.arange(TPC)).reshape(1, 1, TPC)
        mask8 = (kpos > qpos).astype(np.int8)
        maps.append(dict(
            xT=x2s[c], qf=resD[c]["o_q"], kf=kf_sl[b], vt=vt_sl[b], mask8=mask8,
            qao=qao, bo=bo, ones_bf=aux["ones_bf"], ones1=aux["ones1"], consts=aux["consts"]))
    return maps


def host_prep_F(x3s, qj, inputs):
    s1 = slabify(qj["f1s"], NF1)
    s2 = slabify(qj["f2s"], DC)
    b1 = np.ascontiguousarray(inputs["L1_ffn_b1"].reshape(NF1, 128).T).astype(np.float32)
    b2 = np.ascontiguousarray(inputs["L1_ffn_b2"].reshape(DC, 128).T).astype(np.float32)
    aux = _aux()
    return [dict(xT=x3s[c], qf1=s1, qf2=s2, b1=b1, b2=b2, ones_bf=aux["ones_bf"],
                 ones1=aux["ones1"], consts=aux["consts"]) for c in range(NCORE)]


def kernel(**inputs):
    return kernel_fused(**inputs)


# version 06
# speedup vs baseline: 1.1776x; 1.0034x over previous
import contextlib
import numpy as np
import ml_dtypes
import concourse.bass as bass
import concourse.mybir as mybir
from concourse.bass_utils import run_bass_kernel_spmd
from concourse.expressions import RuntimeValue

F32 = mybir.dt.float32
BF16 = mybir.dt.bfloat16
AF = mybir.ActivationFunctionType
ALU = mybir.AluOpType
BF = ml_dtypes.bfloat16

NCORE = 8
B_SZ = 2
SEQ = 4096
D = 2048
DC = D // 128
TPC = 1024
H0 = 5632
H1 = 8192
NH = 16
DH = 128
DT_RANK = 64
D_STATE = 32
LN_EPS = 1e-5
EXP_BIAS = -20.0
MASK_NEG = -30000.0

C_LNEPS, C_ONE, C_NEG1, C_NEGH, C_EXPB, C_HALF, C_EPS8, C_ZERO = range(8)
CONST_VALS = [LN_EPS, 1.0, -1.0, -0.5, EXP_BIAS, 0.5, 1e-8, 0.0]

ENGS = ("sync", "gpsimd", "scalar", "vector", "tensor")


class Rec:

    def __init__(self):
        self.ops = {k: [] for k in ENGS}
        self.cnt = {"ld": 0, "lde": 0, "ldo": 0, "g": 0, "sc": 0, "ve": 0, "te": 0, "cc": 0,
                    "t0": 0, "t1": 0, "t2": 0, "t3": 0, "t4": 0, "t5": 0}

    DMA_SEMS = ("ld", "lde", "ldo", "g", "cc", "t0", "t1", "t2", "t3", "t4", "t5")

    def _emit(self, eng, waits, fn, inc_sem=None, inc_by=0):
        waits = tuple(
            (s, self.cnt[s] - (inc_by if s == inc_sem else 0))
            if s in self.DMA_SEMS else (s, v) for (s, v) in waits
        )

        def op(e, sems):
            for (s, v) in waits:
                e.wait_ge(sems[s], v)
            ins = fn(e)
            if inc_sem is not None and ins is not None:
                ins.then_inc(sems[inc_sem], inc_by)

        self.ops[eng].append(op)

    def dma(self, q, out, in_, waits=(), sem=None):
        if sem is None:
            sem = {"sync": "ld", "gpsimd": "g"}[q]
        self.cnt[sem] += 16
        self._emit(q, waits, lambda e: e.dma_start(out=out, in_=in_), sem, 16)
        return (sem, self.cnt[sem])

    def op(self, eng, fn, waits=(), inc=True):
        sem = {"scalar": "sc", "vector": "ve", "tensor": "te"}[eng]
        if inc:
            self.cnt[sem] += 1
            self._emit(eng, waits, fn, sem, 1)
        else:
            self._emit(eng, waits, fn)
        return (sem, self.cnt[sem])

    def wait(self, eng, waits):
        self._emit(eng, waits, lambda e: None)

    def fence(self):
        return [(s, self.cnt[s]) for s in self.cnt]

    def collective(self, kind, op, groups, ins, outs, waits=()):
        waits = list(waits) + ([("cc", self.cnt["cc"])] if self.cnt["cc"] else [])
        self.cnt["cc"] += 1
        self._emit("gpsimd", waits, lambda e: e.collective_compute(
            kind, op, replica_groups=groups, ins=ins, outs=outs), "cc", 1)
        return ("cc", self.cnt["cc"])

    def replay(self, block, sems):
        for eng in ENGS:
            if not self.ops[eng]:
                continue
            engops = list(self.ops[eng])

            def body(e, engops=engops):
                for f in engops:
                    f(e, sems)

            getattr(block, eng)(body)


def _mk_sems(nc, stack):
    return {n: stack.enter_context(nc.semaphore(n))
            for n in ("ld", "lde", "ldo", "g", "sc", "ve", "te", "cc",
                      "t0", "t1", "t2", "t3", "t4", "t5")}



def emit_ln(r, sb, ps, x_tiles, xl_tiles, n_tok, deps=()):
    stats = sb["stats"]
    sums = stats[:, 0:n_tok]
    sumsq = stats[:, n_tok:2 * n_tok]
    scr = stats[:, 2 * n_tok:3 * n_tok]
    cons = sb["consts"]
    nhalf = n_tok // 512

    casts = []
    for c in range(DC):
        casts.append(r.op("vector", lambda e, c=c: e.tensor_copy(out=xl_tiles[c][:], in_=x_tiles[c][:]), waits=deps))

    prev_copy_s = None
    prev_copy_q = None
    for h in range(nhalf):
        sl = slice(512 * h, 512 * (h + 1))
        prev_mmq = None
        for c in range(DC):
            sqw = [casts[c]] + ([prev_mmq] if prev_mmq else [])
            wsq = r.op("scalar", lambda e, c=c, sl=sl: e.activation(
                out=sb["sqt"][:], in_=xl_tiles[c][:, sl], func=AF.Square), waits=sqw)
            smw = [casts[c]]
            if c == 0 and prev_copy_s is not None:
                smw.append(prev_copy_s)
            r.op("tensor", lambda e, c=c, sl=sl: e.matmul(
                ps["sts"][:], lhsT=sb["ones_bf"][:], rhs=xl_tiles[c][:, sl],
                start=(c == 0), stop=(c == DC - 1)), waits=smw)
            qmw = [wsq]
            if c == 0 and prev_copy_q is not None:
                qmw.append(prev_copy_q)
            prev_mmq = r.op("tensor", lambda e, c=c: e.matmul(
                ps["stq"][:], lhsT=sb["ones_bf"][:], rhs=sb["sqt"][:],
                start=(c == 0), stop=(c == DC - 1)), waits=qmw)
        wt = ("te", r.cnt["te"])
        prev_copy_s = r.op("scalar", lambda e, sl=sl: e.copy(out=sums[:, sl], in_=ps["sts"][:]), waits=[wt])
        prev_copy_q = r.op("scalar", lambda e, sl=sl: e.copy(out=sumsq[:, sl], in_=ps["stq"][:]), waits=[wt])

    w1 = r.op("vector", lambda e: e.tensor_scalar_mul(out=sums, in0=sums, scalar1=1.0 / D), waits=[prev_copy_s, prev_copy_q])
    w2 = r.op("scalar", lambda e: e.activation(out=scr, in_=sums, func=AF.Square), waits=[w1])
    w3 = r.op("vector", lambda e: e.tensor_scalar_mul(out=sumsq, in0=sumsq, scalar1=1.0 / D), waits=[w1])
    w4 = r.op("vector", lambda e: e.tensor_sub(out=sumsq, in0=sumsq, in1=scr), waits=[w2, w3])
    w5 = r.op("scalar", lambda e: e.activation(out=sumsq, in_=sumsq, func=AF.Ln, bias=cons[0:1, C_LNEPS:C_LNEPS + 1]), waits=[w4])
    wrstd = r.op("scalar", lambda e: e.activation(out=sumsq, in_=sumsq, func=AF.Exp, scale=cons[0:1, C_NEGH:C_NEGH + 1]), waits=[w5])

    guard = wrstd
    wlast = None
    for h in range(nhalf):
        sl = slice(512 * h, 512 * (h + 1))
        wm = r.op("tensor", lambda e, sl=sl: e.matmul(ps["mb"][:], lhsT=sb["ones1"][:], rhs=sums[:, sl], start=True, stop=True), waits=[guard])
        wr = r.op("tensor", lambda e, sl=sl: e.matmul(ps["rb"][:], lhsT=sb["ones1"][:], rhs=sumsq[:, sl], start=True, stop=True), waits=[guard])
        for c in range(DC):
            waw = ([wm, wr] if c == 0 else []) + ([wlast] if wlast else [])
            wa = r.op("vector", lambda e, c=c, sl=sl: e.tensor_sub(out=sb["tmp"][:], in0=x_tiles[c][:, sl], in1=ps["mb"][:]), waits=waw)
            wlast = r.op("vector", lambda e, c=c, sl=sl: e.tensor_mul(out=xl_tiles[c][:, sl], in0=sb["tmp"][:], in1=ps["rb"][:]), waits=[wa])
        guard = wlast
    return wlast



def build_A():
    nc = bass.Bass("TRN2", target_bir_lowering=False, debug=False)
    xT = nc.dram_tensor("xT", [D, TPC], F32, kind="ExternalInput").ap()
    wbcT = nc.dram_tensor("wbcT", [D, 128], F32, kind="ExternalInput").ap()
    wdtT = nc.dram_tensor("wdtT", [DT_RANK, D], F32, kind="ExternalInput").ap()
    bdt = nc.dram_tensor("bdt", [128, DC], F32, kind="ExternalInput").ap()
    ones_bf_d = nc.dram_tensor("ones_bf", [128, 1], BF16, kind="ExternalInput").ap()
    ones1_d = nc.dram_tensor("ones1", [1, 128], F32, kind="ExternalInput").ap()
    consts_d = nc.dram_tensor("consts", [128, 8], F32, kind="ExternalInput").ap()

    o_delta = nc.dram_tensor("o_delta", [D, TPC], BF16, kind="ExternalOutput").ap()
    o_u = nc.dram_tensor("o_u", [D, TPC], BF16, kind="ExternalOutput").ap()
    o_bc = nc.dram_tensor("o_bc", [64, TPC], BF16, kind="ExternalOutput").ap()

    r = Rec()
    with contextlib.ExitStack() as stack:
        ec = stack.enter_context
        x_tiles = [ec(nc.sbuf_tensor(f"x{c}", [128, TPC], F32)) for c in range(DC)]
        xl_tiles = [ec(nc.sbuf_tensor(f"xl{c}", [128, TPC], BF16)) for c in range(DC)]
        sb = {
            "stats": ec(nc.sbuf_tensor("stats", [1, 3 * TPC], F32)),
            "sqt": ec(nc.sbuf_tensor("sqt", [128, 512], BF16)),
            "tmp": ec(nc.sbuf_tensor("tmp", [128, 512], F32)),
            "ones_bf": ec(nc.sbuf_tensor("onesbf", [128, 1], BF16)),
            "ones1": ec(nc.sbuf_tensor("ones1s", [1, 128], F32)),
            "consts": ec(nc.sbuf_tensor("constsS", [128, 8], F32)),
        }
        wbc_sb = ec(nc.sbuf_tensor("wbc", [128, DC * 128], BF16))
        wdt_sb = ec(nc.sbuf_tensor("wdt", [DT_RANK, D], BF16))
        bdt_sb = ec(nc.sbuf_tensor("bdtS", [128, DC], F32))
        dtr_sb = ec(nc.sbuf_tensor("dtr", [DT_RANK, TPC], BF16))
        bc_sb = ec(nc.sbuf_tensor("bcS", [64, TPC], BF16))
        delta_sb = [ec(nc.sbuf_tensor(f"dl{c}", [128, TPC], BF16)) for c in range(DC)]
        t_abs = ec(nc.sbuf_tensor("tabs", [128, 512], F32))
        t_rel = ec(nc.sbuf_tensor("trel", [128, 512], F32))

        ps = {
            "sts": ec(nc.psum_tensor("psts", [1, 512], F32)),
            "stq": ec(nc.psum_tensor("pstq", [1, 512], F32)),
            "mb": ec(nc.psum_tensor("pmb", [128, 512], F32)),
            "rb": ec(nc.psum_tensor("prb", [128, 512], F32)),
        }
        ps_db = [ec(nc.psum_tensor(f"pdb{h}", [128, 512], F32)) for h in range(2)]
        ps_dt = [ec(nc.psum_tensor(f"pdt{h}", [128, 512], F32)) for h in range(2)]

        sems = _mk_sems(nc, stack)

        wx = [r.dma("sync", x_tiles[c][:], xT[128 * c:128 * (c + 1), :]) for c in range(DC)]
        ww1 = r.dma("gpsimd", wbc_sb[:].rearrange("p (c o) -> p c o", o=128),
                    wbcT.rearrange("(c p) o -> p c o", p=128))
        ww2 = r.dma("gpsimd", wdt_sb[:], wdtT)
        wb = r.dma("sync", bdt_sb[:], bdt)
        wo1 = r.dma("sync", sb["ones_bf"][:], ones_bf_d)
        wo2 = r.dma("sync", sb["ones1"][:], ones1_d)
        wcn = r.dma("sync", sb["consts"][:], consts_d)

        wln = emit_ln(r, sb, ps, x_tiles, xl_tiles, TPC,
                      deps=[wx[DC - 1], wo1, wo2, wcn])

        ev_dtr = [None, None]
        ev_bc = [None, None]
        for h in range(2):
            sl = slice(512 * h, 512 * (h + 1))
            for c in range(DC):
                r.op("tensor", lambda e, c=c, h=h, sl=sl: e.matmul(
                    ps_db[h][:], lhsT=wbc_sb[:, 128 * c:128 * (c + 1)], rhs=xl_tiles[c][:, sl],
                    start=(c == 0), stop=(c == DC - 1)),
                    waits=[wln, ww1] if (h == 0 and c == 0) else [])
            wt = ("te", r.cnt["te"])
            ev_dtr[h] = r.op("scalar", lambda e, h=h, sl=sl: e.copy(out=dtr_sb[:, sl], in_=ps_db[h][0:64, :]), waits=[wt])
            ev_bc[h] = r.op("scalar", lambda e, h=h, sl=sl: e.copy(out=bc_sb[:, sl], in_=ps_db[h][64:128, :]), waits=[wt])

        prev_psum_reader = [None, None]
        prev_vadd = None
        for c in range(DC):
            for h in range(2):
                sl = slice(512 * h, 512 * (h + 1))
                w0 = [ev_dtr[h], ww2, wb]
                if prev_psum_reader[h] is not None:
                    w0.append(prev_psum_reader[h])
                wmm = r.op("tensor", lambda e, c=c, h=h, sl=sl: e.matmul(
                    ps_dt[h][:], lhsT=wdt_sb[:, 128 * c:128 * (c + 1)], rhs=dtr_sb[:, sl],
                    start=True, stop=True), waits=w0)
                wa_w = [wmm] + ([prev_vadd] if prev_vadd is not None else [])
                wa = r.op("scalar", lambda e, c=c, h=h: e.activation(
                    out=t_abs[:], in_=ps_dt[h][:], func=AF.Abs, bias=bdt_sb[:, c:c + 1]), waits=wa_w)
                wr2 = r.op("scalar", lambda e, c=c, h=h: e.activation(
                    out=t_rel[:], in_=ps_dt[h][:], func=AF.Relu, bias=bdt_sb[:, c:c + 1]), waits=[])
                prev_psum_reader[h] = wr2
                we = r.op("scalar", lambda e: e.activation(out=t_abs[:], in_=t_abs[:], func=AF.Exp, scale=sb["consts"][:, C_NEG1:C_NEG1 + 1]), waits=[])
                wl = r.op("scalar", lambda e: e.activation(out=t_abs[:], in_=t_abs[:], func=AF.Ln, bias=sb["consts"][:, C_ONE:C_ONE + 1]), waits=[])
                prev_vadd = r.op("vector", lambda e, c=c, sl=sl: e.tensor_add(
                    out=delta_sb[c][:, sl], in0=t_rel[:], in1=t_abs[:]), waits=[wl])

        wve = ("ve", r.cnt["ve"])
        wsc = ("sc", r.cnt["sc"])
        for c in range(DC):
            r.dma("sync", o_delta[128 * c:128 * (c + 1), :], delta_sb[c][:], waits=[wve] if c == 0 else [])
            r.dma("sync", o_u[128 * c:128 * (c + 1), :], xl_tiles[c][:], waits=[])
        r.dma("sync", o_bc, bc_sb[:], waits=[wsc])
        final = r.cnt["ld"]
        r.wait("sync", [("ld", final)])

        with nc.Block() as block:
            r.replay(block, sems)
    return nc



TC = 512
NCH = SEQ // TC
NDT = 4
TC1 = TC + 2


def build_B():
    nc = bass.Bass("TRN2", target_bir_lowering=False, debug=False)
    delta_d = nc.dram_tensor("delta", [512, SEQ], BF16, kind="ExternalInput").ap()
    u_d = nc.dram_tensor("u", [512, SEQ], BF16, kind="ExternalInput").ap()
    bc_d = nc.dram_tensor("bc", [64, SEQ], BF16, kind="ExternalInput").ap()
    alog_d = nc.dram_tensor("alog", [128, 128], F32, kind="ExternalInput").ap()
    dpar_d = nc.dram_tensor("dpar", [128, NDT], F32, kind="ExternalInput").ap()
    o_y = nc.dram_tensor("o_y", [512, SEQ], BF16, kind="ExternalOutput").ap()

    r = Rec()
    with contextlib.ExitStack() as stack:
        ec = stack.enter_context
        dA = ec(nc.sbuf_tensor("dA", [128, D_STATE * TC1], BF16))
        dBu = ec(nc.sbuf_tensor("dBu", [128, D_STATE * TC1], BF16))
        hh = ec(nc.sbuf_tensor("hh", [128, D_STATE * TC1], BF16))
        brep = ec(nc.sbuf_tensor("brep", [128, D_STATE * TC], BF16))
        crep = ec(nc.sbuf_tensor("crep", [128, D_STATE * TC], BF16))
        dl_sb = ec(nc.sbuf_tensor("dl", [128, TC], BF16))
        u_sb = ec(nc.sbuf_tensor("uS", [128, TC], BF16))
        du_sb = ec(nc.sbuf_tensor("du", [128, TC], BF16))
        y_sb = ec(nc.sbuf_tensor("yS", [128, TC], BF16))
        carry = ec(nc.sbuf_tensor("carry", [128, D_STATE * NDT], BF16))
        alog_sb = ec(nc.sbuf_tensor("alogS", [128, 128], F32))
        a_sb = ec(nc.sbuf_tensor("aS", [128, 128], F32))
        dpar_sb = ec(nc.sbuf_tensor("dparS", [128, NDT], F32))

        sems = _mk_sems(nc, stack)

        dA3 = dA[:].rearrange("p (n t) -> p n t", t=TC1)
        dBu3 = dBu[:].rearrange("p (n t) -> p n t", t=TC1)
        hh3 = hh[:].rearrange("p (n t) -> p n t", t=TC1)
        brep3 = brep[:].rearrange("p (n t) -> p n t", t=TC)
        crep3 = crep[:].rearrange("p (n t) -> p n t", t=TC)

        wal = r.dma("sync", alog_sb[:], alog_d)
        wdp = r.dma("sync", dpar_sb[:], dpar_d)
        wexp = r.op("scalar", lambda e: e.activation(out=a_sb[:], in_=alog_sb[:], func=AF.Exp), waits=[wal])
        wA = r.op("vector", lambda e: e.tensor_scalar_mul(out=a_sb[:], in0=a_sb[:], scalar1=-1.0), waits=[wexp])

        prev_ve = wA
        prev_sc = wexp
        prev_ystore = None
        for ch in range(NCH):
            tsl = slice(TC * ch, TC * (ch + 1))
            wbr = r.dma("sync", brep3, bc_d[0:32, tsl].unsqueeze(0).broadcast_to([128, 32, TC]), waits=[prev_ve])
            wcr = r.dma("sync", crep3, bc_d[32:64, tsl].unsqueeze(0).broadcast_to([128, 32, TC]), waits=[])
            for dt in range(NDT):
                psl = slice(128 * dt, 128 * (dt + 1))
                wld = r.dma("sync", dl_sb[:], delta_d[psl, tsl], waits=[prev_sc, prev_ve])
                wlu = r.dma("sync", u_sb[:], u_d[psl, tsl], waits=[])
                wexps = None
                for n in range(D_STATE):
                    wexps = r.op("scalar", lambda e, n=n, dt=dt: e.activation(
                        out=dA[:, n * TC1 + 1: (n + 1) * TC1], in_=dl_sb[:],
                        func=AF.Exp, scale=a_sb[:, dt * 32 + n: dt * 32 + n + 1]),
                        waits=[wld, prev_ve, wA] if n == 0 else [])
                wms = r.op("vector", lambda e: e.memset(dA3[:, :, 0:1], 0.0), waits=[prev_ve] if ch + dt else [])
                wdu = r.op("vector", lambda e: e.tensor_mul(out=du_sb[:], in0=dl_sb[:], in1=u_sb[:]), waits=[wld, wlu])
                wdb = r.op("vector", lambda e: e.tensor_tensor(
                    out=dBu3[:, :, 1:TC + 1], in0=du_sb[:].unsqueeze(1).broadcast_to([128, 32, TC]),
                    in1=brep3, op=ALU.mult), waits=[wbr, wdu])
                if ch == 0:
                    wcc = r.op("vector", lambda e, dt=dt: e.memset(dBu3[:, :, 0:1], 0.0), waits=[])
                else:
                    wcc = r.op("vector", lambda e, dt=dt: e.tensor_copy(
                        out=dBu3[:, :, 0:1], in_=carry[:, dt * 32:(dt + 1) * 32].unsqueeze(2)), waits=[])
                wsc_ = r.op("vector", lambda e: e.tensor_tensor_scan(
                    out=hh[:], data0=dA[:], data1=dBu[:], initial=0.0,
                    op0=ALU.mult, op1=ALU.add), waits=[wexps, wms, wdb, wcc])
                wco = r.op("vector", lambda e, dt=dt: e.tensor_copy(
                    out=carry[:, dt * 32:(dt + 1) * 32].unsqueeze(2), in_=hh3[:, :, TC:TC + 1]), waits=[wsc_])
                whc = r.op("vector", lambda e: e.tensor_tensor(
                    out=dA3[:, :, 1:TC + 1], in0=hh3[:, :, 1:TC + 1], in1=crep3, op=ALU.mult), waits=[wcr, wsc_])
                wtr = whc
                for half in (16, 8, 4, 2, 1):
                    wtr = r.op("vector", lambda e, half=half: e.tensor_tensor(
                        out=dA3[:, 0:half, 1:TC + 1], in0=dA3[:, 0:half, 1:TC + 1],
                        in1=dA3[:, half:2 * half, 1:TC + 1], op=ALU.add), waits=[wtr])
                yw = [wdp, wtr] + ([prev_ystore] if prev_ystore else [])
                wy = r.op("vector", lambda e, dt=dt: e.scalar_tensor_tensor(
                    out=y_sb[:], in0=u_sb[:], scalar=dpar_sb[:, dt:dt + 1],
                    in1=dA[:, 1:TC + 1], op0=ALU.mult, op1=ALU.add), waits=yw)
                prev_ystore = r.dma("sync", o_y[psl, tsl], y_sb[:], waits=[wy])
                prev_ve = wy
                prev_sc = wexps
        final = r.cnt["ld"]
        r.wait("sync", [("ld", final)])

        with nc.Block() as block:
            r.replay(block, sems)
    return nc


def host_prep_B(inputs, resA):
    dseq = {b: np.concatenate([resA[4 * b + j]["o_delta"] for j in range(4)], axis=1) for b in range(B_SZ)}
    useq = {b: np.concatenate([resA[4 * b + j]["o_u"] for j in range(4)], axis=1) for b in range(B_SZ)}
    bcseq = {b: np.ascontiguousarray(np.concatenate([resA[4 * b + j]["o_bc"] for j in range(4)], axis=1)) for b in range(B_SZ)}
    alog = np.asarray(inputs["L0_attn_A_log"], dtype=np.float32)
    dpar = np.asarray(inputs["L0_attn_D_param"], dtype=np.float32)
    maps = []
    for c in range(NCORE):
        b, j = c // 4, c % 4
        rows = slice(512 * j, 512 * (j + 1))
        al = np.ascontiguousarray(
            alog[rows].reshape(4, 128, 32).transpose(1, 0, 2).reshape(128, 128))
        dp = np.ascontiguousarray(dpar[rows.start:rows.stop].reshape(4, 128).T)
        maps.append(dict(
            delta=np.ascontiguousarray(dseq[b][rows.start:rows.stop]),
            u=np.ascontiguousarray(useq[b][rows.start:rows.stop]),
            bc=bcseq[b], alog=al, dpar=dp))
    return maps



QSPEC = [
    ("w1s", 256, H0, 0),
    ("w2s", 256, H0, 0),
    ("w3s", 704, D, 0),
    ("f1s", 256, H1, 1),
    ("f2s", 1024, D, 1),
]
CSPEC = [("aq", 256, D), ("ak", 256, D), ("av", 256, D), ("ao", 256, D)]
QCOUNT = [H0 * D, H0 * D, H0 * D, H1 * D, H1 * D]


def build_Q():
    nc = bass.Bass("TRN2", target_bir_lowering=False, debug=False)
    ins = {}
    outs = {}
    for (nm, rows, cols, _) in QSPEC:
        ins[nm] = nc.dram_tensor(nm, [rows, cols], F32, kind="ExternalInput").ap()
        outs[nm] = nc.dram_tensor("q" + nm, [rows, cols], BF16, kind="ExternalOutput").ap()
    for (nm, rows, cols) in CSPEC:
        ins[nm] = nc.dram_tensor(nm, [rows, cols], F32, kind="ExternalInput").ap()
        outs[nm] = nc.dram_tensor("q" + nm, [rows, cols], BF16, kind="ExternalOutput").ap()
    ones1_d = nc.dram_tensor("ones1", [1, 128], F32, kind="ExternalInput").ap()
    invcnt_d = nc.dram_tensor("invcnt", [1, 8], F32, kind="ExternalInput").ap()
    sel_d = nc.dram_tensor("sel", [1, 8], F32, kind="ExternalInput").ap()

    g_b = nc.dram_tensor("g_bounce", [1, 8], F32)
    g_ar = nc.dram_tensor("g_ar", [1, 8], F32)

    r = Rec()
    with contextlib.ExitStack() as stack:
        ec = stack.enter_context
        wt = ec(nc.sbuf_tensor("wt", [128, H1], F32))
        qt = ec(nc.sbuf_tensor("qt", [128, H1], BF16))
        q2 = ec(nc.sbuf_tensor("q2", [128, H1], BF16))
        pacc = ec(nc.sbuf_tensor("pacc", [128, 16], F32))
        redc = ec(nc.sbuf_tensor("redc", [128, 1], F32))
        grow = ec(nc.sbuf_tensor("grow", [1, 16], F32))
        thb = ec(nc.sbuf_tensor("thb", [128, 8], F32))
        nthb = ec(nc.sbuf_tensor("nthb", [128, 8], F32))
        ones1 = ec(nc.sbuf_tensor("ones1s", [1, 128], F32))
        onescol = ec(nc.sbuf_tensor("onescol", [128, 1], F32))
        ps1 = ec(nc.psum_tensor("ps1", [1, 8], F32))
        psb = ec(nc.psum_tensor("psb", [128, 8], F32))
        sems = _mk_sems(nc, stack)

        w_on = r.dma("sync", ones1[:], ones1_d)
        wic = r.dma("sync", grow[0:1, 8:16], invcnt_d)
        wsel = r.dma("sync", thb[0:1, 0:8], sel_d)
        wmo = r.op("vector", lambda e: e.memset(pacc[:], 0.0), waits=[])
        woc = r.op("vector", lambda e: e.memset(onescol[:], 1.0), waits=[])

        def tiles_of(rows):
            out = []
            p = 0
            while p < rows:
                out.append((p, min(128, rows - p)))
                p += 128
            return out

        prev = wmo
        wmm_dep = []
        for wi, (nm, rows, cols, _) in enumerate(QSPEC):
            accs = []
            for ti, (p0, np_) in enumerate(tiles_of(rows)):
                wl = r.dma("sync", wt[0:np_, 0:cols], ins[nm][p0:p0 + np_, :], waits=[prev])
                prev = r.op("vector", lambda e, np_=np_, cols=cols, ti=ti, wi=wi: e.reduce_sum(
                    out=pacc[0:np_, ti + 1:ti + 2], in_=wt[0:np_, 0:cols],
                    axis=mybir.AxisListType.X, apply_absolute_value=True), waits=[wl])
                accs.append(ti)
            prev = r.op("vector", lambda e, n=len(accs): e.reduce_sum(
                out=redc[:], in_=pacc[:, 0:16], axis=mybir.AxisListType.X), waits=[prev])
            wq = r.op("tensor", lambda e, wi=wi: e.matmul(
                ps1[0:1, wi:wi + 1], lhsT=onescol[:], rhs=redc[:],
                start=True, stop=True), waits=[prev, woc])
            wmm_dep.append(wq)
            prev = r.op("vector", lambda e: e.memset(pacc[:], 0.0), waits=[wq])
        wg = r.op("scalar", lambda e: e.copy(out=grow[0:1, 0:8], in_=ps1[0:1, 0:8]), waits=wmm_dep)
        wgd = r.dma("gpsimd", g_b.ap(), grow[0:1, 0:8], waits=[wg])
        war = r.collective("AllReduce", ALU.add, [list(range(NCORE))],
                           [g_b.ap().opt()], [g_ar.ap().opt()], waits=[wgd])
        wgb = r.dma("sync", grow[0:1, 0:8], g_ar.ap(), waits=[war])

        wm1 = r.op("vector", lambda e: e.tensor_mul(out=grow[0:1, 0:8], in0=grow[0:1, 0:8], in1=grow[0:1, 8:16]), waits=[wgb, wic])
        wa_ = r.op("vector", lambda e: e.tensor_scalar_add(out=grow[0:1, 8:16], in0=grow[0:1, 0:8], scalar1=1e-8), waits=[wm1])
        wb_ = r.op("vector", lambda e: e.tensor_scalar_max(out=grow[0:1, 0:8], in0=grow[0:1, 0:8], scalar1=1e-8), waits=[wa_])
        wbl1 = r.op("vector", lambda e: e.tensor_sub(out=grow[0:1, 8:16], in0=grow[0:1, 8:16], in1=grow[0:1, 0:8]), waits=[wa_, wb_])
        wbl2 = r.op("vector", lambda e: e.tensor_mul(out=grow[0:1, 8:16], in0=grow[0:1, 8:16], in1=thb[0:1, 0:8]), waits=[wbl1, wsel])
        wbl3 = r.op("vector", lambda e: e.tensor_add(out=grow[0:1, 0:8], in0=grow[0:1, 0:8], in1=grow[0:1, 8:16]), waits=[wbl2])
        wth = r.op("vector", lambda e: e.tensor_scalar_mul(out=grow[0:1, 0:8], in0=grow[0:1, 0:8], scalar1=0.5), waits=[wbl3])
        wtb = r.op("tensor", lambda e: e.matmul(psb[:], lhsT=ones1[:], rhs=grow[0:1, 0:8], start=True, stop=True), waits=[wth, w_on])
        wtc = r.op("vector", lambda e: e.tensor_copy(out=thb[:], in_=psb[:]), waits=[wtb])
        wtn = r.op("vector", lambda e: e.tensor_scalar_mul(out=nthb[:], in0=thb[:], scalar1=-1.0), waits=[wtc])

        prev_q = wtn
        prev_out = None
        for wi, (nm, rows, cols, _) in enumerate(QSPEC):
            for (p0, np_) in tiles_of(rows):
                wl = r.dma("sync", wt[0:np_, 0:cols], ins[nm][p0:p0 + np_, :], waits=[prev_q])
                w1w = [wl] + ([prev_out] if prev_out else [])
                w1_ = r.op("vector", lambda e, np_=np_, cols=cols, wi=wi: e.tensor_scalar(
                    out=qt[0:np_, 0:cols], in0=wt[0:np_, 0:cols], scalar1=thb[0:np_, wi:wi + 1],
                    scalar2=None, op0=ALU.is_ge), waits=w1w)
                w2_ = r.op("vector", lambda e, np_=np_, cols=cols, wi=wi: e.tensor_scalar(
                    out=q2[0:np_, 0:cols], in0=wt[0:np_, 0:cols], scalar1=nthb[0:np_, wi:wi + 1],
                    scalar2=None, op0=ALU.is_le), waits=[])
                w3_ = r.op("vector", lambda e, np_=np_, cols=cols: e.tensor_sub(
                    out=qt[0:np_, 0:cols], in0=qt[0:np_, 0:cols], in1=q2[0:np_, 0:cols]), waits=[w1_, w2_])
                prev_out = r.dma("sync", outs[nm][p0:p0 + np_, :], qt[0:np_, 0:cols], waits=[w3_])
                prev_q = w3_
        for (nm, rows, cols) in CSPEC:
            for (p0, np_) in tiles_of(rows):
                wl = r.dma("sync", wt[0:np_, 0:cols], ins[nm][p0:p0 + np_, :], waits=[prev_q])
                cw = [wl] + ([prev_out] if prev_out else [])
                prev_q = r.op("vector", lambda e, np_=np_, cols=cols: e.tensor_copy(
                    out=qt[0:np_, 0:cols], in_=wt[0:np_, 0:cols]), waits=cw)
                prev_out = r.dma("sync", outs[nm][p0:p0 + np_, :], qt[0:np_, 0:cols], waits=[prev_q])
        final = r.cnt["ld"]
        r.wait("sync", [("ld", final)])
        with nc.Block() as block:
            r.replay(block, sems)
    return nc


def host_prep_Q(inputs):
    w1T = np.ascontiguousarray(inputs["L0_ffn_w1"].T).astype(np.float32)
    w2T = np.ascontiguousarray(inputs["L0_ffn_w2"].T).astype(np.float32)
    w3T = np.ascontiguousarray(inputs["L0_ffn_w3"].T).astype(np.float32)
    f1T = np.ascontiguousarray(inputs["L1_ffn_w1"].T).astype(np.float32)
    f2T = np.ascontiguousarray(inputs["L1_ffn_w2"].T).astype(np.float32)
    aqT = np.ascontiguousarray(inputs["L1_attn_Wq"].T).astype(np.float32)
    akT = np.ascontiguousarray(inputs["L1_attn_Wk"].T).astype(np.float32)
    avT = np.ascontiguousarray(inputs["L1_attn_Wv"].T).astype(np.float32)
    aoT = np.ascontiguousarray(inputs["L1_attn_Wo"].T).astype(np.float32)
    full = dict(w1s=w1T, w2s=w2T, w3s=w3T, f1s=f1T, f2s=f2T, aq=aqT, ak=akT, av=avT, ao=aoT)
    shard_rows = dict(w1s=256, w2s=256, w3s=704, f1s=256, f2s=1024, aq=256, ak=256, av=256, ao=256)
    ones1 = np.ones((1, 128), dtype=np.float32)
    invcnt = np.zeros((1, 8), dtype=np.float32)
    for i, c in enumerate(QCOUNT):
        invcnt[0, i] = 1.0 / c
    sel = np.zeros((1, 8), dtype=np.float32)
    sel[0, 0:3] = 1.0
    maps = []
    for c in range(NCORE):
        m = dict(ones1=ones1, invcnt=invcnt, sel=sel)
        for nm, arr in full.items():
            rr = shard_rows[nm]
            m[nm] = np.ascontiguousarray(arr[rr * c:rr * (c + 1)])
        maps.append(m)
    return maps


def host_join_Q(resQ):
    out = {}
    for nm in ("w1s", "w2s", "w3s", "f1s", "f2s", "aq", "ak", "av", "ao"):
        out[nm] = np.concatenate([resQ[c]["q" + nm] for c in range(NCORE)], axis=0)
    return out



NF0 = H0 // 128


def build_C():
    nc = bass.Bass("TRN2", target_bir_lowering=False, debug=False)
    xT = nc.dram_tensor("xT", [D, TPC], F32, kind="ExternalInput").ap()
    yT = nc.dram_tensor("yT", [D, TPC], BF16, kind="ExternalInput").ap()
    qw1 = nc.dram_tensor("qw1", [NF0, 128, D], BF16, kind="ExternalInput").ap()
    qw2 = nc.dram_tensor("qw2", [NF0, 128, D], BF16, kind="ExternalInput").ap()
    qw3 = nc.dram_tensor("qw3", [DC, 128, H0], BF16, kind="ExternalInput").ap()
    ones_bf_d = nc.dram_tensor("ones_bf", [128, 1], BF16, kind="ExternalInput").ap()
    ones1_d = nc.dram_tensor("ones1", [1, 128], F32, kind="ExternalInput").ap()
    consts_d = nc.dram_tensor("consts", [128, 8], F32, kind="ExternalInput").ap()
    o_x2 = nc.dram_tensor("o_x2", [D, TPC], F32, kind="ExternalOutput").ap()

    r = Rec()
    with contextlib.ExitStack() as stack:
        ec = stack.enter_context
        x_tiles = [ec(nc.sbuf_tensor(f"x{c}", [128, TPC], F32)) for c in range(DC)]
        xl_tiles = [ec(nc.sbuf_tensor(f"xl{c}", [128, TPC], BF16)) for c in range(DC)]
        hid = [ec(nc.sbuf_tensor(f"hid{f}", [128, 512], BF16)) for f in range(NF0)]
        w1s = [ec(nc.sbuf_tensor(f"w1s{p}", [128, D], BF16)) for p in range(2)]
        w2s = [ec(nc.sbuf_tensor(f"w2s{p}", [128, D], BF16)) for p in range(2)]
        w3s = [ec(nc.sbuf_tensor(f"w3s{p}", [128, H0], BF16)) for p in range(2)]
        sg_sb = ec(nc.sbuf_tensor("sg", [128, 512], BF16))
        sb = {
            "stats": ec(nc.sbuf_tensor("stats", [1, 3 * TPC], F32)),
            "sqt": ec(nc.sbuf_tensor("sqt", [128, 512], BF16)),
            "tmp": ec(nc.sbuf_tensor("tmp", [128, 512], F32)),
            "ones_bf": ec(nc.sbuf_tensor("onesbf", [128, 1], BF16)),
            "ones1": ec(nc.sbuf_tensor("ones1s", [1, 128], F32)),
            "consts": ec(nc.sbuf_tensor("constsS", [128, 8], F32)),
        }
        ps = {
            "sts": ec(nc.psum_tensor("psts", [1, 512], F32)),
            "stq": ec(nc.psum_tensor("pstq", [1, 512], F32)),
            "mb": ec(nc.psum_tensor("pmb", [128, 512], F32)),
            "rb": ec(nc.psum_tensor("prb", [128, 512], F32)),
        }
        px1 = ec(nc.psum_tensor("px1", [128, 512], F32))
        px2 = ec(nc.psum_tensor("px2", [128, 512], F32))
        sems = _mk_sems(nc, stack)

        wx = [r.dma("sync", x_tiles[c][:], xT[128 * c:128 * (c + 1), :]) for c in range(DC)]
        wy = [r.dma("sync", xl_tiles[c][:], yT[128 * c:128 * (c + 1), :]) for c in range(DC)]
        wo1 = r.dma("sync", sb["ones_bf"][:], ones_bf_d)
        wo2 = r.dma("sync", sb["ones1"][:], ones1_d)
        wcn = r.dma("sync", sb["consts"][:], consts_d)

        wadd = None
        for c in range(DC):
            wadd = r.op("vector", lambda e, c=c: e.tensor_tensor(
                out=x_tiles[c][:], in0=x_tiles[c][:], in1=xl_tiles[c][:], op=ALU.add),
                waits=[wx[c], wy[c]])

        wln = emit_ln(r, sb, ps, x_tiles, xl_tiles, TPC, deps=[wadd, wo1, wo2, wcn])

        rd1 = [None, None]
        rd2 = [None, None]
        rd3 = [None, None]
        prev_sig = None
        prev_hct = None
        prev_out_te = None
        prev_radd = None
        for h in range(2):
            sl = slice(512 * h, 512 * (h + 1))
            for f in range(NF0):
                par = f % 2
                psem = "lde" if par == 0 else "ldo"
                wd1 = r.dma("sync", w1s[par][:], qw1[f], waits=[w for w in (rd1[par],) if w], sem=psem)
                wd2 = r.dma("sync", w2s[par][:], qw2[f], waits=[w for w in (rd2[par],) if w], sem=psem)
                w0 = [wd1, wln] + ([prev_sig] if prev_sig else [])
                for c in range(DC):
                    r.op("tensor", lambda e, c=c, f=f, par=par, sl=sl: e.matmul(
                        px1[:], lhsT=w1s[par][:, 128 * c:128 * (c + 1)], rhs=xl_tiles[c][:, sl],
                        start=(c == 0), stop=(c == DC - 1)), waits=w0 if c == 0 else [])
                rd1[par] = ("te", r.cnt["te"])
                wte1 = rd1[par]
                w0 = [wd2] + ([prev_hct] if prev_hct else [])
                for c in range(DC):
                    r.op("tensor", lambda e, c=c, f=f, par=par, sl=sl: e.matmul(
                        px2[:], lhsT=w2s[par][:, 128 * c:128 * (c + 1)], rhs=xl_tiles[c][:, sl],
                        start=(c == 0), stop=(c == DC - 1)), waits=w0 if c == 0 else [])
                rd2[par] = ("te", r.cnt["te"])
                wte2 = rd2[par]
                sgw = [wte1] + ([prev_hct] if prev_hct else [])
                prev_sig = r.op("scalar", lambda e: e.activation(
                    out=sg_sb[:], in_=px1[:], func=AF.Sigmoid), waits=sgw)
                hw = [wte2, prev_sig] + ([prev_out_te] if (prev_out_te and h == 1) else [])
                prev_hct = r.op("vector", lambda e, f=f: e.tensor_tensor(
                    out=hid[f][:], in0=px2[:], in1=sg_sb[:], op=ALU.mult), waits=hw)
            whid = prev_hct
            for dci in range(DC):
                par = dci % 2
                wd3 = r.dma("sync", w3s[par][:], qw3[dci], waits=[w for w in (rd3[par],) if w],
                            sem="lde" if par == 0 else "ldo")
                w0 = [wd3, whid, prev_sig] + ([prev_radd] if prev_radd else [])
                for k in range(NF0):
                    r.op("tensor", lambda e, k=k, par=par: e.matmul(
                        px1[:], lhsT=w3s[par][:, 128 * k:128 * (k + 1)], rhs=hid[k][:],
                        start=(k == 0), stop=(k == NF0 - 1)), waits=w0 if k == 0 else [])
                rd3[par] = ("te", r.cnt["te"])
                prev_out_te = rd3[par]
                prev_radd = r.op("vector", lambda e, dci=dci, sl=sl: e.tensor_tensor(
                    out=x_tiles[dci][:, sl], in0=x_tiles[dci][:, sl], in1=px1[:], op=ALU.add),
                    waits=[rd3[par]])
                prev_sig = prev_radd
        for c in range(DC):
            r.dma("sync", o_x2[128 * c:128 * (c + 1), :], x_tiles[c][:], waits=[prev_radd] if c == 0 else [])
        final = r.cnt["ld"]
        r.wait("sync", [("ld", final)])
        with nc.Block() as block:
            r.replay(block, sems)
    return nc


def slabify(wT, fchunks):
    K, F = wT.shape
    kc = K // 128
    assert F == fchunks * 128
    return np.ascontiguousarray(
        wT.reshape(kc, 128, fchunks, 128).transpose(2, 1, 0, 3).reshape(fchunks, 128, K))


def host_prep_C(resA_or_x, yseq, qj):
    s1 = slabify(qj["w1s"], NF0)
    s2 = slabify(qj["w2s"], NF0)
    s3 = slabify(qj["w3s"], DC)
    ones_bf = np.ones((128, 1), dtype=BF)
    ones1 = np.ones((1, 128), dtype=np.float32)
    consts = np.tile(np.asarray([CONST_VALS], dtype=np.float32), (128, 1))
    return [
        dict(xT=resA_or_x[c], yT=yseq[c], qw1=s1, qw2=s2, qw3=s3,
             ones_bf=ones_bf, ones1=ones1, consts=consts)
        for c in range(NCORE)
    ]


def host_y_from_B(resB):
    ys = []
    for c in range(NCORE):
        b = c // 4
        yb = np.concatenate([resB[4 * b + j]["o_y"] for j in range(4)], axis=0)
        tloc = (c % 4)
        ys.append(np.ascontiguousarray(yb[:, TPC * tloc:TPC * (tloc + 1)]))
    return ys




def build_D():
    nc = bass.Bass("TRN2", target_bir_lowering=False, debug=False)
    xT = nc.dram_tensor("xT", [D, TPC], F32, kind="ExternalInput").ap()
    qaq = nc.dram_tensor("qaq", [DC, 128, D], BF16, kind="ExternalInput").ap()
    qak = nc.dram_tensor("qak", [DC, 128, D], BF16, kind="ExternalInput").ap()
    qav = nc.dram_tensor("qav", [DC, 128, D], BF16, kind="ExternalInput").ap()
    bq_d = nc.dram_tensor("bq", [128, DC], F32, kind="ExternalInput").ap()
    bk_d = nc.dram_tensor("bk", [128, DC], F32, kind="ExternalInput").ap()
    bv_d = nc.dram_tensor("bv", [128, DC], F32, kind="ExternalInput").ap()
    cosq_d = nc.dram_tensor("cosq", [128, TPC], BF16, kind="ExternalInput").ap()
    sinq_d = nc.dram_tensor("sinq", [128, TPC], BF16, kind="ExternalInput").ap()
    cosk_d = nc.dram_tensor("cosk", [128, TPC], BF16, kind="ExternalInput").ap()
    sink_d = nc.dram_tensor("sink", [128, TPC], BF16, kind="ExternalInput").ap()
    ones_bf_d = nc.dram_tensor("ones_bf", [128, 1], BF16, kind="ExternalInput").ap()
    ones1_d = nc.dram_tensor("ones1", [1, 128], F32, kind="ExternalInput").ap()
    consts_d = nc.dram_tensor("consts", [128, 8], F32, kind="ExternalInput").ap()
    o_q = nc.dram_tensor("o_q", [D, TPC], BF16, kind="ExternalOutput").ap()
    o_k = nc.dram_tensor("o_k", [D, TPC], BF16, kind="ExternalOutput").ap()
    o_v = nc.dram_tensor("o_v", [D, TPC], BF16, kind="ExternalOutput").ap()

    r = Rec()
    with contextlib.ExitStack() as stack:
        ec = stack.enter_context
        x_tiles = [ec(nc.sbuf_tensor(f"x{c}", [128, TPC], F32)) for c in range(DC)]
        xl_tiles = [ec(nc.sbuf_tensor(f"xl{c}", [128, TPC], BF16)) for c in range(DC)]
        wsl = [ec(nc.sbuf_tensor(f"wsl{p}", [128, D], BF16)) for p in range(2)]
        pt = ec(nc.sbuf_tensor("pt", [128, TPC], BF16))
        rt = ec(nc.sbuf_tensor("rt", [128, TPC], BF16))
        rt2 = ec(nc.sbuf_tensor("rt2", [128, TPC], BF16))
        tabs = {nm: ec(nc.sbuf_tensor(nm + "S", [128, TPC], BF16))
                for nm in ("cosq", "sinq", "cosk", "sink")}
        bias_sb = {nm: ec(nc.sbuf_tensor(nm + "S", [128, DC], F32))
                   for nm in ("bq", "bk", "bv")}
        sb = {
            "stats": ec(nc.sbuf_tensor("stats", [1, 3 * TPC], F32)),
            "sqt": ec(nc.sbuf_tensor("sqt", [128, 512], BF16)),
            "tmp": ec(nc.sbuf_tensor("tmp", [128, 512], F32)),
            "ones_bf": ec(nc.sbuf_tensor("onesbf", [128, 1], BF16)),
            "ones1": ec(nc.sbuf_tensor("ones1s", [1, 128], F32)),
            "consts": ec(nc.sbuf_tensor("constsS", [128, 8], F32)),
        }
        ps = {
            "sts": ec(nc.psum_tensor("psts", [1, 512], F32)),
            "stq": ec(nc.psum_tensor("pstq", [1, 512], F32)),
            "mb": ec(nc.psum_tensor("pmb", [128, 512], F32)),
            "rb": ec(nc.psum_tensor("prb", [128, 512], F32)),
        }
        pq = ec(nc.psum_tensor("pq", [128, 512], F32))
        sems = _mk_sems(nc, stack)

        wx = [r.dma("sync", x_tiles[c][:], xT[128 * c:128 * (c + 1), :]) for c in range(DC)]
        wo1 = r.dma("sync", sb["ones_bf"][:], ones_bf_d)
        wo2 = r.dma("sync", sb["ones1"][:], ones1_d)
        wcn = r.dma("sync", sb["consts"][:], consts_d)
        for nm, d_ in (("cosq", cosq_d), ("sinq", sinq_d), ("cosk", cosk_d), ("sink", sink_d)):
            r.dma("sync", tabs[nm][:], d_)
        for nm, d_ in (("bq", bq_d), ("bk", bk_d), ("bv", bv_d)):
            r.dma("sync", bias_sb[nm][:], d_)
        wtabs = ("ld", r.cnt["ld"])

        wln = emit_ln(r, sb, ps, x_tiles, xl_tiles, TPC, deps=[wx[DC - 1], wo1, wo2, wcn, wtabs])

        rdw = [None, None]
        prev_ev = None
        prev_store = None
        projs = [("q", qaq, "bq", o_q, "cosq", "sinq"),
                 ("k", qak, "bk", o_k, "cosk", "sink"),
                 ("v", qav, "bv", o_v, None, None)]
        fidx = 0
        for (pname, slabs, bnm, odst, cosn, sinn) in projs:
            for f in range(DC):
                par = fidx % 2
                psem = "lde" if par == 0 else "ldo"
                wd = r.dma("sync", wsl[par][:], slabs[f], waits=[w for w in (rdw[par],) if w], sem=psem)
                for h in range(2):
                    sl = slice(512 * h, 512 * (h + 1))
                    w0 = [wd, wln] + ([prev_ev] if prev_ev else [])
                    for c in range(DC):
                        r.op("tensor", lambda e, c=c, par=par, sl=sl: e.matmul(
                            pq[:], lhsT=wsl[par][:, 128 * c:128 * (c + 1)], rhs=xl_tiles[c][:, sl],
                            start=(c == 0), stop=(c == DC - 1)), waits=w0 if c == 0 else [])
                    rdw[par] = ("te", r.cnt["te"])
                    evw = [rdw[par]] + ([prev_store] if prev_store else [])
                    prev_ev = r.op("scalar", lambda e, sl=sl, bnm=bnm, f=f: e.activation(
                        out=pt[:, sl], in_=pq[:], func=AF.Identity,
                        bias=bias_sb[bnm][:, f:f + 1]), waits=evw)
                if cosn is not None:
                    wc1 = r.op("vector", lambda e: e.tensor_copy(out=rt[0:64, :], in_=pt[64:128, :]), waits=[prev_ev])
                    wc2 = r.op("vector", lambda e: e.tensor_copy(out=rt[64:128, :], in_=pt[0:64, :]), waits=[prev_ev])
                    wm1 = r.op("vector", lambda e, cosn=cosn: e.tensor_tensor(
                        out=pt[:], in0=pt[:], in1=tabs[cosn][:], op=ALU.mult), waits=[wc1, wc2])
                    wm2 = r.op("vector", lambda e, sinn=sinn: e.tensor_tensor(
                        out=rt2[:], in0=rt[:], in1=tabs[sinn][:], op=ALU.mult), waits=[wm1])
                    wad = r.op("vector", lambda e: e.tensor_tensor(
                        out=pt[:], in0=pt[:], in1=rt2[:], op=ALU.add), waits=[wm2])
                    prev_ev = wad
                prev_store = r.dma("sync", odst[128 * f:128 * (f + 1), :], pt[:], waits=[prev_ev])
                fidx += 1
        r.wait("sync", [("ld", r.cnt["ld"])])
        with nc.Block() as block:
            r.replay(block, sems)
    return nc



NKT = SEQ // 128


def build_E():
    nc = bass.Bass("TRN2", target_bir_lowering=False, debug=False)
    xT = nc.dram_tensor("xT", [D, TPC], F32, kind="ExternalInput").ap()
    qf_d = nc.dram_tensor("qf", [D, TPC], BF16, kind="ExternalInput").ap()
    kf_d = nc.dram_tensor("kf", [NH, 128, SEQ], BF16, kind="ExternalInput").ap()
    vt_d = nc.dram_tensor("vt", [NH, 128, SEQ], BF16, kind="ExternalInput").ap()
    mask_d = nc.dram_tensor("mask8", [NKT, 128, TPC], mybir.dt.int8, kind="ExternalInput").ap()
    qao = nc.dram_tensor("qao", [DC, 128, D], BF16, kind="ExternalInput").ap()
    bo_d = nc.dram_tensor("bo", [128, DC], F32, kind="ExternalInput").ap()
    ones_bf_d = nc.dram_tensor("ones_bf", [128, 1], BF16, kind="ExternalInput").ap()
    ones1_d = nc.dram_tensor("ones1", [1, 128], F32, kind="ExternalInput").ap()
    consts_d = nc.dram_tensor("consts", [128, 8], F32, kind="ExternalInput").ap()
    o_x3 = nc.dram_tensor("o_x3", [D, TPC], F32, kind="ExternalOutput").ap()

    r = Rec()
    with contextlib.ExitStack() as stack:
        ec = stack.enter_context
        x_tiles = [ec(nc.sbuf_tensor(f"x{c}", [128, TPC], F32)) for c in range(DC)]
        qf = [ec(nc.sbuf_tensor(f"qf{c}", [128, TPC], BF16)) for c in range(DC)]
        mg = [ec(nc.sbuf_tensor(f"mg{c}", [128, TPC], BF16)) for c in range(NH)]
        m8 = ec(nc.sbuf_tensor("m8", [128, NKT * TPC], mybir.dt.int8))
        kvh = [ec(nc.sbuf_tensor(f"kvh{p}", [128, SEQ], BF16)) for p in range(2)]
        vvh = [ec(nc.sbuf_tensor(f"vvh{p}", [128, SEQ], BF16)) for p in range(2)]
        s_sb = ec(nc.sbuf_tensor("sS", [128, 512], BF16))
        binv = ec(nc.sbuf_tensor("binv", [128, 512], BF16))
        srow = ec(nc.sbuf_tensor("srow", [1, 512], F32))
        sb = {
            "ones_bf": ec(nc.sbuf_tensor("onesbf", [128, 1], BF16)),
            "ones1": ec(nc.sbuf_tensor("ones1s", [1, 128], F32)),
            "consts": ec(nc.sbuf_tensor("constsS", [128, 8], F32)),
        }
        bo_sb = ec(nc.sbuf_tensor("boS", [128, DC], F32))
        ps_s = [ec(nc.psum_tensor(f"pss{i}", [128, 512], F32)) for i in range(2)]
        ps_ctx = ec(nc.psum_tensor("psctx", [128, 512], F32))
        ps_sum = ec(nc.psum_tensor("pssum", [1, 512], F32))
        ps_b = ec(nc.psum_tensor("psb", [128, 512], F32))
        sems = _mk_sems(nc, stack)

        wx = [r.dma("sync", x_tiles[c][:], xT[128 * c:128 * (c + 1), :]) for c in range(DC)]
        wq = [r.dma("sync", qf[c][:], qf_d[128 * c:128 * (c + 1), :]) for c in range(DC)]
        wm8 = r.dma("sync", m8[:].rearrange("p (kt t) -> p kt t", t=TPC),
                    mask_d.rearrange("kt p t -> p kt t"))
        wo1 = r.dma("sync", sb["ones_bf"][:], ones_bf_d)
        wo2 = r.dma("sync", sb["ones1"][:], ones1_d)
        wcn = r.dma("sync", sb["consts"][:], consts_d)
        wbo = r.dma("sync", bo_sb[:], bo_d)
        m83 = m8[:].rearrange("p (kt t) -> p kt t", t=TPC)
        wall = ("ld", r.cnt["ld"])

        rdk = [None, None]
        prev_exp = None
        prev_pv = None
        prev_evs = None
        prev_ctxev = None
        prev_sumev = None
        prev_bev = None
        cons = sb["consts"]
        for h in range(NH):
            par = h % 2
            psem = "lde" if par == 0 else "ldo"
            wkh = r.dma("sync", kvh[par][:], kf_d[h], waits=[w for w in (rdk[par],) if w], sem=psem)
            wvh = r.dma("sync", vvh[par][:], vt_d[h], waits=[], sem=psem)
            for qc in range(2):
                sl = slice(512 * qc, 512 * (qc + 1))
                for kt in range(NKT):
                    pp = kt % 2
                    w0 = [wkh, wall] + ([prev_exp] if prev_exp else [])
                    wmm = r.op("tensor", lambda e, h=h, kt=kt, par=par, sl=sl, pp=pp: e.matmul(
                        ps_s[pp][:], lhsT=kvh[par][:, 128 * kt:128 * (kt + 1)], rhs=qf[h][:, sl],
                        start=True, stop=True), waits=w0)
                    evw = [wmm] + ([prev_exp] if prev_exp else []) + ([prev_pv] if prev_pv else [])
                    wev = r.op("vector", lambda e, kt=kt, sl=sl, pp=pp: e.scalar_tensor_tensor(
                        out=s_sb[:], in0=m83[:, kt, sl], scalar=MASK_NEG, in1=ps_s[pp][:],
                        op0=ALU.mult, op1=ALU.add), waits=evw)
                    wexp = r.op("scalar", lambda e: e.activation(
                        out=s_sb[:], in_=s_sb[:], func=AF.Exp,
                        bias=cons[:, C_EXPB:C_EXPB + 1]), waits=[wev])
                    prev_evs = wev
                    pvw = [wexp, wvh]
                    if kt == 0 and prev_ctxev:
                        pvw.append(prev_ctxev)
                    r.op("tensor", lambda e, kt=kt, par=par: e.matmul(
                        ps_ctx[:], lhsT=vvh[par][:, 128 * kt:128 * (kt + 1)], rhs=s_sb[:],
                        start=(kt == 0), stop=(kt == NKT - 1)), waits=pvw)
                    smw = []
                    if kt == 0 and prev_sumev:
                        smw.append(prev_sumev)
                    prev_pv = r.op("tensor", lambda e, kt=kt: e.matmul(
                        ps_sum[:], lhsT=sb["ones_bf"][:], rhs=s_sb[:],
                        start=(kt == 0), stop=(kt == NKT - 1)), waits=smw)
                    prev_exp = wexp
                wte = ("te", r.cnt["te"])
                prev_sumev = r.op("scalar", lambda e: e.copy(out=srow[:], in_=ps_sum[:]), waits=[wte])
                wrc = r.op("vector", lambda e: e.reciprocal(out=srow[:], in_=srow[:]), waits=[prev_sumev])
                wbc = r.op("tensor", lambda e: e.matmul(
                    ps_b[:], lhsT=sb["ones1"][:], rhs=srow[:], start=True, stop=True),
                    waits=[wrc] + ([prev_bev] if prev_bev else []))
                prev_bev = r.op("vector", lambda e: e.tensor_copy(out=binv[:], in_=ps_b[:]), waits=[wbc])
                prev_ctxev = r.op("vector", lambda e, h=h, sl=sl: e.tensor_tensor(
                    out=mg[h][:, sl], in0=ps_ctx[:], in1=binv[:], op=ALU.mult),
                    waits=[prev_bev, wte])
            rdk[par] = ("te", r.cnt["te"])
        rdw = [prev_ctxev, prev_ctxev]
        prev_radd = None
        wsl = kvh
        for dci in range(DC):
            par = dci % 2
            psem = "lde" if par == 0 else "ldo"
            wd = r.dma("sync", wsl[par][:, 0:D], qao[dci], waits=[w for w in (rdw[par],) if w], sem=psem)
            for hf in range(2):
                sl = slice(512 * hf, 512 * (hf + 1))
                w0 = [wd, prev_ctxev]
                if prev_radd:
                    w0.append(prev_radd)
                for k in range(DC):
                    r.op("tensor", lambda e, k=k, par=par, sl=sl: e.matmul(
                        ps_b[:], lhsT=wsl[par][:, 0:D][:, 128 * k:128 * (k + 1)], rhs=mg[k][:, sl],
                        start=(k == 0), stop=(k == DC - 1)), waits=w0 if k == 0 else [])
                rdw[par] = ("te", r.cnt["te"])
                prev_radd = r.op("vector", lambda e, dci=dci, sl=sl: e.scalar_tensor_tensor(
                    out=x_tiles[dci][:, sl], in0=ps_b[:], scalar=bo_sb[:, dci:dci + 1],
                    in1=x_tiles[dci][:, sl], op0=ALU.add, op1=ALU.add),
                    waits=[rdw[par], wx[dci]])
        for c in range(DC):
            r.dma("sync", o_x3[128 * c:128 * (c + 1), :], x_tiles[c][:], waits=[prev_radd] if c == 0 else [])
        r.wait("sync", [("ld", r.cnt["ld"])])
        with nc.Block() as block:
            r.replay(block, sems)
    return nc



NF1 = H1 // 128


def build_F():
    nc = bass.Bass("TRN2", target_bir_lowering=False, debug=False)
    xT = nc.dram_tensor("xT", [D, TPC], F32, kind="ExternalInput").ap()
    qf1 = nc.dram_tensor("qf1", [NF1, 128, D], BF16, kind="ExternalInput").ap()
    qf2 = nc.dram_tensor("qf2", [DC, 128, H1], BF16, kind="ExternalInput").ap()
    b1_d = nc.dram_tensor("b1", [128, NF1], F32, kind="ExternalInput").ap()
    b2_d = nc.dram_tensor("b2", [128, DC], F32, kind="ExternalInput").ap()
    ones_bf_d = nc.dram_tensor("ones_bf", [128, 1], BF16, kind="ExternalInput").ap()
    ones1_d = nc.dram_tensor("ones1", [1, 128], F32, kind="ExternalInput").ap()
    consts_d = nc.dram_tensor("consts", [128, 8], F32, kind="ExternalInput").ap()
    o_xo = nc.dram_tensor("o_xo", [D, TPC], F32, kind="ExternalOutput").ap()

    r = Rec()
    with contextlib.ExitStack() as stack:
        ec = stack.enter_context
        x_tiles = [ec(nc.sbuf_tensor(f"x{c}", [128, TPC], F32)) for c in range(DC)]
        xl_tiles = [ec(nc.sbuf_tensor(f"xl{c}", [128, TPC], BF16)) for c in range(DC)]
        hid = [ec(nc.sbuf_tensor(f"hid{f}", [128, 512], BF16)) for f in range(NF1)]
        w1s = [ec(nc.sbuf_tensor(f"w1s{p}", [128, D], BF16)) for p in range(2)]
        w2s = ec(nc.sbuf_tensor("w2s", [128, H1], BF16))
        b1_sb = ec(nc.sbuf_tensor("b1S", [128, NF1], F32))
        b2_sb = ec(nc.sbuf_tensor("b2S", [128, DC], F32))
        sb = {
            "stats": ec(nc.sbuf_tensor("stats", [1, 3 * TPC], F32)),
            "sqt": ec(nc.sbuf_tensor("sqt", [128, 512], BF16)),
            "tmp": ec(nc.sbuf_tensor("tmp", [128, 512], F32)),
            "ones_bf": ec(nc.sbuf_tensor("onesbf", [128, 1], BF16)),
            "ones1": ec(nc.sbuf_tensor("ones1s", [1, 128], F32)),
            "consts": ec(nc.sbuf_tensor("constsS", [128, 8], F32)),
        }
        ps = {
            "sts": ec(nc.psum_tensor("psts", [1, 512], F32)),
            "stq": ec(nc.psum_tensor("pstq", [1, 512], F32)),
            "mb": ec(nc.psum_tensor("pmb", [128, 512], F32)),
            "rb": ec(nc.psum_tensor("prb", [128, 512], F32)),
        }
        px1 = ec(nc.psum_tensor("px1", [128, 512], F32))
        sems = _mk_sems(nc, stack)

        wx = [r.dma("sync", x_tiles[c][:], xT[128 * c:128 * (c + 1), :]) for c in range(DC)]
        wo1 = r.dma("sync", sb["ones_bf"][:], ones_bf_d)
        wo2 = r.dma("sync", sb["ones1"][:], ones1_d)
        wcn = r.dma("sync", sb["consts"][:], consts_d)
        wb1 = r.dma("sync", b1_sb[:], b1_d)
        wb2 = r.dma("sync", b2_sb[:], b2_d)

        wln = emit_ln(r, sb, ps, x_tiles, xl_tiles, TPC, deps=[wx[DC - 1], wo1, wo2, wcn, wb1, wb2])

        rd1 = [None, None]
        rd2 = None
        prev_gel = None
        prev_out_te = None
        prev_radd = None
        for h in range(2):
            sl = slice(512 * h, 512 * (h + 1))
            for f in range(NF1):
                par = f % 2
                psem = "lde" if par == 0 else "ldo"
                wd1 = r.dma("sync", w1s[par][:], qf1[f], waits=[w for w in (rd1[par],) if w], sem=psem)
                w0 = [wd1, wln] + ([prev_gel] if prev_gel else [])
                for c in range(DC):
                    r.op("tensor", lambda e, c=c, par=par, sl=sl: e.matmul(
                        px1[:], lhsT=w1s[par][:, 128 * c:128 * (c + 1)], rhs=xl_tiles[c][:, sl],
                        start=(c == 0), stop=(c == DC - 1)), waits=w0 if c == 0 else [])
                rd1[par] = ("te", r.cnt["te"])
                gw = [rd1[par]]
                if h == 1 and prev_out_te:
                    gw.append(prev_out_te)
                prev_gel = r.op("scalar", lambda e, f=f: e.activation(
                    out=hid[f][:], in_=px1[:], func=AF.Gelu, bias=b1_sb[:, f:f + 1]), waits=gw)
            whid = prev_gel
            for dci in range(DC):
                wd2 = r.dma("sync", w2s[:], qf2[dci], waits=[w for w in (rd2,) if w], sem="lde")
                w0 = [wd2, whid, prev_gel] + ([prev_radd] if prev_radd else [])
                for k in range(NF1):
                    r.op("tensor", lambda e, k=k: e.matmul(
                        px1[:], lhsT=w2s[:, 128 * k:128 * (k + 1)], rhs=hid[k][:],
                        start=(k == 0), stop=(k == NF1 - 1)), waits=w0 if k == 0 else [])
                rd2 = ("te", r.cnt["te"])
                prev_out_te = rd2
                prev_radd = r.op("vector", lambda e, dci=dci, sl=sl: e.scalar_tensor_tensor(
                    out=x_tiles[dci][:, sl], in0=px1[:], scalar=b2_sb[:, dci:dci + 1],
                    in1=x_tiles[dci][:, sl], op0=ALU.add, op1=ALU.add),
                    waits=[rd2])
                prev_gel = prev_radd
        for c in range(DC):
            r.dma("sync", o_xo[128 * c:128 * (c + 1), :], x_tiles[c][:], waits=[prev_radd] if c == 0 else [])
        r.wait("sync", [("ld", r.cnt["ld"])])
        with nc.Block() as block:
            r.replay(block, sems)
    return nc



QSPEC_F = [
    ("w1f", 704, D, 0),
    ("w2f", 704, D, 0),
    ("w3f", 256, H0, 0),
    ("f1f", 1024, D, 1),
    ("f2f", 256, H1, 1),
]
CSPEC_F = [("aqf", 256, D), ("akf", 256, D), ("avf", 256, D), ("aof", 256, D)]
GRP4 = [[0, 1, 2, 3], [4, 5, 6, 7]]
GRP8 = [list(range(NCORE))]


def build_fused(upto="F"):
    nc = bass.Bass("TRN2", target_bir_lowering=False, debug=False)
    ap = lambda t: t.ap()
    xT = nc.dram_tensor("xT", [D, TPC], F32, kind="ExternalInput").ap()
    wbcT = nc.dram_tensor("wbcT", [D, 128], F32, kind="ExternalInput").ap()
    wdtT = nc.dram_tensor("wdtT", [DT_RANK, D], F32, kind="ExternalInput").ap()
    bdt = nc.dram_tensor("bdt", [128, DC], F32, kind="ExternalInput").ap()
    alog_d = nc.dram_tensor("alog", [128, 128], F32, kind="ExternalInput").ap()
    dpar_d = nc.dram_tensor("dpar", [128, NDT], F32, kind="ExternalInput").ap()
    wsh_in = {}
    for (nm, rows, cols, _) in QSPEC_F:
        wsh_in[nm] = nc.dram_tensor(nm, [rows, cols], F32, kind="ExternalInput").ap()
    for (nm, rows, cols) in CSPEC_F:
        wsh_in[nm] = nc.dram_tensor(nm, [rows, cols], F32, kind="ExternalInput").ap()
    invcnt_d = nc.dram_tensor("invcnt", [1, 8], F32, kind="ExternalInput").ap()
    sel_d = nc.dram_tensor("sel", [1, 8], F32, kind="ExternalInput").ap()
    bq_d = nc.dram_tensor("bq", [128, DC], F32, kind="ExternalInput").ap()
    bk_d = nc.dram_tensor("bk", [128, DC], F32, kind="ExternalInput").ap()
    bvrow_d = nc.dram_tensor("bvrow", [1, D], F32, kind="ExternalInput").ap()
    bo_d = nc.dram_tensor("bo", [128, DC], F32, kind="ExternalInput").ap()
    b1_d = nc.dram_tensor("b1", [128, NF1], F32, kind="ExternalInput").ap()
    b2_d = nc.dram_tensor("b2", [128, DC], F32, kind="ExternalInput").ap()
    rope_d = {nm: nc.dram_tensor(nm, [128, TPC], BF16, kind="ExternalInput").ap()
              for nm in ("cosq", "sinq", "cosk", "sink")}
    mask_d = nc.dram_tensor("mask8", [NKT, 128, TPC], mybir.dt.int8, kind="ExternalInput").ap()
    ones_bf_d = nc.dram_tensor("ones_bf", [128, 1], BF16, kind="ExternalInput").ap()
    ones1_d = nc.dram_tensor("ones1", [1, 128], F32, kind="ExternalInput").ap()
    consts_d = nc.dram_tensor("consts", [128, 8], F32, kind="ExternalInput").ap()
    o_out = nc.dram_tensor("o_out", [D, TPC], F32, kind="ExternalOutput").ap()

    g_b = nc.dram_tensor("g_b", [1, 8], F32).ap()
    g_ar = nc.dram_tensor("g_ar", [1, 8], F32).ap()
    ag_in = {}
    ag_out = {}
    for (nm, rows, cols, _) in QSPEC_F:
        ag_in[nm] = nc.dram_tensor("agi_" + nm, [rows, cols], BF16).ap()
        ag_out[nm] = nc.dram_tensor("ago_" + nm, [NCORE * rows, cols], BF16).ap()
    for (nm, rows, cols) in CSPEC_F:
        ag_in[nm] = nc.dram_tensor("agi_" + nm, [rows, cols], BF16).ap()
        ag_out[nm] = nc.dram_tensor("ago_" + nm, [NCORE * rows, cols], BF16).ap()
    du_cat = nc.dram_tensor("du_cat", [4 * TPC, TPC], BF16).ap()
    du_ag = nc.dram_tensor("du_ag", [16 * TPC, TPC], BF16).ap()
    bc_in = nc.dram_tensor("bc_in", [64, TPC], BF16).ap()
    bc_ag = nc.dram_tensor("bc_ag", [256, TPC], BF16).ap()
    y_in = nc.dram_tensor("y_in", [D, TPC], BF16).ap()
    y_ag = nc.dram_tensor("y_ag", [4 * D, TPC], BF16).ap()
    q_loc = nc.dram_tensor("q_loc", [D, TPC], BF16).ap()
    k_in = nc.dram_tensor("k_in", [D, TPC], BF16).ap()
    k_ag = nc.dram_tensor("k_ag", [4 * D, TPC], BF16).ap()
    v_in = nc.dram_tensor("v_in", [D, TPC], BF16).ap()
    v_ag = nc.dram_tensor("v_ag", [4 * D, TPC], BF16).ap()
    x2_loc = nc.dram_tensor("x2_loc", [D, TPC], F32).ap()
    x3_loc = nc.dram_tensor("x3_loc", [D, TPC], F32).ap()

    r = Rec()
    rst = {}

    def dyn_expr(e, key, build):
        k = (e.engine, key)
        if k not in rst:
            rst[k] = build(e)
        return rst[k]

    def jc_of(e):
        return dyn_expr(e, "jc", lambda e: e.scalar_reg_alu(
            ALU.mod, e.partition_id(), 4))

    with contextlib.ExitStack() as stack:
        ec = stack.enter_context
        sb = {
            "stats": ec(nc.sbuf_tensor("stats", [1, 3 * TPC], F32)),
            "sqt": ec(nc.sbuf_tensor("sqt", [128, 512], BF16)),
            "tmp": ec(nc.sbuf_tensor("tmpG", [128, 512], F32)),
            "ones_bf": ec(nc.sbuf_tensor("onesbf", [128, 1], BF16)),
            "ones1": ec(nc.sbuf_tensor("ones1s", [1, 128], F32)),
            "consts": ec(nc.sbuf_tensor("constsS", [128, 8], F32)),
        }
        ps = {
            "sts": ec(nc.psum_tensor("psts", [1, 512], F32)),
            "stq": ec(nc.psum_tensor("pstq", [1, 512], F32)),
            "p0": ec(nc.psum_tensor("p0", [128, 512], F32)),
            "p1": ec(nc.psum_tensor("p1", [128, 512], F32)),
            "p2": ec(nc.psum_tensor("p2", [128, 512], F32)),
            "p3": ec(nc.psum_tensor("p3", [128, 512], F32)),
            "row": ec(nc.psum_tensor("prow", [1, 512], F32)),
        }
        ps["mb"] = ps["p0"]
        ps["rb"] = ps["p1"]
        sems = _mk_sems(nc, stack)
        cons = sb["consts"]

        wstack = contextlib.ExitStack()
        wbc_sb = wstack.enter_context(nc.sbuf_tensor("wbc", [128, DC * 128], BF16))
        wdt_sb = wstack.enter_context(nc.sbuf_tensor("wdt", [DT_RANK, D], BF16))
        wo1 = r.dma("sync", sb["ones_bf"][:], ones_bf_d)
        wo2 = r.dma("sync", sb["ones1"][:], ones1_d)
        wcn = r.dma("sync", sb["consts"][:], consts_d)
        waux = ("ld", r.cnt["ld"])
        ww1 = r.dma("gpsimd", wbc_sb[:].rearrange("p (c o) -> p c o", o=128),
                    wbcT.rearrange("(c p) o -> p c o", p=128))
        ww2 = r.dma("gpsimd", wdt_sb[:], wdtT)

        def phase_Q():
            with contextlib.ExitStack() as qs:
                qc = qs.enter_context
                wt = qc(nc.sbuf_tensor("wt", [128, H1], F32))
                qt = qc(nc.sbuf_tensor("qt", [128, H1], BF16))
                q2 = qc(nc.sbuf_tensor("q2", [128, H1], BF16))
                pacc = qc(nc.sbuf_tensor("pacc", [128, 16], F32))
                redc = qc(nc.sbuf_tensor("redc", [128, 1], F32))
                grow = qc(nc.sbuf_tensor("grow", [1, 16], F32))
                thb = qc(nc.sbuf_tensor("thb", [128, 8], F32))
                nthb = qc(nc.sbuf_tensor("nthb", [128, 8], F32))
                onescol = qc(nc.sbuf_tensor("onescol", [128, 1], F32))

                wic = r.dma("sync", grow[0:1, 8:16], invcnt_d)
                wsel = r.dma("sync", thb[0:1, 0:8], sel_d)
                wmo = r.op("vector", lambda e: e.memset(pacc[:], 0.0), waits=[])
                woc = r.op("vector", lambda e: e.memset(onescol[:], 1.0), waits=[])

                def tiles_of(rows):
                    out_ = []
                    p = 0
                    while p < rows:
                        out_.append((p, min(128, rows - p)))
                        p += 128
                    return out_

                prev = wmo
                wmm_dep = []
                for wi, (nm, rows, cols, _) in enumerate(QSPEC_F):
                    for ti, (p0_, np_) in enumerate(tiles_of(rows)):
                        wl = r.dma("sync", wt[0:np_, 0:cols], wsh_in[nm][p0_:p0_ + np_, :], waits=[prev])
                        prev = r.op("vector", lambda e, np_=np_, cols=cols, ti=ti: e.reduce_sum(
                            out=pacc[0:np_, ti + 1:ti + 2], in_=wt[0:np_, 0:cols],
                            axis=mybir.AxisListType.X, apply_absolute_value=True), waits=[wl])
                    prev = r.op("vector", lambda e: e.reduce_sum(
                        out=redc[:], in_=pacc[:, 0:16], axis=mybir.AxisListType.X), waits=[prev])
                    wq = r.op("tensor", lambda e, wi=wi: e.matmul(
                        ps["row"][0:1, wi:wi + 1], lhsT=onescol[:], rhs=redc[:],
                        start=True, stop=True), waits=[prev, woc])
                    wmm_dep.append(wq)
                    prev = r.op("vector", lambda e: e.memset(pacc[:], 0.0), waits=[wq])
                wmg = r.op("vector", lambda e: e.memset(grow[0:1, 0:8], 1.0), waits=[])
                wg = r.op("scalar", lambda e: e.copy(out=grow[0:1, 0:5], in_=ps["row"][0:1, 0:5]), waits=wmm_dep + [wmg])
                wgd = r.dma("gpsimd", g_b, grow[0:1, 0:8], waits=[wg])
                war = r.collective("AllReduce", ALU.add, GRP8, [g_b.opt()], [g_ar.opt()], waits=[wgd])
                wgb = r.dma("sync", grow[0:1, 0:8], g_ar, waits=[war])

                wm1 = r.op("vector", lambda e: e.tensor_mul(out=grow[0:1, 0:8], in0=grow[0:1, 0:8], in1=grow[0:1, 8:16]), waits=[wgb, wic])
                wa_ = r.op("vector", lambda e: e.tensor_scalar_add(out=grow[0:1, 8:16], in0=grow[0:1, 0:8], scalar1=1e-8), waits=[wm1])
                wb_ = r.op("vector", lambda e: e.tensor_scalar_max(out=grow[0:1, 0:8], in0=grow[0:1, 0:8], scalar1=1e-8), waits=[wa_])
                wbl1 = r.op("vector", lambda e: e.tensor_sub(out=grow[0:1, 8:16], in0=grow[0:1, 8:16], in1=grow[0:1, 0:8]), waits=[wa_, wb_])
                wbl2 = r.op("vector", lambda e: e.tensor_mul(out=grow[0:1, 8:16], in0=grow[0:1, 8:16], in1=thb[0:1, 0:8]), waits=[wbl1, wsel])
                wbl3 = r.op("vector", lambda e: e.tensor_add(out=grow[0:1, 0:8], in0=grow[0:1, 0:8], in1=grow[0:1, 8:16]), waits=[wbl2])
                wth = r.op("vector", lambda e: e.tensor_scalar_mul(out=grow[0:1, 0:8], in0=grow[0:1, 0:8], scalar1=0.5), waits=[wbl3])
                wtb = r.op("tensor", lambda e: e.matmul(ps["p2"][0:128, 0:8], lhsT=sb["ones1"][:], rhs=grow[0:1, 0:8], start=True, stop=True), waits=[wth, waux])
                wtc = r.op("vector", lambda e: e.tensor_copy(out=thb[:], in_=ps["p2"][0:128, 0:8]), waits=[wtb])
                wtn = r.op("vector", lambda e: e.tensor_scalar_mul(out=nthb[:], in0=thb[:], scalar1=-1.0), waits=[wtc])

                prev_q = wtn
                prev_out = None
                ag_store = {}
                for wi, (nm, rows, cols, _) in enumerate(QSPEC_F):
                    for (p0_, np_) in tiles_of(rows):
                        wl = r.dma("sync", wt[0:np_, 0:cols], wsh_in[nm][p0_:p0_ + np_, :],
                                   waits=[prev_q] + ([prev_out] if prev_out else []))
                        w1w = [wl]
                        w1_ = r.op("vector", lambda e, np_=np_, cols=cols, wi=wi: e.tensor_scalar(
                            out=qt[0:np_, 0:cols], in0=wt[0:np_, 0:cols], scalar1=thb[0:np_, wi:wi + 1],
                            scalar2=None, op0=ALU.is_ge), waits=w1w)
                        w2_ = r.op("vector", lambda e, np_=np_, cols=cols, wi=wi: e.tensor_scalar(
                            out=q2[0:np_, 0:cols], in0=wt[0:np_, 0:cols], scalar1=nthb[0:np_, wi:wi + 1],
                            scalar2=None, op0=ALU.is_le), waits=[])
                        w3_ = r.op("vector", lambda e, np_=np_, cols=cols: e.tensor_sub(
                            out=qt[0:np_, 0:cols], in0=qt[0:np_, 0:cols], in1=q2[0:np_, 0:cols]), waits=[w1_, w2_])
                        prev_out = r.dma("sync", ag_in[nm][p0_:p0_ + np_, :], qt[0:np_, 0:cols], waits=[w3_])
                        prev_q = w3_
                    ag_store[nm] = prev_out
                for (nm, rows, cols) in CSPEC_F:
                    for (p0_, np_) in tiles_of(rows):
                        wl = r.dma("sync", wt[0:np_, 0:cols], wsh_in[nm][p0_:p0_ + np_, :],
                                   waits=[prev_q] + ([prev_out] if prev_out else []))
                        cw = [wl]
                        prev_q = r.op("vector", lambda e, np_=np_, cols=cols: e.tensor_copy(
                            out=qt[0:np_, 0:cols], in_=wt[0:np_, 0:cols]), waits=cw)
                        prev_out = r.dma("sync", ag_in[nm][p0_:p0_ + np_, :], qt[0:np_, 0:cols], waits=[prev_q])
                    ag_store[nm] = prev_out
            return ag_store
        ag_store = phase_Q()
        ag_waits = {}

        def phase_A():
            r.wait("sync", r.fence())
            with contextlib.ExitStack() as asx:
                ac = asx.enter_context
                x_tiles = [ac(nc.sbuf_tensor(f"ax{c}", [128, TPC], F32)) for c in range(DC)]
                xl_tiles = [ac(nc.sbuf_tensor(f"axl{c}", [128, TPC], BF16)) for c in range(DC)]
                bdt_sb = ac(nc.sbuf_tensor("bdtS", [128, DC], F32))
                dtr_sb = ac(nc.sbuf_tensor("dtr", [DT_RANK, TPC], BF16))
                bc_sb = ac(nc.sbuf_tensor("bcS", [64, TPC], BF16))
                delta_sb = [ac(nc.sbuf_tensor(f"adl{c}", [128, TPC], BF16)) for c in range(DC)]
                t_abs = ac(nc.sbuf_tensor("tabs", [128, 512], F32))
                t_rel = ac(nc.sbuf_tensor("trel", [128, 512], F32))
                ps_db = [ps["p2"], ps["p3"]]
                ps_dt = [ps["p0"], ps["p1"]]

                wx = [r.dma("sync", x_tiles[c][:], xT[128 * c:128 * (c + 1), :]) for c in range(DC)]
                wb = r.dma("sync", bdt_sb[:], bdt)

                wln = emit_ln(r, sb, ps, x_tiles, xl_tiles, TPC, deps=[wx[DC - 1], waux])

                ev_dtr = [None, None]
                for h in range(2):
                    sl = slice(512 * h, 512 * (h + 1))
                    for c in range(DC):
                        r.op("tensor", lambda e, c=c, h=h, sl=sl: e.matmul(
                            ps_db[h][:], lhsT=wbc_sb[:, 128 * c:128 * (c + 1)], rhs=xl_tiles[c][:, sl],
                            start=(c == 0), stop=(c == DC - 1)),
                            waits=[wln, ww1] if c == 0 else [])
                    wt_ = ("te", r.cnt["te"])
                    ev_dtr[h] = r.op("scalar", lambda e, h=h, sl=sl: e.copy(out=dtr_sb[:, sl], in_=ps_db[h][0:64, :]), waits=[wt_])
                    r.op("scalar", lambda e, h=h, sl=sl: e.copy(out=bc_sb[:, sl], in_=ps_db[h][64:128, :]), waits=[])
                wbc_done = ("sc", r.cnt["sc"])
                wbcst = r.dma("sync", bc_in, bc_sb[:], waits=[wbc_done])
                wag_bc = r.collective("AllGather", ALU.bypass, GRP4, [bc_in.opt()], [bc_ag.opt()], waits=[wbcst])

                prev_psum_reader = [None, None]
                prev_vadd = None
                du_st = None
                for c in range(DC):
                    for h in range(2):
                        sl = slice(512 * h, 512 * (h + 1))
                        w0 = [ev_dtr[h], ww2, wb]
                        if prev_psum_reader[h] is not None:
                            w0.append(prev_psum_reader[h])
                        wmm = r.op("tensor", lambda e, c=c, h=h, sl=sl: e.matmul(
                            ps_dt[h][:], lhsT=wdt_sb[:, 128 * c:128 * (c + 1)], rhs=dtr_sb[:, sl],
                            start=True, stop=True), waits=w0)
                        wa_w = [wmm] + ([prev_vadd] if prev_vadd is not None else [])
                        wa = r.op("scalar", lambda e, c=c, h=h: e.activation(
                            out=t_abs[:], in_=ps_dt[h][:], func=AF.Abs, bias=bdt_sb[:, c:c + 1]), waits=wa_w)
                        wr2 = r.op("scalar", lambda e, c=c, h=h: e.activation(
                            out=t_rel[:], in_=ps_dt[h][:], func=AF.Relu, bias=bdt_sb[:, c:c + 1]), waits=[])
                        prev_psum_reader[h] = wr2
                        we = r.op("scalar", lambda e: e.activation(out=t_abs[:], in_=t_abs[:], func=AF.Exp, scale=cons[:, C_NEG1:C_NEG1 + 1]), waits=[wa])
                        wl = r.op("scalar", lambda e: e.activation(out=t_abs[:], in_=t_abs[:], func=AF.Ln, bias=cons[:, C_ONE:C_ONE + 1]), waits=[we])
                        prev_vadd = r.op("vector", lambda e, c=c, sl=sl: e.tensor_add(
                            out=delta_sb[c][:, sl], in0=t_rel[:], in1=t_abs[:]), waits=[wl, wr2])
                    row = 1024 * (c // 4) + 128 * (c % 4)
                    du_st = r.dma("sync", du_cat[row:row + 128, :], delta_sb[c][:], waits=[prev_vadd])
                    du_st = r.dma("sync", du_cat[row + 512:row + 640, :], xl_tiles[c][:], waits=[])
                wag_du = r.collective("AllGather", ALU.bypass, GRP4, [du_cat.opt()], [du_ag.opt()], waits=[du_st])
                for nm in ag_store:
                    ag_waits[nm] = r.collective("AllGather", ALU.bypass, GRP8,
                                                [ag_in[nm].opt()], [ag_out[nm].opt()],
                                                waits=[ag_store[nm]])
            return wag_bc, wag_du
        if upto >= "A":
            wag_bc, wag_du = phase_A()

        def phase_B():
            r.wait("sync", r.fence())
            with contextlib.ExitStack() as bs:
                bcx = bs.enter_context
                dA = bcx(nc.sbuf_tensor("dA", [128, D_STATE * TC1], BF16))
                dBu = bcx(nc.sbuf_tensor("dBu", [128, D_STATE * TC1], BF16))
                hh = bcx(nc.sbuf_tensor("hh", [128, D_STATE * TC1], BF16))
                brep = bcx(nc.sbuf_tensor("brep", [128, D_STATE * TC], BF16))
                crep = bcx(nc.sbuf_tensor("crep", [128, D_STATE * TC], BF16))
                dlu8 = bcx(nc.sbuf_tensor("dlu8", [128, 8 * TPC], BF16))
                du_sb = bcx(nc.sbuf_tensor("du", [128, TC], BF16))
                y_sb = bcx(nc.sbuf_tensor("yS", [128, TC], BF16))
                carry = bcx(nc.sbuf_tensor("carry", [128, D_STATE * NDT], BF16))
                alog_sb = bcx(nc.sbuf_tensor("alogS", [128, 128], F32))
                a_sb = bcx(nc.sbuf_tensor("aS", [128, 128], F32))
                dpar_sb = bcx(nc.sbuf_tensor("dparS", [128, NDT], F32))

                dA3 = dA[:].rearrange("p (n t) -> p n t", t=TC1)
                dBu3 = dBu[:].rearrange("p (n t) -> p n t", t=TC1)
                hh3 = hh[:].rearrange("p (n t) -> p n t", t=TC1)
                brep3 = brep[:].rearrange("p (n t) -> p n t", t=TC)
                crep3 = crep[:].rearrange("p (n t) -> p n t", t=TC)

                wal = r.dma("sync", alog_sb[:], alog_d)
                wdp = r.dma("sync", dpar_sb[:], dpar_d)
                wexp0 = r.op("scalar", lambda e: e.activation(out=a_sb[:], in_=alog_sb[:], func=AF.Exp), waits=[wal])
                wA = r.op("vector", lambda e: e.tensor_scalar_mul(out=a_sb[:], in0=a_sb[:], scalar1=-1.0), waits=[wexp0])

                wpad1 = r.op("vector", lambda e: e.memset(dA3[:, :, TC + 1:TC + 2], 0.0), waits=[])
                wpad2 = r.op("vector", lambda e: e.memset(dBu3[:, :, TC + 1:TC + 2], 0.0), waits=[wpad1])
                wpad3 = r.op("vector", lambda e: e.memset(hh3[:, :, TC + 1:TC + 2], 0.0), waits=[wpad2])
                prev_ve = wpad3
                prev_sc = wexp0
                prev_ystore = None
                ds_ = bass.ds
                for ch in range(NCH):
                    tsl = slice(TC * ch, TC * (ch + 1))
                    rnk = ch // 2
                    csl = slice(512 * (ch % 2), 512 * (ch % 2 + 1))
                    wbr = r.dma("sync", brep3,
                                bc_ag[64 * rnk:64 * rnk + 32, csl].unsqueeze(0).broadcast_to([128, 32, TC]),
                                waits=[prev_ve, wag_bc])
                    wcr = r.dma("sync", crep3,
                                bc_ag[64 * rnk + 32:64 * rnk + 64, csl].unsqueeze(0).broadcast_to([128, 32, TC]),
                                waits=[])
                    if ch % 2 == 0:
                        def mk_dma(waits, rnk=rnk):
                            sem = "ld"
                            r.cnt[sem] += 16
                            tgt = r.cnt[sem]
                            waits2 = tuple((s, r.cnt[s]) if s in r.DMA_SEMS else (s, v) for (s, v) in waits)
                            def op(e, sems, waits2=waits2, tgt=tgt):
                                for (s, v) in waits2:
                                    e.wait_ge(sems[s], v)
                                off = dyn_expr(e, ("du", rnk), lambda e: e.scalar_reg_alu(
                                    ALU.add, e.scalar_reg_alu(ALU.mult, jc_of(e), 1024), 4096 * rnk))
                                e.dma_start(
                                    out=dlu8[:].rearrange("p (blk t) -> p blk t", t=TPC),
                                    in_=du_ag[ds_(off, 1024), :].rearrange("(blk p) t -> p blk t", p=128),
                                ).then_inc(sems[sem], 16)
                            r.ops["sync"].append(op)
                            return (sem, tgt)
                        wdu8 = mk_dma([prev_sc, prev_ve, wag_du])
                    for dt in range(NDT):
                        dl_sb = dlu8[:, dt * TPC + 512 * (ch % 2): dt * TPC + 512 * (ch % 2) + TC]
                        u_sb = dlu8[:, (4 + dt) * TPC + 512 * (ch % 2): (4 + dt) * TPC + 512 * (ch % 2) + TC]
                        wld = wdu8
                        wlu = wdu8
                        wexps = None
                        for n in range(D_STATE):
                            wexps = r.op("scalar", lambda e, n=n, dt=dt, dl_sb=dl_sb: e.activation(
                                out=dA[:, n * TC1 + 1: n * TC1 + 1 + TC], in_=dl_sb,
                                func=AF.Exp, scale=a_sb[:, dt * 32 + n: dt * 32 + n + 1]),
                                waits=[wld, prev_ve, wA] if n == 0 else [])
                        wms = r.op("vector", lambda e: e.memset(dA3[:, :, 0:1], 0.0), waits=[prev_ve] if ch + dt else [])
                        wdu = r.op("vector", lambda e, dl_sb=dl_sb, u_sb=u_sb: e.tensor_mul(
                            out=du_sb[:], in0=dl_sb, in1=u_sb), waits=[wld, wlu])
                        wdb = r.op("vector", lambda e: e.tensor_tensor(
                            out=dBu3[:, :, 1:TC + 1], in0=du_sb[:].unsqueeze(1).broadcast_to([128, 32, TC]),
                            in1=brep3, op=ALU.mult), waits=[wbr, wdu])
                        if ch == 0:
                            wcc = r.op("vector", lambda e, dt=dt: e.memset(dBu3[:, :, 0:1], 0.0), waits=[])
                        else:
                            wcc = r.op("vector", lambda e, dt=dt: e.tensor_copy(
                                out=dBu3[:, :, 0:1], in_=carry[:, dt * 32:(dt + 1) * 32].unsqueeze(2)), waits=[])
                        wsc_ = r.op("vector", lambda e: e.tensor_tensor_scan(
                            out=hh[:], data0=dA[:], data1=dBu[:], initial=0.0,
                            op0=ALU.mult, op1=ALU.add), waits=[wexps, wms, wdb, wcc])
                        wco = r.op("vector", lambda e, dt=dt: e.tensor_copy(
                            out=carry[:, dt * 32:(dt + 1) * 32].unsqueeze(2), in_=hh3[:, :, TC:TC + 1]), waits=[wsc_])
                        whc = r.op("vector", lambda e: e.tensor_tensor(
                            out=dA3[:, :, 1:TC + 1], in0=hh3[:, :, 1:TC + 1], in1=crep3, op=ALU.mult), waits=[wcr, wsc_])
                        wtr = whc
                        for half in (16, 8, 4, 2, 1):
                            wtr = r.op("vector", lambda e, half=half: e.tensor_tensor(
                                out=dA3[:, 0:half, 1:TC + 1], in0=dA3[:, 0:half, 1:TC + 1],
                                in1=dA3[:, half:2 * half, 1:TC + 1], op=ALU.add), waits=[wtr])
                        yw = [wdp, wtr] + ([prev_ystore] if prev_ystore else [])
                        wy = r.op("vector", lambda e, dt=dt, u_sb=u_sb: e.scalar_tensor_tensor(
                            out=y_sb[:], in0=u_sb, scalar=dpar_sb[:, dt:dt + 1],
                            in1=dA[:, 1:TC + 1], op0=ALU.mult, op1=ALU.add), waits=yw)
                        yrow = 512 * rnk + 128 * dt
                        prev_ystore = r.dma("sync", y_in[yrow:yrow + 128, csl], y_sb[:], waits=[wy])
                        prev_ve = wy
                        prev_sc = wexps
                wag_y = r.collective("AllGather", ALU.bypass, GRP4, [y_in.opt()], [y_ag.opt()], waits=[prev_ystore])

            return wag_y
        wstack.close()
        if upto >= "B":
            wag_y = phase_B()

        def phase_C():
            r.wait("sync", r.fence())
            with contextlib.ExitStack() as cs:
                cc = cs.enter_context
                x_tiles = [cc(nc.sbuf_tensor(f"cx{c}", [128, TPC], F32)) for c in range(DC)]
                xlbig = cc(nc.sbuf_tensor("cxlbig", [128, DC * TPC], BF16))
                xl_tiles = [xlbig[:, TPC * c:TPC * (c + 1)] for c in range(DC)]
                hid = [cc(nc.sbuf_tensor(f"chid{f}", [128, 512], BF16)) for f in range(NF0)]
                w1s = [cc(nc.sbuf_tensor(f"cw1s{p}", [128, D], BF16)) for p in range(2)]
                w2s = [cc(nc.sbuf_tensor(f"cw2s{p}", [128, D], BF16)) for p in range(2)]
                w3s = [cc(nc.sbuf_tensor(f"cw3s{p}", [128, H0], BF16)) for p in range(2)]
                sg2 = [cc(nc.sbuf_tensor(f"csg{p}", [128, 512], BF16)) for p in range(2)]
                px1b = [ps["p2"], ps["p0"]]
                px2b = [ps["p3"], ps["p1"]]

                wx = [r.dma("sync", x_tiles[c][:], xT[128 * c:128 * (c + 1), :]) for c in range(DC)]
                wy_l = [None] * DC
                for i in range(4):
                    def mk_ydma(i=i):
                        sem = "ld"
                        r.cnt[sem] += 16
                        tgt = r.cnt[sem]
                        def op(e, sems, i=i, tgt=tgt):
                            e.wait_ge(sems["cc"], wag_y[1])
                            off = dyn_expr(e, ("y", i), lambda e: e.scalar_reg_alu(
                                ALU.add, e.scalar_reg_alu(ALU.mult, jc_of(e), 512), 2048 * i))
                            e.dma_start(
                                out=xlbig[:, 4 * TPC * i:4 * TPC * (i + 1)].rearrange("p (c t) -> p c t", t=TPC),
                                in_=y_ag[bass.ds(off, 512), :].rearrange("(c p) t -> p c t", p=128),
                            ).then_inc(sems[sem], 16)
                        r.ops["sync"].append(op)
                        return (sem, tgt)
                    w_ = mk_ydma()
                    for cdx in range(4 * i, 4 * (i + 1)):
                        wy_l[cdx] = w_
                wadd = None
                for c in range(DC):
                    wadd = r.op("vector", lambda e, c=c: e.tensor_tensor(
                        out=x_tiles[c][:], in0=x_tiles[c][:], in1=xl_tiles[c][:], op=ALU.add),
                        waits=[wx[c], wy_l[c]])
                wln = emit_ln(r, sb, ps, x_tiles, xl_tiles, TPC, deps=[wadd])

                rd1 = [None, None]
                rd2 = [None, None]
                rd3 = [None, None]
                sig_hist = [None, None]
                hct_hist = [None, None]
                radd_hist = [None, None]
                prev_sig = None
                prev_hct = None
                prev_out_te = None
                prev_radd = None
                for h in range(2):
                    sl = slice(512 * h, 512 * (h + 1))
                    for f in range(NF0):
                        par = f % 2
                        wd1 = r.dma("sync", w1s[par][:], ag_out["w1f"][128 * f:128 * (f + 1), :],
                                    waits=[w for w in (rd1[par], ag_waits["w1f"]) if w], sem="t0" if par == 0 else "t1")
                        wd2 = r.dma("sync", w2s[par][:], ag_out["w2f"][128 * f:128 * (f + 1), :],
                                    waits=[w for w in (rd2[par], ag_waits["w2f"]) if w], sem="t2" if par == 0 else "t3")
                        w0 = [wd1, wln] + ([sig_hist[par]] if sig_hist[par] else []) + ([radd_hist[par]] if radd_hist[par] else [])
                        for c in range(DC):
                            r.op("tensor", lambda e, c=c, par=par, sl=sl: e.matmul(
                                px1b[par][:], lhsT=w1s[par][:, 128 * c:128 * (c + 1)], rhs=xl_tiles[c][:, sl],
                                start=(c == 0), stop=(c == DC - 1)), waits=w0 if c == 0 else [], inc=(c == DC - 1))
                        rd1[par] = ("te", r.cnt["te"])
                        wte1 = rd1[par]
                        w0 = [wd2] + ([hct_hist[par]] if hct_hist[par] else [])
                        for c in range(DC):
                            r.op("tensor", lambda e, c=c, par=par, sl=sl: e.matmul(
                                px2b[par][:], lhsT=w2s[par][:, 128 * c:128 * (c + 1)], rhs=xl_tiles[c][:, sl],
                                start=(c == 0), stop=(c == DC - 1)), waits=w0 if c == 0 else [], inc=(c == DC - 1))
                        rd2[par] = ("te", r.cnt["te"])
                        wte2 = rd2[par]
                        sgw = [wte1] + ([hct_hist[par]] if hct_hist[par] else [])
                        prev_sig = r.op("scalar", lambda e, par=par: e.activation(
                            out=sg2[par][:], in_=px1b[par][:], func=AF.Sigmoid), waits=sgw)
                        sig_hist[par] = prev_sig
                        hw = [wte2, prev_sig] + ([prev_out_te] if (prev_out_te and h == 1) else [])
                        prev_hct = r.op("vector", lambda e, f=f, par=par: e.tensor_tensor(
                            out=hid[f][:], in0=px2b[par][:], in1=sg2[par][:], op=ALU.mult), waits=hw)
                        hct_hist[par] = prev_hct
                    whid = prev_hct
                    for dci in range(DC):
                        par = dci % 2
                        wd3 = r.dma("sync", w3s[par][:], ag_out["w3f"][128 * dci:128 * (dci + 1), :],
                                    waits=[w for w in (rd3[par], ag_waits["w3f"]) if w],
                                    sem="t4" if par == 0 else "t5")
                        w0 = [wd3, whid] + ([sig_hist[par]] if sig_hist[par] else []) + ([radd_hist[par]] if radd_hist[par] else [])
                        for k in range(NF0):
                            r.op("tensor", lambda e, k=k, par=par: e.matmul(
                                px1b[par][:], lhsT=w3s[par][:, 128 * k:128 * (k + 1)], rhs=hid[k][:],
                                start=(k == 0), stop=(k == NF0 - 1)), waits=w0 if k == 0 else [], inc=(k == NF0 - 1))
                        rd3[par] = ("te", r.cnt["te"])
                        prev_out_te = rd3[par]
                        prev_radd = r.op("vector", lambda e, dci=dci, sl=sl, par=par: e.tensor_tensor(
                            out=x_tiles[dci][:, sl], in0=x_tiles[dci][:, sl], in1=px1b[par][:], op=ALU.add),
                            waits=[rd3[par]])
                        radd_hist[par] = prev_radd
                        sig_hist[par] = prev_radd
                x2st = None
                for c in range(DC):
                    x2st = r.dma("sync", x2_loc[128 * c:128 * (c + 1), :], x_tiles[c][:],
                                 waits=[prev_radd] if c == 0 else [])
            return x2st
        if upto >= "C":
            x2st = phase_C()

        def phase_D():
            r.wait("sync", r.fence())
            with contextlib.ExitStack() as dsx:
                dcx = dsx.enter_context
                xl_tiles = [dcx(nc.sbuf_tensor(f"dxl{c}", [128, TPC], BF16)) for c in range(DC)]
                wsl = [dcx(nc.sbuf_tensor(f"dwsl{p}", [128, D], BF16)) for p in range(2)]
                bias_sb = {nm: dcx(nc.sbuf_tensor(nm + "S", [128, DC], F32)) for nm in ("bq", "bk")}
                pqb = [ps["p2"], ps["p0"]]

                with contextlib.ExitStack() as dxs:
                    x_tiles = [dxs.enter_context(nc.sbuf_tensor(f"dx{c}", [128, TPC], F32)) for c in range(DC)]
                    wx = [r.dma("sync", x_tiles[c][:], x2_loc[128 * c:128 * (c + 1), :], waits=[x2st]) for c in range(DC)]
                    wln = emit_ln(r, sb, ps, x_tiles, xl_tiles, TPC, deps=[wx[DC - 1], waux])

                av_res = [dcx(nc.sbuf_tensor(f"dav{c}", [128, D], BF16)) for c in range(DC)]
                pt = dcx(nc.sbuf_tensor("dpt", [128, TPC], BF16))
                rt = dcx(nc.sbuf_tensor("drt", [128, TPC], BF16))
                rt2 = dcx(nc.sbuf_tensor("drt2", [128, TPC], BF16))
                vt_t = dcx(nc.sbuf_tensor("dvt", [128, D], BF16))
                bvr = dcx(nc.sbuf_tensor("dbvr", [128, D], BF16))
                bvrow_sb = dcx(nc.sbuf_tensor("dbvrow", [1, D], F32))
                tabs = {nm: dcx(nc.sbuf_tensor(nm + "S", [128, TPC], BF16))
                        for nm in ("cosq", "sinq", "cosk", "sink")}
                for nm, d_ in (("cosq", rope_d["cosq"]), ("sinq", rope_d["sinq"]),
                               ("cosk", rope_d["cosk"]), ("sink", rope_d["sink"])):
                    r.dma("sync", tabs[nm][:], d_, waits=[wln])
                r.dma("sync", bias_sb["bq"][:], bq_d)
                r.dma("sync", bias_sb["bk"][:], bk_d)
                wbv = r.dma("sync", bvrow_sb[:], bvrow_d)
                wavl = [r.dma("sync", av_res[c][:], ag_out["avf"][128 * c:128 * (c + 1), :],
                              waits=[ag_waits["avf"]] if c == 0 else []) for c in range(DC)]
                wtabs2 = ("ld", r.cnt["ld"])
                wbvb = None
                for j in range(4):
                    wmmb = r.op("tensor", lambda e, j=j: e.matmul(
                        ps["p3"][:], lhsT=sb["ones1"][:], rhs=bvrow_sb[0:1, 512 * j:512 * (j + 1)],
                        start=True, stop=True), waits=[wbv, waux] + ([wbvb] if wbvb else []))
                    wbvb = r.op("vector", lambda e, j=j: e.tensor_copy(
                        out=bvr[:, 512 * j:512 * (j + 1)], in_=ps["p3"][:]), waits=[wmmb])
                wln = wtabs2

                rdw = [None, None]
                ev_hist = [wbvb, None]
                prev_ev = wbvb
                prev_store = None
                projs = [("q", "aqf", "bq", q_loc, "cosq", "sinq"),
                         ("k", "akf", "bk", k_in, "cosk", "sink")]
                fidx = 0
                gidx = 0
                for (pname, wnm, bnm, odst, cosn, sinn) in projs:
                    for f in range(DC):
                        par = fidx % 2
                        wd = r.dma("sync", wsl[par][:], ag_out[wnm][128 * f:128 * (f + 1), :],
                                   waits=[w for w in (rdw[par], ag_waits[wnm]) if w],
                                   sem="t0" if par == 0 else "t1")
                        for h in range(2):
                            bpar = gidx % 2
                            gidx += 1
                            sl = slice(512 * h, 512 * (h + 1))
                            w0 = [wd, wtabs2] + ([ev_hist[bpar]] if ev_hist[bpar] else [])
                            for c in range(DC):
                                r.op("tensor", lambda e, c=c, par=par, sl=sl, bpar=bpar: e.matmul(
                                    pqb[bpar][:], lhsT=wsl[par][:, 128 * c:128 * (c + 1)], rhs=xl_tiles[c][:, sl],
                                    start=(c == 0), stop=(c == DC - 1)), waits=w0 if c == 0 else [], inc=(c == DC - 1))
                            rdw[par] = ("te", r.cnt["te"])
                            evw = [rdw[par]] + ([prev_store] if prev_store else [])
                            prev_ev = r.op("scalar", lambda e, sl=sl, bnm=bnm, f=f, bpar=bpar: e.activation(
                                out=pt[:, sl], in_=pqb[bpar][:], func=AF.Identity,
                                bias=bias_sb[bnm][:, f:f + 1]), waits=evw)
                            ev_hist[bpar] = prev_ev
                        wc1 = r.op("vector", lambda e: e.tensor_copy(out=rt[0:64, :], in_=pt[64:128, :]), waits=[prev_ev])
                        wc2 = r.op("vector", lambda e: e.tensor_copy(out=rt[64:128, :], in_=pt[0:64, :]), waits=[prev_ev])
                        wm1 = r.op("vector", lambda e, cosn=cosn: e.tensor_tensor(
                            out=pt[:], in0=pt[:], in1=tabs[cosn][:], op=ALU.mult), waits=[wc1, wc2])
                        wm2 = r.op("vector", lambda e, sinn=sinn: e.tensor_tensor(
                            out=rt2[:], in0=rt[:], in1=tabs[sinn][:], op=ALU.mult), waits=[wm1])
                        wad = r.op("vector", lambda e: e.tensor_tensor(
                            out=pt[:], in0=pt[:], in1=rt2[:], op=ALU.add), waits=[wm2])
                        prev_ev = wad
                        prev_store = r.dma("sync", odst[128 * f:128 * (f + 1), :], pt[:], waits=[prev_ev])
                        fidx += 1
                wag_k = r.collective("AllGather", ALU.bypass, GRP4, [k_in.opt()], [k_ag.opt()], waits=[prev_store])
                v_view = v_in.rearrange("(h p) (kt i) -> p h kt i", p=128, i=128)
                prev_vst = None
                for tc in range(8):
                    for j in range(4):
                        bpar = gidx % 2
                        gidx += 1
                        jsl = slice(512 * j, 512 * (j + 1))
                        w0 = [wavl[DC - 1]] + ([ev_hist[bpar]] if ev_hist[bpar] else [])
                        if tc == 0 and j == 0 and prev_ev:
                            w0.append(prev_ev)
                        for c in range(DC):
                            r.op("tensor", lambda e, c=c, tc=tc, jsl=jsl, bpar=bpar: e.matmul(
                                pqb[bpar][:], lhsT=xl_tiles[c][:, 128 * tc:128 * (tc + 1)], rhs=av_res[c][:, jsl],
                                start=(c == 0), stop=(c == DC - 1)), waits=w0 if c == 0 else [], inc=(c == DC - 1))
                        wte_ = ("te", r.cnt["te"])
                        vw = [wte_, wbvb] + ([prev_vst] if prev_vst else [])
                        prev_ev = r.op("vector", lambda e, jsl=jsl, bpar=bpar: e.tensor_tensor(
                            out=vt_t[:, jsl], in0=pqb[bpar][:], in1=bvr[:, jsl], op=ALU.add), waits=vw)
                        ev_hist[bpar] = prev_ev
                    wvst = r.dma("sync", v_view[:, :, tc, :],
                                 vt_t[:].rearrange("p (h i) -> p h i", i=128), waits=[prev_ev])
                    prev_vst = wvst
                wag_v = r.collective("AllGather", ALU.bypass, GRP4, [v_in.opt()], [v_ag.opt()], waits=[wvst])
            return wag_k, wag_v
        if upto >= "D":
            wag_k, wag_v = phase_D()

        def phase_E():
            r.wait("sync", r.fence())
            with contextlib.ExitStack() as es:
                ecx = es.enter_context
                mg = [ecx(nc.sbuf_tensor(f"emg{c}", [128, TPC], BF16)) for c in range(NH)]
                kvh = [ecx(nc.sbuf_tensor(f"ekvh{p}", [128, SEQ], BF16)) for p in range(2)]
                s2 = [ecx(nc.sbuf_tensor(f"esS{p}", [128, 512], BF16)) for p in range(2)]
                binv = ecx(nc.sbuf_tensor("ebinv", [128, 512], BF16))
                srow = ecx(nc.sbuf_tensor("esrow", [1, 512], F32))
                bo_sb = ecx(nc.sbuf_tensor("eboS", [128, DC], F32))
                eqs = contextlib.ExitStack()
                qf = [eqs.enter_context(nc.sbuf_tensor(f"eqf{c}", [128, TPC], BF16)) for c in range(DC)]
                m8 = eqs.enter_context(nc.sbuf_tensor("em8", [128, NKT * TPC], mybir.dt.int8))
                vvh = [eqs.enter_context(nc.sbuf_tensor(f"evvh{p}", [128, SEQ], BF16)) for p in range(2)]
                ps_s = [ps["p0"], ps["p1"]]
                ps_ctx = ps["p2"]
                ps_b = ps["p3"]
                ps_sum = ps["row"]

                wq = [r.dma("sync", qf[c][:], q_loc[128 * c:128 * (c + 1), :],
                            waits=[wag_k] if c == 0 else []) for c in range(DC)]
                wm8 = r.dma("sync", m8[:].rearrange("p (kt t) -> p kt t", t=TPC),
                            mask_d.rearrange("kt p t -> p kt t"))
                wbo = r.dma("sync", bo_sb[:], bo_d)
                m83 = m8[:].rearrange("p (kt t) -> p kt t", t=TPC)
                wall = ("ld", r.cnt["ld"])

                rdk = [None, None]
                prev_exp = None
                prev_pv = None
                wev_hist = [None, None]
                wexp_hist = [None, None]
                wpv_hist = [None, None]
                prev_ctxev = None
                prev_sumev = None
                prev_bev = None
                for h in range(NH):
                    par = h % 2
                    psem = "t0" if par == 0 else "t1"
                    kw = [w for w in (rdk[par], wag_k) if w]
                    wkh = None
                    for rr in range(4):
                        wkh = r.dma("sync", kvh[par][:, 1024 * rr:1024 * (rr + 1)],
                                    k_ag[2048 * rr + 128 * h:2048 * rr + 128 * (h + 1), :],
                                    waits=kw if rr == 0 else [], sem=psem)
                    wvh = None
                    for rr in range(4):
                        wvh = r.dma("sync", vvh[par][:, 1024 * rr:1024 * (rr + 1)],
                                    v_ag[2048 * rr + 128 * h:2048 * rr + 128 * (h + 1), :],
                                    waits=[wag_v] if rr == 0 else [], sem=psem)
                    for qc in range(2):
                        sl = slice(512 * qc, 512 * (qc + 1))
                        wev_hist = [None, None]
                        wexp_hist = [None, None]
                        wpv_hist = [None, None]

                        qk_te = {}

                        def emit_qk(kt):
                            pp = kt % 2
                            w0 = [wkh, wall]
                            if wev_hist[pp]:
                                w0.append(wev_hist[pp])
                            qk_te[kt] = r.op("tensor", lambda e, h=h, kt=kt, par=par, sl=sl, pp=pp: e.matmul(
                                ps_s[pp][:], lhsT=kvh[par][:, 128 * kt:128 * (kt + 1)], rhs=qf[h][:, sl],
                                start=True, stop=True), waits=w0)

                        emit_qk(0)
                        emit_qk(1)
                        for kt in range(NKT):
                            pp = kt % 2
                            wmm = qk_te[kt]
                            evw = [wmm]
                            if wexp_hist[pp]:
                                evw.append(wexp_hist[pp])
                            if wpv_hist[pp]:
                                evw.append(wpv_hist[pp])
                            wev = r.op("vector", lambda e, kt=kt, sl=sl, pp=pp: e.scalar_tensor_tensor(
                                out=s2[pp][:], in0=m83[:, kt, sl], scalar=MASK_NEG, in1=ps_s[pp][:],
                                op0=ALU.mult, op1=ALU.add), waits=evw)
                            wev_hist[pp] = wev
                            wexp = r.op("scalar", lambda e, pp=pp: e.activation(
                                out=s2[pp][:], in_=s2[pp][:], func=AF.Exp,
                                bias=cons[:, C_EXPB:C_EXPB + 1]), waits=[wev])
                            wexp_hist[pp] = wexp
                            pvw = [wexp, wvh]
                            if kt == 0 and prev_ctxev:
                                pvw.append(prev_ctxev)
                            r.op("tensor", lambda e, kt=kt, par=par, pp=pp: e.matmul(
                                ps_ctx[:], lhsT=vvh[par][:, 128 * kt:128 * (kt + 1)], rhs=s2[pp][:],
                                start=(kt == 0), stop=(kt == NKT - 1)), waits=pvw, inc=(kt == NKT - 1))
                            smw = []
                            if kt == 0 and prev_sumev:
                                smw.append(prev_sumev)
                            prev_pv = r.op("tensor", lambda e, kt=kt, pp=pp: e.matmul(
                                ps_sum[:], lhsT=sb["ones_bf"][:], rhs=s2[pp][:],
                                start=(kt == 0), stop=(kt == NKT - 1)), waits=smw)
                            wpv_hist[pp] = prev_pv
                            prev_exp = wexp
                            if kt + 2 < NKT:
                                emit_qk(kt + 2)
                        wte = ("te", r.cnt["te"])
                        prev_sumev = r.op("scalar", lambda e: e.copy(out=srow[:], in_=ps_sum[:]), waits=[wte])
                        wrc = r.op("vector", lambda e: e.reciprocal(out=srow[:], in_=srow[:]), waits=[prev_sumev])
                        wbc2 = r.op("tensor", lambda e: e.matmul(
                            ps_b[:], lhsT=sb["ones1"][:], rhs=srow[:], start=True, stop=True),
                            waits=[wrc, waux] + ([prev_bev] if prev_bev else []))
                        prev_bev = r.op("vector", lambda e: e.tensor_copy(out=binv[:], in_=ps_b[:]), waits=[wbc2])
                        prev_ctxev = r.op("vector", lambda e, h=h, sl=sl: e.tensor_tensor(
                            out=mg[h][:, sl], in0=ps_ctx[:], in1=binv[:], op=ALU.mult),
                            waits=[prev_bev, wte])
                    rdk[par] = ("te", r.cnt["te"])
                eqs.close()
                x_tiles = [ecx(nc.sbuf_tensor(f"ex{c}", [128, TPC], F32)) for c in range(DC)]
                wx = [r.dma("sync", x_tiles[c][:], x2_loc[128 * c:128 * (c + 1), :],
                            waits=[x2st, prev_ctxev] if c == 0 else []) for c in range(DC)]
                rdw = [prev_ctxev, prev_ctxev]
                prev_radd = None
                for dci in range(DC):
                    par = dci % 2
                    wd = r.dma("sync", kvh[par][:, 0:D], ag_out["aof"][128 * dci:128 * (dci + 1), :],
                               waits=[w for w in (rdw[par], ag_waits["aof"]) if w],
                               sem="t2" if par == 0 else "t3")
                    for hf in range(2):
                        sl = slice(512 * hf, 512 * (hf + 1))
                        w0 = [wd, prev_ctxev, wbo]
                        if prev_radd:
                            w0.append(prev_radd)
                        for k in range(DC):
                            r.op("tensor", lambda e, k=k, par=par, sl=sl: e.matmul(
                                ps_b[:], lhsT=kvh[par][:, 0:D][:, 128 * k:128 * (k + 1)], rhs=mg[k][:, sl],
                                start=(k == 0), stop=(k == DC - 1)), waits=w0 if k == 0 else [], inc=(k == DC - 1))
                        rdw[par] = ("te", r.cnt["te"])
                        prev_radd = r.op("vector", lambda e, dci=dci, sl=sl: e.scalar_tensor_tensor(
                            out=x_tiles[dci][:, sl], in0=ps_b[:], scalar=bo_sb[:, dci:dci + 1],
                            in1=x_tiles[dci][:, sl], op0=ALU.add, op1=ALU.add),
                            waits=[rdw[par], wx[dci]])
                x3st = None
                for c in range(DC):
                    x3st = r.dma("sync", x3_loc[128 * c:128 * (c + 1), :], x_tiles[c][:],
                                 waits=[prev_radd] if c == 0 else [])
            return x3st
        if upto >= "E":
            x3st = phase_E()

        def phase_F():
            r.wait("sync", r.fence())
            with contextlib.ExitStack() as fs:
                fcx = fs.enter_context
                x_tiles = [fcx(nc.sbuf_tensor(f"fx{c}", [128, TPC], F32)) for c in range(DC)]
                xl_tiles = [fcx(nc.sbuf_tensor(f"fxl{c}", [128, TPC], BF16)) for c in range(DC)]
                hid = [fcx(nc.sbuf_tensor(f"fhid{f}", [128, 512], BF16)) for f in range(NF1)]
                w1s = [fcx(nc.sbuf_tensor(f"fw1s{p}", [128, D], BF16)) for p in range(2)]
                w2s = fcx(nc.sbuf_tensor("fw2s", [128, H1], BF16))
                b1_sb = fcx(nc.sbuf_tensor("fb1S", [128, NF1], F32))
                b2_sb = fcx(nc.sbuf_tensor("fb2S", [128, DC], F32))
                pxb = [ps["p2"], ps["p0"]]

                wx = [r.dma("sync", x_tiles[c][:], x3_loc[128 * c:128 * (c + 1), :], waits=[x3st]) for c in range(DC)]
                wb1 = r.dma("sync", b1_sb[:], b1_d)
                wb2 = r.dma("sync", b2_sb[:], b2_d)
                wln = emit_ln(r, sb, ps, x_tiles, xl_tiles, TPC, deps=[wx[DC - 1], wb1, wb2])

                rd1 = [None, None]
                rd2 = None
                gel_hist = [None, None]
                radd_hist = [None, None]
                prev_gel = None
                prev_out_te = None
                prev_radd = None
                for h in range(2):
                    sl = slice(512 * h, 512 * (h + 1))
                    for f in range(NF1):
                        par = f % 2
                        wd1 = r.dma("sync", w1s[par][:], ag_out["f1f"][128 * f:128 * (f + 1), :],
                                    waits=[w for w in (rd1[par], ag_waits["f1f"]) if w],
                                    sem="t0" if par == 0 else "t1")
                        w0 = [wd1, wln] + ([gel_hist[par]] if gel_hist[par] else []) + ([radd_hist[par]] if radd_hist[par] else [])
                        for c in range(DC):
                            r.op("tensor", lambda e, c=c, par=par, sl=sl: e.matmul(
                                pxb[par][:], lhsT=w1s[par][:, 128 * c:128 * (c + 1)], rhs=xl_tiles[c][:, sl],
                                start=(c == 0), stop=(c == DC - 1)), waits=w0 if c == 0 else [], inc=(c == DC - 1))
                        rd1[par] = ("te", r.cnt["te"])
                        gw = [rd1[par]]
                        if h == 1 and prev_out_te:
                            gw.append(prev_out_te)
                        prev_gel = r.op("scalar", lambda e, f=f, par=par: e.activation(
                            out=hid[f][:], in_=pxb[par][:], func=AF.Gelu, bias=b1_sb[:, f:f + 1]), waits=gw)
                        gel_hist[par] = prev_gel
                    whid = prev_gel
                    for dci in range(DC):
                        par = dci % 2
                        wd2 = r.dma("sync", w2s[:], ag_out["f2f"][128 * dci:128 * (dci + 1), :],
                                    waits=[w for w in (rd2, ag_waits["f2f"]) if w], sem="t2")
                        w0 = [wd2, whid] + ([gel_hist[par]] if gel_hist[par] else []) + ([radd_hist[par]] if radd_hist[par] else [])
                        for k in range(NF1):
                            r.op("tensor", lambda e, k=k, par=par: e.matmul(
                                pxb[par][:], lhsT=w2s[:, 128 * k:128 * (k + 1)], rhs=hid[k][:],
                                start=(k == 0), stop=(k == NF1 - 1)), waits=w0 if k == 0 else [], inc=(k == NF1 - 1))
                        rd2 = ("te", r.cnt["te"])
                        prev_out_te = rd2
                        prev_radd = r.op("vector", lambda e, dci=dci, sl=sl, par=par: e.scalar_tensor_tensor(
                            out=x_tiles[dci][:, sl], in0=pxb[par][:], scalar=b2_sb[:, dci:dci + 1],
                            in1=x_tiles[dci][:, sl], op0=ALU.add, op1=ALU.add),
                            waits=[rd2])
                        radd_hist[par] = prev_radd
                        gel_hist[par] = prev_radd
                for c in range(DC):
                    r.dma("sync", o_out[128 * c:128 * (c + 1), :], x_tiles[c][:],
                          waits=[prev_radd] if c == 0 else [])
                r.wait("sync", [("ld", r.cnt["ld"])])
            return None
        if upto >= "F":
            phase_F()

        if upto < "F":
            r.wait("sync", r.fence())
        with nc.Block() as block:
            r.replay(block, sems)
    return nc


def host_prep_fused(inputs):
    bf = BF
    aux = _aux()
    wbcT = np.ascontiguousarray(inputs["L0_attn_W_bc"].T).astype(np.float32)
    wdtT = np.ascontiguousarray(inputs["L0_attn_W_dt"].T).astype(np.float32)
    bdt = np.ascontiguousarray(inputs["L0_attn_b_dt"].reshape(DC, 128).T).astype(np.float32)
    alog = np.asarray(inputs["L0_attn_A_log"], dtype=np.float32)
    dpar = np.asarray(inputs["L0_attn_D_param"], dtype=np.float32)
    flats = {
        "w1f": slabify(np.ascontiguousarray(inputs["L0_ffn_w1"].T).astype(np.float32), NF0).reshape(H0, D),
        "w2f": slabify(np.ascontiguousarray(inputs["L0_ffn_w2"].T).astype(np.float32), NF0).reshape(H0, D),
        "w3f": slabify(np.ascontiguousarray(inputs["L0_ffn_w3"].T).astype(np.float32), DC).reshape(D, H0),
        "f1f": slabify(np.ascontiguousarray(inputs["L1_ffn_w1"].T).astype(np.float32), NF1).reshape(H1, D),
        "f2f": slabify(np.ascontiguousarray(inputs["L1_ffn_w2"].T).astype(np.float32), DC).reshape(D, H1),
        "aqf": slabify(np.ascontiguousarray(inputs["L1_attn_Wq"].T).astype(np.float32), DC).reshape(D, D),
        "akf": slabify(np.ascontiguousarray(inputs["L1_attn_Wk"].T).astype(np.float32), DC).reshape(D, D),
        "avf": np.ascontiguousarray(inputs["L1_attn_Wv"].T).astype(np.float32),
        "aof": slabify(np.ascontiguousarray(inputs["L1_attn_Wo"].T).astype(np.float32), DC).reshape(D, D),
    }
    invcnt = np.zeros((1, 8), dtype=np.float32)
    for i, c in enumerate(QCOUNT):
        invcnt[0, i] = 1.0 / c
    sel = np.zeros((1, 8), dtype=np.float32)
    sel[0, 0:3] = 1.0
    bq = np.ascontiguousarray(inputs["L1_attn_bq"].reshape(DC, 128).T).astype(np.float32)
    bk = np.ascontiguousarray(inputs["L1_attn_bk"].reshape(DC, 128).T).astype(np.float32)
    bvrow = np.ascontiguousarray(inputs["L1_attn_bv"].reshape(1, D)).astype(np.float32)
    bo = np.ascontiguousarray(inputs["L1_attn_bo"].reshape(DC, 128).T).astype(np.float32)
    b1 = np.ascontiguousarray(inputs["L1_ffn_b1"].reshape(NF1, 128).T).astype(np.float32)
    b2 = np.ascontiguousarray(inputs["L1_ffn_b2"].reshape(DC, 128).T).astype(np.float32)
    inv_freq = 1.0 / (10000.0 ** (np.arange(0, DH, 2, dtype=np.float64) / DH))
    x = np.asarray(inputs["L0_attn_x"], dtype=np.float32)
    xt = x.reshape(B_SZ * SEQ, D)
    maps = []
    for c in range(NCORE):
        t0 = (TPC * c) % SEQ
        t = np.arange(t0, t0 + TPC, dtype=np.float64)
        fr = t[:, None] * inv_freq[None, :]
        cosT = np.cos(fr).T
        sinT = np.sin(fr).T
        sc = 1.0 / np.sqrt(DH)
        kpos = np.arange(SEQ).reshape(NKT, 128, 1)
        qpos = (t0 + np.arange(TPC)).reshape(1, 1, TPC)
        mask8 = (kpos > qpos).astype(np.int8)
        j = c % 4
        rows = slice(512 * j, 512 * (j + 1))
        al = np.ascontiguousarray(alog[rows].reshape(4, 128, 32).transpose(1, 0, 2).reshape(128, 128))
        dp = np.ascontiguousarray(dpar[rows.start:rows.stop].reshape(4, 128).T)
        m = dict(
            xT=np.ascontiguousarray(xt[TPC * c:TPC * (c + 1), :].T),
            wbcT=wbcT, wdtT=wdtT, bdt=bdt, alog=al, dpar=dp,
            invcnt=invcnt, sel=sel, bq=bq, bk=bk, bvrow=bvrow, bo=bo, b1=b1, b2=b2,
            cosq=(np.concatenate([cosT, cosT], 0) * sc).astype(bf),
            sinq=(np.concatenate([-sinT, sinT], 0) * sc).astype(bf),
            cosk=np.concatenate([cosT, cosT], 0).astype(bf),
            sink=np.concatenate([-sinT, sinT], 0).astype(bf),
            mask8=mask8, ones_bf=aux["ones_bf"], ones1=aux["ones1"], consts=aux["consts"],
        )
        for nm, arr in flats.items():
            rr = arr.shape[0] // NCORE
            m[nm] = np.ascontiguousarray(arr[rr * c:rr * (c + 1)])
        maps.append(m)
    return maps


def kernel_fused(**inputs):
    inputs = {k: np.asarray(v) for k, v in inputs.items()}
    maps = host_prep_fused(inputs)
    res = _run("FUSED", build_fused, maps)
    out = np.empty((B_SZ * SEQ, D), dtype=np.float32)
    for c in range(NCORE):
        out[TPC * c:TPC * (c + 1), :] = res[c]["o_out"].T
    return out.reshape(B_SZ, SEQ, D)


_CACHE = {}


def _run(name, builder, in_maps):
    if name not in _CACHE:
        _CACHE[name] = builder()
    res = run_bass_kernel_spmd(_CACHE[name], in_maps, core_ids=list(range(NCORE)))
    return res.results


def _aux():
    return dict(
        ones_bf=np.ones((128, 1), dtype=BF),
        ones1=np.ones((1, 128), dtype=np.float32),
        consts=np.tile(np.asarray([CONST_VALS], dtype=np.float32), (128, 1)),
    )


def host_prep_A(inputs, xs_T):
    wbcT = np.ascontiguousarray(inputs["L0_attn_W_bc"].T).astype(np.float32)
    wdtT = np.ascontiguousarray(inputs["L0_attn_W_dt"].T).astype(np.float32)
    bdt = np.ascontiguousarray(inputs["L0_attn_b_dt"].reshape(DC, 128).T).astype(np.float32)
    aux = _aux()
    return [dict(xT=xs_T[c], wbcT=wbcT, wdtT=wdtT, bdt=bdt, ones_bf=aux["ones_bf"],
                 ones1=aux["ones1"], consts=aux["consts"]) for c in range(NCORE)]


def host_prep_D(x2s, qj, inputs):
    s = dict(qaq=slabify(qj["aq"], DC), qak=slabify(qj["ak"], DC), qav=slabify(qj["av"], DC))
    bq = np.ascontiguousarray(inputs["L1_attn_bq"].reshape(DC, 128).T).astype(np.float32)
    bk = np.ascontiguousarray(inputs["L1_attn_bk"].reshape(DC, 128).T).astype(np.float32)
    bv = np.ascontiguousarray(inputs["L1_attn_bv"].reshape(DC, 128).T).astype(np.float32)
    inv_freq = 1.0 / (10000.0 ** (np.arange(0, DH, 2, dtype=np.float64) / DH))
    aux = _aux()
    maps = []
    for c in range(NCORE):
        t0 = (TPC * c) % SEQ
        t = np.arange(t0, t0 + TPC, dtype=np.float64)
        fr = t[:, None] * inv_freq[None, :]
        cosT = np.cos(fr).T
        sinT = np.sin(fr).T
        sc = 1.0 / np.sqrt(DH)
        cosq = np.concatenate([cosT, cosT], 0) * sc
        sinq = np.concatenate([-sinT, sinT], 0) * sc
        cosk = np.concatenate([cosT, cosT], 0)
        sink = np.concatenate([-sinT, sinT], 0)
        maps.append(dict(
            xT=x2s[c], qaq=s["qaq"], qak=s["qak"], qav=s["qav"],
            bq=bq, bk=bk, bv=bv,
            cosq=cosq.astype(BF), sinq=sinq.astype(BF),
            cosk=cosk.astype(BF), sink=sink.astype(BF),
            ones_bf=aux["ones_bf"], ones1=aux["ones1"], consts=aux["consts"]))
    return maps


def host_prep_E(x2s, resD, qj, inputs):
    qao = slabify(qj["ao"], DC)
    bo = np.ascontiguousarray(inputs["L1_attn_bo"].reshape(DC, 128).T).astype(np.float32)
    aux = _aux()
    kf_sl = {}
    vt_sl = {}
    for b in range(B_SZ):
        kfull = np.concatenate([resD[4 * b + j]["o_k"] for j in range(4)], axis=1)
        kf_sl[b] = np.ascontiguousarray(kfull.reshape(NH, 128, SEQ))
        vfull = np.concatenate([resD[4 * b + j]["o_v"] for j in range(4)], axis=1)
        vtm = np.ascontiguousarray(vfull.T)
        vt_sl[b] = np.ascontiguousarray(
            vtm.reshape(NKT, 128, NH, 128).transpose(2, 1, 0, 3).reshape(NH, 128, SEQ))
    maps = []
    for c in range(NCORE):
        b = c // 4
        t0 = (TPC * c) % SEQ
        kpos = np.arange(SEQ).reshape(NKT, 128, 1)
        qpos = (t0 + np.arange(TPC)).reshape(1, 1, TPC)
        mask8 = (kpos > qpos).astype(np.int8)
        maps.append(dict(
            xT=x2s[c], qf=resD[c]["o_q"], kf=kf_sl[b], vt=vt_sl[b], mask8=mask8,
            qao=qao, bo=bo, ones_bf=aux["ones_bf"], ones1=aux["ones1"], consts=aux["consts"]))
    return maps


def host_prep_F(x3s, qj, inputs):
    s1 = slabify(qj["f1s"], NF1)
    s2 = slabify(qj["f2s"], DC)
    b1 = np.ascontiguousarray(inputs["L1_ffn_b1"].reshape(NF1, 128).T).astype(np.float32)
    b2 = np.ascontiguousarray(inputs["L1_ffn_b2"].reshape(DC, 128).T).astype(np.float32)
    aux = _aux()
    return [dict(xT=x3s[c], qf1=s1, qf2=s2, b1=b1, b2=b2, ones_bf=aux["ones_bf"],
                 ones1=aux["ones1"], consts=aux["consts"]) for c in range(NCORE)]


def kernel(**inputs):
    return kernel_fused(**inputs)
